# version 06
# baseline (speedup 1.0000x reference)
import numpy as np
import ml_dtypes
import concourse.bass as bass
import concourse.mybir as mybir
from concourse.bass_utils import run_bass_kernel_spmd

F32 = mybir.dt.float32
BF16 = mybir.dt.bfloat16
AF = mybir.ActivationFunctionType
ALU = mybir.AluOpType
AX = mybir.AxisListType
BF = ml_dtypes.bfloat16

D = 1024
S = 4096
CH = 512
NCH = S // CH
GH = 8
HPG = GH // 2
DL = 512
DHH = 4096
NKC = D // 128
NJ = S // 128
EPS = 1.0e-5
NEG = -1.0e9
N_ITERS = 50
LOGK = float(np.log(np.float32(1152.0)))
A1 = float(np.log(np.float32(1152.0)) - np.log(np.float32(4096.0)))

ENGINES = ("sync", "tensor", "scalar", "vector", "gpsimd")

KBUF_O = 0
VBUF_O = 16384
AEX_O = 33024
HW1_O = 32768
ARB_N = 71680
ARF_N = 7680


def later(a, b):
    if a is None:
        return b
    if b is None:
        return a
    assert a[0] == b[0], (a, b)
    return a if a[1] >= b[1] else b


class Sched:
    def __init__(self, nc):
        self.nc = nc
        self.ops = {e: [] for e in ENGINES}
        self.sem = {}
        self.cnt = {}
        self.waited = {e: {} for e in ENGINES}
        for name in ("pe", "act", "dve", "pool", "dqL", "dqS", "dqG", "dqH"):
            self.sem[name] = nc.alloc_semaphore(name)
            self.cnt[name] = 0
        self.pread_m = {}

    OWN = {"tensor": "pe", "scalar": "act", "vector": "dve", "gpsimd": "pool"}

    def wait(self, eng, mst):
        if mst is None:
            return
        name, val = mst
        if self.OWN.get(eng) == name:
            return
        if val <= 0 or val <= self.waited[eng].get(name, 0):
            return
        self.waited[eng][name] = val
        h = self.sem[name]
        self.ops[eng].append(lambda e, h=h, v=val: e.wait_ge(h, v))

    def run(self, eng, fn, inc=None):
        if inc is None:
            self.ops[eng].append(fn)
            return None
        self.cnt[inc] += 1
        v = self.cnt[inc]
        h = self.sem[inc]

        def g(e, fn=fn, h=h):
            fn(e).then_inc(h, 1)
        self.ops[eng].append(g)
        return (inc, v)

    def dma(self, eng, qsem, out, in_, **kw):
        self.cnt[qsem] += 16
        v = self.cnt[qsem]
        h = self.sem[qsem]

        def g(e, out=out, in_=in_, h=h, kw=kw):
            e.dma_start(out=out, in_=in_, **kw).then_inc(h, 16)
        self.ops[eng].append(g)
        return (qsem, v)

    def mark(self, name):
        return (name, self.cnt[name])

    def puse(self, slot):
        for name, val in self.pread_m.get(slot, {}).items():
            self.wait("tensor", (name, val))

    def pread(self, slot, mst):
        if mst is None:
            return
        d = self.pread_m.setdefault(slot, {})
        name, val = mst
        d[name] = max(d.get(name, 0), val)


class KB:
    def __init__(self, taps=False, upto="E2"):
        nc = bass.Bass("TRN2", target_bir_lowering=False)
        self.nc = nc
        self.upto = upto
        self.s = Sched(nc)
        ktap = "ExternalOutput" if taps else "Internal"
        dt = nc.dram_tensor

        self.xT = dt("xT", [D, S], F32, kind="ExternalInput").ap()
        self.wq = dt("wq", [D, D], F32, kind="ExternalInput").ap()
        self.wk = dt("wk", [D, D], F32, kind="ExternalInput").ap()
        self.wv = dt("wv", [D, D], F32, kind="ExternalInput").ap()
        self.wo = dt("wo", [D, D], F32, kind="ExternalInput").ap()
        self.bo = dt("bo", [D, 1], F32, kind="ExternalInput").ap()
        self.rt = dt("rt", [D, 1], F32, kind="ExternalInput").ap()
        self.lgam = dt("lgam", [D, 1], F32, kind="ExternalInput").ap()
        self.lw1 = dt("lw1", [D, DL], F32, kind="ExternalInput").ap()
        self.lb1 = dt("lb1", [DL, 1], F32, kind="ExternalInput").ap()
        self.lw2 = dt("lw2", [DL, D], F32, kind="ExternalInput").ap()
        self.lb2 = dt("lb2", [D, 1], F32, kind="ExternalInput").ap()
        self.hgam = dt("hgam", [D, 1], F32, kind="ExternalInput").ap()
        self.hw1 = dt("hw1", [D, DHH], F32, kind="ExternalInput").ap()
        self.hb1 = dt("hb1", [DHH, 1], F32, kind="ExternalInput").ap()
        self.hw2 = dt("hw2", [DHH, D], F32, kind="ExternalInput").ap()
        self.hb2 = dt("hb2", [D, 1], F32, kind="ExternalInput").ap()
        self.cmaskd = dt("cmaskd", [128, 4 * CH], BF16, kind="ExternalInput").ap()
        self.idbd = dt("idbd", [128, 128], BF16, kind="ExternalInput").ap()
        self.idfd = dt("idfd", [128, 128], F32, kind="ExternalInput").ap()

        self.yT = dt("yT", [128, NCH * 8 * CH], F32, kind="ExternalOutput").ap()
        self.xlnb = dt("xlnb", [128, NCH * 8 * CH], BF16, kind=ktap).ap()
        self.k1b = dt("k1b", [128, HPG * S], BF16, kind=ktap).ap()
        self.v1b = dt("v1b", [128, NJ * GH * 65], BF16, kind=ktap).ap()
        self.attg0b = dt("attg0b", [128, NCH * HPG * CH], BF16, kind=ktap).ap()
        self.out1b = dt("out1b", [128, NCH * 8 * CH], F32, kind=ktap).ap()
        self.xrb = dt("xrb", [128, NCH * 8 * CH], BF16, kind=ktap).ap()
        self.lightb = dt("lightb", [128, NCH * 8 * CH], F32, kind=ktap).ap()
        self.srowb = dt("srowb", [1, S], F32, kind=ktap).ap()
        self.dbgb = dt("dbgb", [4, S], F32, kind=ktap).ap()
        self.offsd = dt("offsd", [1, 64], F32, kind="Internal").ap()
        self.selbd = dt("selbd", [1, S], F32, kind=ktap).ap()
        self.hhb = dt("hhb", [128, NCH * 32 * CH], BF16, kind=ktap).ap()

        self.arb = nc.alloc_sbuf_tensor("arb", [128, ARB_N], BF16)
        self.arf = nc.alloc_sbuf_tensor("arf", [128, ARF_N], F32)

        self.pqk = nc.alloc_psum_tensor("pqk", [128, 4 * 512], F32)
        self.pav0 = nc.alloc_psum_tensor("pav0", [128, 512], F32)
        self.pav1 = nc.alloc_psum_tensor("pav1", [128, 512], F32)
        self.pbc0 = nc.alloc_psum_tensor("pbc0", [128, 512], F32)
        self.pbc1 = nc.alloc_psum_tensor("pbc1", [128, 512], F32)

        self.ones_col = nc.alloc_sbuf_tensor("ones_col", [128, 1], F32)
        self.ones_bcol = nc.alloc_sbuf_tensor("ones_bcol", [128, 1], BF16)
        self.ones_row = nc.alloc_sbuf_tensor("ones_row", [128, 128], F32)
        self.eps_ap = nc.alloc_sbuf_tensor("eps_ap", [1, 1], F32)
        self.bo_sb = nc.alloc_sbuf_tensor("bo_sb", [128, 8], F32)
        self.lb2_sb = nc.alloc_sbuf_tensor("lb2_sb", [128, 8], F32)
        self.hb2_sb = nc.alloc_sbuf_tensor("hb2_sb", [128, 8], F32)
        self.lgam_sb = nc.alloc_sbuf_tensor("lgam_sb", [128, 8], F32)
        self.hgam_sb = nc.alloc_sbuf_tensor("hgam_sb", [128, 8], F32)
        self.lb1_sb = nc.alloc_sbuf_tensor("lb1_sb", [128, 4], F32)
        self.hb1_sb = nc.alloc_sbuf_tensor("hb1_sb", [128, 32], F32)
        self.rt_sb = nc.alloc_sbuf_tensor("rt_sb", [128, 8], F32)
        self.R_sb = nc.alloc_sbuf_tensor("R_sb", [128, 1], F32)
        self.R_sc = nc.alloc_sbuf_tensor("R_sc", [1, 1], F32)
        self.cm_sb = nc.alloc_sbuf_tensor("cm_sb", [128, 4 * CH], BF16)
        self.idb_sb = nc.alloc_sbuf_tensor("idb_sb", [128, 128], BF16)
        self.idf_sb = nc.alloc_sbuf_tensor("idf_sb", [128, 128], F32)
        self.tch_v = nc.alloc_sbuf_tensor("tch_v", [128, 1], F32)
        self.tch_a = nc.alloc_sbuf_tensor("tch_a", [128, 1], F32)
        self.tch_p = nc.alloc_sbuf_tensor("tch_p", [128, 1], F32)
        self.rows = nc.alloc_sbuf_tensor("rows", [128, 3072], F32)
        self.dent = nc.alloc_sbuf_tensor("dent", [128, 1024], F32)

        self.r_s = nc.alloc_sbuf_tensor("r_s", [64, 64], F32)
        self.r_t = nc.alloc_sbuf_tensor("r_t", [64, 64], F32)
        self.r_e = nc.alloc_sbuf_tensor("r_e", [64, 64], F32)
        self.r_m = nc.alloc_sbuf_tensor("r_m", [64, 64], F32)
        self.r_c1 = nc.alloc_sbuf_tensor("r_c1", [64, 64], F32)
        self.r_c2 = nc.alloc_sbuf_tensor("r_c2", [64, 64], F32)
        self.r_col = nc.alloc_sbuf_tensor("r_col", [64, 1], F32)
        self.r_acol = nc.alloc_sbuf_tensor("r_acol", [64, 1], F32)
        self.r_row1 = nc.alloc_sbuf_tensor("r_row1", [1, 64], F32)
        self.r_row2 = nc.alloc_sbuf_tensor("r_row2", [1, 64], F32)
        self.r_sc1 = nc.alloc_sbuf_tensor("r_sc1", [1, 1], F32)

    def ab(self, off, n):
        assert off + n <= ARB_N, (off, n)
        return self.arb[:, off:off + n]

    def af(self, off, n):
        assert off + n <= ARF_N, (off, n)
        return self.arf[:, off:off + n]

    def rseg(self, lane, off, n=CH):
        return self.rows[lane:lane + 1, off:off + n]

    def pslot(self, name):
        if name.startswith("q"):
            i = int(name[1])
            return self.pqk[:, i * 512:(i + 1) * 512]
        return {"av0": self.pav0, "av1": self.pav1,
                "bc0": self.pbc0, "bc1": self.pbc1}[name][:, 0:512]

    def runT(self, eng, fn, out_ap, inc, base=0):
        s = self.s
        s.run(eng, fn)
        src = out_ap[0:1, 0:1]
        if eng == "vector":
            dst = self.tch_v[base:base + 1, 0:1]
            return s.run("vector", lambda e, src=src, dst=dst: e.tensor_copy(dst, src),
                         inc=inc)
        if eng == "scalar":
            dst = self.tch_a[base:base + 1, 0:1]
            return s.run("scalar", lambda e, src=src, dst=dst: e.activation(
                dst, src, AF.Identity), inc=inc)
        if eng == "gpsimd":
            dst = self.tch_p[base:base + 1, 0:1]
            return s.run("gpsimd", lambda e, src=src, dst=dst: e.tensor_copy(dst, src),
                         inc=inc)
        raise ValueError(eng)

    def bcast(self, slot, row_ap, nrows=128, base=0):
        s = self.s
        s.puse(slot)
        p = self.pslot(slot)[0:nrows, :row_ap.shape[-1]]
        return s.run("tensor", lambda e, p=p, r=row_ap, n=nrows, b=base: e.matmul(
            p, self.ones_row[b:b + 1, 0:n], r, start=True, stop=True), inc="pe")

    def emit_preamble(self):
        s = self.s
        s.run("vector", lambda e: e.memset(self.ones_col[:, :], 1.0))
        s.run("vector", lambda e: e.memset(self.ones_bcol[:, :], 1.0))
        s.run("vector", lambda e: e.memset(self.ones_row[:, :], 1.0))
        self.m_ones = self.runT("vector", lambda e: e.memset(self.eps_ap[:, :], EPS),
                                self.eps_ap, "dve")
        for dst, src in [
            (self.bo_sb, self.bo), (self.lb2_sb, self.lb2), (self.hb2_sb, self.hb2),
            (self.lgam_sb, self.lgam), (self.hgam_sb, self.hgam), (self.rt_sb, self.rt),
            (self.lb1_sb, self.lb1), (self.hb1_sb, self.hb1),
        ]:
            s.dma("sync", "dqL", dst[:, :], src.rearrange("(c p) one -> p (c one)", p=128),
                  allow_slow_non_contiguous=True)
        s.dma("sync", "dqL", self.cm_sb[:, :], self.cmaskd[:, :])
        s.dma("sync", "dqL", self.idb_sb[:, :], self.idbd[:, :])
        self.m_consts = s.dma("sync", "dqL", self.idf_sb[:, :], self.idfd[:, :])
        s.wait("tensor", self.m_consts)
        s.wait("tensor", self.m_ones)
        s.wait("vector", self.m_consts)
        mrow = self.runT("vector", lambda e: e.reduce_sum(self.R_sb[:, :], self.rt_sb[:, :],
                                                          axis=AX.X), self.R_sb, "dve")
        s.wait("tensor", mrow)
        m = s.run("tensor", lambda e: e.matmul(self.pbc0[0:1, 0:1], self.ones_col[:, :],
                                               self.R_sb[:, :], start=True, stop=True), inc="pe")
        s.wait("vector", m)
        self.m_R = s.run("vector", lambda e: e.tensor_copy(self.R_sc[:, :],
                                                           self.pbc0[0:1, 0:1]), inc="dve")
        s.pread("bc0", self.m_R)

    def ln_stats(self, tiles, sq_scr, bf=False, wait_src=None, slots=("av0", "av1")):
        s = self.s
        ones = self.ones_bcol if bf else self.ones_col
        sl0, sl1 = slots
        p0 = self.pslot(sl0)[0:1, 0:CH]
        p1 = self.pslot(sl1)[0:1, 0:CH]
        s.puse(sl0)
        s.puse(sl1)
        m_sum = m_sq = None
        m_sq_mm = [None, None]
        for k in range(NKC):
            t = tiles[k]
            if wait_src is not None:
                s.wait("tensor", wait_src)
                s.wait("scalar", wait_src)
            m_sum = s.run("tensor", lambda e, t=t, k=k, o=ones: e.matmul(
                p0, o[:, :], t, start=(k == 0), stop=(k == NKC - 1)),
                inc="pe" if k == NKC - 1 else None)
            scr = sq_scr[k % 2]
            s.wait("scalar", m_sq_mm[k % 2])
            ma = s.run("scalar", lambda e, t=t, scr=scr: e.activation(scr, t, AF.Square),
                       inc="act")
            s.wait("tensor", ma)
            m_sq = m_sq_mm[k % 2] = s.run("tensor", lambda e, scr=scr, k=k: e.matmul(
                p1, self.ones_col[:, :], scr,
                start=(k == 0), stop=(k == NKC - 1)), inc="pe")
        s.wait("vector", m_sum)
        s.wait("vector", m_sq)
        s.run("vector", lambda e: e.tensor_copy(self.rseg(0, 0), p0))
        m = s.run("vector", lambda e: e.tensor_copy(self.rseg(0, 512), p1),
                  inc="dve")
        s.pread(sl0, m)
        s.pread(sl1, m)
        return m

    def ln_rows(self, m_stats, rms=False):
        s = self.s
        SUMS = self.rseg(0, 0)
        SQ = self.rseg(0, 512)
        VAR = self.rseg(0, 1024)
        MEAN = self.rseg(0, 1536)
        Aseg = self.rseg(0, 2048)
        Bseg = self.rseg(0, 2560)
        s.wait("vector", m_stats)
        s.run("vector", lambda e: e.tensor_scalar(MEAN, SUMS, 1.0 / D, None, ALU.mult))
        s.run("vector", lambda e: e.tensor_scalar(VAR, SQ, 1.0 / D, None, ALU.mult))
        s.run("vector", lambda e: e.scalar_tensor_tensor(SUMS, MEAN, 1.0, MEAN, ALU.mult,
                                                         ALU.mult))
        m = self.runT("vector", lambda e: e.tensor_sub(VAR, VAR, SUMS), VAR, "dve")
        s.wait("scalar", m)
        s.wait("scalar", self.m_ones)
        s.run("scalar", lambda e: e.activation(Aseg, VAR, AF.Ln, bias=self.eps_ap[:, :]))
        m_rms = None
        if rms:
            ml2 = self.runT("scalar", lambda e: e.activation(VAR, VAR, AF.Ln), VAR, "act")
            s.wait("vector", ml2)
            s.run("vector", lambda e: e.tensor_sub(VAR, Aseg, VAR))
            mv = self.runT("vector", lambda e: e.tensor_scalar(VAR, VAR, 0.5, None, ALU.mult),
                           VAR, "dve")
            s.wait("scalar", mv)
            m_rms = self.runT("scalar", lambda e: e.activation(VAR, VAR, AF.Exp), VAR, "act")
        mA = self.runT("scalar", lambda e: e.activation(Aseg, Aseg, AF.Exp, scale=-0.5),
                       Aseg, "act")
        s.wait("vector", mA)
        mB = self.runT("vector", lambda e: e.scalar_tensor_tensor(
            Bseg, MEAN, -1.0, Aseg, ALU.mult, ALU.mult), Bseg, "dve")
        return mB, m_rms

    def affine(self, src_tiles, dst_tiles, tmp, m_ready, resid=None, pool_out=None,
               slots=("bc0", "bc1")):
        s = self.s
        m_aff = None
        m_res = None
        sa, sb = slots
        s.wait("tensor", m_ready)
        mA_ = self.bcast(sa, self.rseg(0, 2048))
        mB_ = self.bcast(sb, self.rseg(0, 2560))
        s.wait("vector", mA_)
        s.wait("vector", mB_)
        for k in range(NKC):
            s.run("vector", lambda e, k=k: e.tensor_mul(
                tmp, src_tiles[k], self.pslot(sa)[:, 0:CH]))
            if resid is None:
                m_aff = s.run("vector", lambda e, k=k: e.tensor_add(
                    dst_tiles[k], tmp, self.pslot(sb)), inc="dve")
            else:
                tmp2 = pool_out
                s.wait("vector", m_res)
                m_aff = s.run("vector", lambda e, k=k, tmp2=tmp2: e.tensor_add(
                    tmp2, tmp, self.pslot(sb)), inc="dve")
                s.wait("gpsimd", m_aff)
                m_res = s.run("gpsimd", lambda e, k=k, tmp2=tmp2: e.tensor_add(
                    resid[k], resid[k], tmp2), inc="pool")
        s.pread(sa, m_aff)
        s.pread(sb, m_aff)
        return m_aff, m_res

    def emit_phase_a(self):
        s = self.s
        o = AEX_O
        wk_b = self.ab(o, NKC * D); o += NKC * D
        wv_b = self.ab(o, NKC * D); o += NKC * D
        xlnw = self.ab(o, 2 * NKC * CH); o += 2 * NKC * CH
        ktmp = [self.ab(o, CH), self.ab(o + CH, CH)]; o += 2 * CH
        vtmp = [self.ab(o, GH * 65), self.ab(o + GH * 65 + 8, GH * 65)]
        o += 2 * (GH * 65 + 8)
        assert o <= ARB_N, o
        xc = self.af(0, NKC * CH)
        xct = [xc[:, k * CH:(k + 1) * CH] for k in range(NKC)]
        sqs = [self.af(NKC * CH, CH), self.af(NKC * CH + CH, CH)]
        tmp = self.af(NKC * CH + 2 * CH, CH)
        kbuf = self.ab(KBUF_O, HPG * S)
        vbuf = self.ab(VBUF_O, NJ * GH * 65)
        vview = vbuf.rearrange("p (t e) -> p t e", e=65)

        s.run("vector", lambda e: e.memset(vview[:, :, 64:65], 1.0))
        for b in range(2):
            vv = vtmp[b].rearrange("p (t e) -> p t e", e=65)[:, 0:GH, :]
            s.run("vector", lambda e, vv=vv: e.memset(vv[:, :, 64:65], 1.0))
        m_vones = self.runT("vector", lambda e: e.tensor_copy(
            self.tch_v[0:1, 0:1], vbuf[0:1, 64:65]), vbuf, "dve")

        for k in range(NKC):
            s.dma("gpsimd", "dqG", wk_b[:, k * D:(k + 1) * D], self.wk[k * 128:(k + 1) * 128, :])
            s.dma("gpsimd", "dqG", wv_b[:, k * D:(k + 1) * D], self.wv[k * 128:(k + 1) * 128, :])
        m_w = s.mark("dqG")

        m_xc_cons = None
        m_xln_cons = [None, None]
        m_xln_st = [None, None]
        m_rows_cons = None
        m_ktmp = [None, None]
        m_vtmp = [None, None]
        for i in range(NCH):
            par = i % 2
            xln = xlnw[:, par * NKC * CH:(par + 1) * NKC * CH]
            xlt = [xln[:, k * CH:(k + 1) * CH] for k in range(NKC)]
            s.wait("sync", m_xc_cons)
            mld = None
            for k in range(NKC):
                mld = s.dma("sync", "dqL", xct[k],
                            self.xT[k * 128:(k + 1) * 128, i * CH:(i + 1) * CH])
            s.wait("tensor", mld)
            s.wait("scalar", mld)
            s.wait("tensor", self.m_ones)
            m_stats = self.ln_stats(xct, sqs)
            s.wait("vector", m_rows_cons)
            s.wait("scalar", m_rows_cons)
            mB, _ = self.ln_rows(m_stats)
            s.wait("vector", m_xln_cons[par])
            s.wait("vector", m_xln_st[par])
            m_aff, _ = self.affine(xct, xlt, tmp, mB)
            m_rows_cons = ("pe", s.cnt["pe"])
            m_xc_cons = m_aff
            s.wait("scalar", m_aff)
            m_xln_st[par] = s.dma("scalar", "dqS",
                                  self.xlnb[:, i * 8 * CH:(i + 1) * 8 * CH], xln)

            s.wait("tensor", m_w)
            s.wait("tensor", m_aff)
            for oc in range(8):
                slot = "q0" if oc % 2 == 0 else "q1"
                po = self.pslot(slot)
                s.puse(slot)
                mm = None
                for k in range(NKC):
                    mm = s.run("tensor", lambda e, oc=oc, k=k, po=po, xlt=xlt: e.matmul(
                        po, wk_b[:, k * D + oc * 128:k * D + (oc + 1) * 128], xlt[k],
                        start=(k == 0), stop=(k == NKC - 1)),
                        inc="pe" if k == NKC - 1 else None)
                s.wait("vector", mm)
                if oc < HPG:
                    dst = kbuf[:, oc * S + i * CH:oc * S + (i + 1) * CH]
                    mev = s.run("vector", lambda e, dst=dst, po=po: e.tensor_copy(dst, po),
                                inc="dve")
                else:
                    kt = ktmp[oc % 2]
                    s.wait("vector", m_ktmp[oc % 2])
                    mev = s.run("vector", lambda e, kt=kt, po=po: e.tensor_copy(kt, po),
                                inc="dve")
                    s.wait("scalar", mev)
                    m_ktmp[oc % 2] = s.dma(
                        "scalar", "dqS",
                        self.k1b[:, (oc - HPG) * S + i * CH:(oc - HPG) * S + (i + 1) * CH], kt)
                s.pread(slot, mev)
            s.wait("vector", m_vones)
            for t4 in range(4):
                j = i * 4 + t4
                for half in range(2):
                    slot = "av0" if half == 0 else "av1"
                    pv = self.pslot(slot)
                    s.puse(slot)
                    mm = None
                    for k in range(NKC):
                        mm = s.run("tensor", lambda e, t4=t4, k=k, half=half, pv=pv, xlt=xlt:
                                   e.matmul(pv,
                                            xlt[k][:, t4 * 128:(t4 + 1) * 128],
                                            wv_b[:, k * D + half * 512:k * D + (half + 1) * 512],
                                            start=(k == 0), stop=(k == NKC - 1)),
                                   inc="pe" if k == NKC - 1 else None)
                    s.wait("vector", mm)
                    src = pv.rearrange("p (h d) -> p h d", d=64)
                    if half == 0:
                        dv = vview[:, j * GH:(j + 1) * GH, 0:64]
                        mev = s.run("vector", lambda e, dv=dv, src=src: e.tensor_copy(dv, src),
                                    inc="dve")
                    else:
                        vt = vtmp[t4 % 2]
                        s.wait("vector", m_vtmp[t4 % 2])
                        dvv = vt.rearrange("p (t e) -> p t e", e=65)[:, 0:GH, 0:64]
                        mev = s.run("vector", lambda e, dvv=dvv, src=src: e.tensor_copy(dvv, src),
                                    inc="dve")
                        s.wait("scalar", mev)
                        m_vtmp[t4 % 2] = s.dma(
                            "scalar", "dqS",
                            self.v1b[:, j * GH * 65:(j + 1) * GH * 65],
                            vt[:, 0:GH * 65])
                    s.pread(slot, mev)
            m_xln_cons[par] = ("pe", s.cnt["pe"])
        self.m_A_kv = ("dve", s.cnt["dve"])
        self.m_A_rows_cons = m_rows_cons
        self.m_A_pe = ("pe", s.cnt["pe"])

    def emit_phase_b(self, g):
        s = self.s
        kbuf = self.ab(KBUF_O, HPG * S)
        vbuf = self.ab(VBUF_O, NJ * GH * 65)
        vview = vbuf.rearrange("p (t e) -> p t e", e=65)
        o = AEX_O
        wqh = self.ab(o, NKC * 512); o += NKC * 512
        if g == 1:
            wo_b = self.ab(o, NKC * D); o += NKC * D
        xlnw = self.ab(o, 2 * NKC * CH); o += 2 * NKC * CH
        qcw = self.ab(o, 2 * HPG * CH); o += 2 * HPG * CH
        attcw = self.ab(o, 2 * HPG * CH); o += 2 * HPG * CH
        if g == 1:
            ag0 = self.ab(o, HPG * CH); o += HPG * CH
            obuf = self.ab(o, NKC * CH); o += NKC * CH
            obt = [obuf[:, k * CH:(k + 1) * CH] for k in range(NKC)]
        ebuf = self.ab(o, 3 * 1024); o += 3 * 1024
        assert o <= ARB_N, o
        xc = self.af(0, NKC * CH)
        xct = [xc[:, k * CH:(k + 1) * CH] for k in range(NKC)]
        tmp = self.af(NKC * CH, CH)
        tmp2 = self.af(NKC * CH + CH, CH)
        sqs = [self.af(NKC * CH + 2 * CH, CH), self.af(NKC * CH + 3 * CH, CH)]

        if g == 1:
            s.wait("sync", self.m_B1_pe)
            s.wait("sync", self.m_A_store)
            s.dma("sync", "dqL", kbuf, self.k1b[:, :])
            m_kv_ready = s.dma("sync", "dqL", vview[:, :, 0:64],
                               self.v1b.rearrange("p (t e) -> p t e", e=65)[:, :, 0:64])
        else:
            m_kv_ready = None
        m_prev = self.m_A_pe if g == 0 else self.m_B1_pe
        m_prev_st = self.m_A_store if g == 0 else self.m_B1_store
        m_prev_dve = self.m_A_kv if g == 0 else self.m_B1_dve
        for eng in ("gpsimd", "vector", "scalar", "sync"):
            s.wait(eng, m_prev)
            s.wait(eng, m_prev_st)
            s.wait(eng, m_prev_dve)
        for k in range(NKC):
            s.dma("gpsimd", "dqG", wqh[:, k * 512:(k + 1) * 512],
                  self.wq[k * 128:(k + 1) * 128, g * 512:(g + 1) * 512])
        if g == 1:
            for k in range(NKC):
                s.dma("gpsimd", "dqG", wo_b[:, k * D:(k + 1) * D],
                      self.wo[k * 128:(k + 1) * 128, :])
        m_w = s.mark("dqG")

        m_qc_cons = [None, None]
        m_xln_cons = [m_prev, m_prev]
        m_attc_cons = [None, None]
        m_ag0_cons = m_prev
        m_xc_cons = self.m_A_kv if g == 1 else None
        m_rows_cons = self.m_A_rows_cons if g == 0 else self.m_B_rows_cons
        m_u = None
        m_av_hist = [None, None, None]
        for i in range(NCH):
            jmax = 4 * (i + 1)
            par = i % 2
            xln = xlnw[:, par * NKC * CH:(par + 1) * NKC * CH]
            xlt = [xln[:, k * CH:(k + 1) * CH] for k in range(NKC)]
            qc = qcw[:, par * HPG * CH:(par + 1) * HPG * CH]
            attc = attcw[:, par * HPG * CH:(par + 1) * HPG * CH]
            s.wait("sync", m_xln_cons[par])
            s.wait("sync", self.m_A_store)
            m_xln = s.dma("sync", "dqL", xln, self.xlnb[:, i * 8 * CH:(i + 1) * 8 * CH])
            if g == 1:
                s.wait("sync", m_ag0_cons)
                s.wait("sync", self.m_B1_store)
                m_ag0 = s.dma("sync", "dqL", ag0,
                              self.attg0b[:, i * HPG * CH:(i + 1) * HPG * CH])
                s.wait("sync", m_xc_cons)
                m_xc = None
                for k in range(NKC):
                    m_xc = s.dma("sync", "dqL", xct[k],
                                 self.xT[k * 128:(k + 1) * 128, i * CH:(i + 1) * CH])
            s.wait("tensor", m_w)
            s.wait("tensor", m_xln)
            if g == 0:
                s.wait("tensor", self.m_A_kv)
            else:
                s.wait("tensor", m_kv_ready)
            m_q = None
            for hp in range(HPG):
                slot = "av0" if hp % 2 == 0 else "av1"
                pq = self.pslot(slot)
                s.puse(slot)
                mm = None
                for k in range(NKC):
                    mm = s.run("tensor", lambda e, hp=hp, k=k, pq=pq, xlt=xlt: e.matmul(
                        pq, wqh[:, k * 512 + hp * 128:k * 512 + (hp + 1) * 128], xlt[k],
                        start=(k == 0), stop=(k == NKC - 1)),
                        inc="pe" if k == NKC - 1 else None)
                s.wait("vector", mm)
                s.wait("vector", m_qc_cons[par])
                m_q = s.run("vector", lambda e, hp=hp, pq=pq, qc=qc: e.tensor_copy(
                    qc[:, hp * CH:(hp + 1) * CH], pq), inc="dve")
                s.pread(slot, m_q)
            s.wait("tensor", m_q)
            for hp in range(HPG):
                mqk_hist = {}

                def emit_qk(j, hp=hp, i=i, qc=qc):
                    sp = j % 2
                    sa, sb = f"q{2 * sp}", f"q{2 * sp + 1}"
                    pa, pb = self.pslot(sa), self.pslot(sb)
                    diag = j >= 4 * i
                    s.puse(sa)
                    s.puse(sb)
                    s.run("tensor", lambda e, pa=pa, j=j, diag=diag: e.matmul(
                        pa, kbuf[0:64, hp * S + j * 128:hp * S + (j + 1) * 128],
                        qc[0:64, hp * CH:(hp + 1) * CH],
                        start=True, stop=not diag, tile_position=(0, 0)), inc=None)
                    if diag:
                        dj = j - 4 * i
                        s.run("tensor", lambda e, pa=pa, dj=dj: e.matmul(
                            pa, self.idb_sb[:, :], self.cm_sb[:, dj * CH:(dj + 1) * CH],
                            start=False, stop=True), inc=None)
                    mqk = s.run("tensor", lambda e, pb=pb, j=j, diag=diag: e.matmul(
                        pb, kbuf[64:128, hp * S + j * 128:hp * S + (j + 1) * 128],
                        qc[64:128, hp * CH:(hp + 1) * CH],
                        start=True, stop=not diag, tile_position=(64, 0)),
                        inc=None if diag else "pe")
                    if diag:
                        dj = j - 4 * i
                        mqk = s.run("tensor", lambda e, pb=pb, dj=dj: e.matmul(
                            pb, self.idb_sb[:, :], self.cm_sb[:, dj * CH:(dj + 1) * CH],
                            start=False, stop=True), inc="pe")
                    mqk_hist[j] = mqk

                emit_qk(0)
                for j in range(jmax):
                    sp = j % 2
                    se = j % 3
                    sa, sb = f"q{2 * sp}", f"q{2 * sp + 1}"
                    if j + 1 < jmax:
                        emit_qk(j + 1)
                    s.wait("scalar", mqk_hist.pop(j))
                    s.wait("scalar", m_av_hist[se])
                    m_exp = s.run("scalar", lambda e, sp=sp, se=se: e.activation(
                        ebuf[:, se * 1024:(se + 1) * 1024],
                        self.pqk[:, (2 * sp) * 512:(2 * sp + 2) * 512],
                        AF.Exp, scale=0.125), inc="act")
                    s.pread(sa, m_exp)
                    s.pread(sb, m_exp)
                    s.wait("tensor", m_exp)
                    if j == 0:
                        s.puse("av0")
                        s.puse("av1")
                    s.run("tensor", lambda e, hp=hp, j=j, se=se, jmax=jmax: e.matmul(
                        self.pav0[0:65, :], vview[:, j * GH + 2 * hp, :],
                        ebuf[:, se * 1024:se * 1024 + 512],
                        start=(j == 0), stop=(j == jmax - 1)), inc=None)
                    m_av = s.run("tensor", lambda e, hp=hp, j=j, se=se, jmax=jmax: e.matmul(
                        self.pav1[0:65, :], vview[:, j * GH + 2 * hp + 1, :],
                        ebuf[:, se * 1024 + 512:(se + 1) * 1024],
                        start=(j == 0), stop=(j == jmax - 1)), inc="pe")
                    m_av_hist[se] = m_av
                s.wait("vector", m_av)
                s.wait("vector", m_attc_cons[par])
                s.run("vector", lambda e, hp=hp, attc=attc: e.tensor_copy(
                    attc[0:64, hp * CH:(hp + 1) * CH], self.pav0[0:64, :]))
                s.run("vector", lambda e, hp=hp, attc=attc: e.tensor_copy(
                    attc[64:128, hp * CH:(hp + 1) * CH], self.pav1[0:64, :]))
                s.run("vector", lambda e: e.tensor_copy(
                    self.dent[64:65, 0:512], self.pav0[64:65, :]))
                m_den = self.runT("vector", lambda e: e.tensor_copy(
                    self.dent[64:65, 512:1024], self.pav1[64:65, :]),
                    self.dent[64:65, :], "dve", base=64)
                s.pread("av0", m_den)
                s.pread("av1", m_den)
                s.wait("scalar", m_den)
                s.run("scalar", lambda e: e.activation(
                    self.dent[64:65, 0:1024], self.dent[64:65, 0:1024], AF.Ln))
                m_rc = self.runT("scalar", lambda e: e.activation(
                    self.dent[64:65, 0:1024], self.dent[64:65, 0:1024], AF.Exp, scale=-1.0),
                    self.dent[64:65, :], "act", base=64)
                s.wait("tensor", m_rc)
                for ab in range(2):
                    slot = "bc0" if ab == 0 else "bc1"
                    mb = self.bcast(slot, self.dent[64:65, ab * 512:(ab + 1) * 512],
                                    nrows=64, base=64)
                    s.wait("vector", mb)
                    sl = attc[ab * 64:(ab + 1) * 64, hp * CH:(hp + 1) * CH]
                    m_u = s.run("vector", lambda e, sl=sl, slot=slot: e.tensor_mul(
                        sl, sl, self.pslot(slot)[0:64, :]), inc="dve")
                    s.pread(slot, m_u)
            if g == 0:
                s.wait("scalar", m_u)
                m_attc_cons[par] = s.dma("scalar", "dqS",
                                         self.attg0b[:, i * HPG * CH:(i + 1) * HPG * CH],
                                         attc)
                m_qc_cons[par] = ("pe", s.cnt["pe"])
                m_xln_cons[par] = ("pe", s.cnt["pe"])
            else:
                s.wait("tensor", m_ag0)
                s.wait("tensor", m_u)
                m_ob = [None, None]
                for f in range(NKC):
                    slot = "bc0" if f % 2 == 0 else "bc1"
                    pf = self.pslot(slot)
                    s.puse(slot)
                    mm = None
                    for k in range(NKC):
                        rhs = ag0[:, k * CH:(k + 1) * CH] if k < 4 else \
                            attc[:, (k - 4) * CH:(k - 3) * CH]
                        mm = s.run("tensor", lambda e, f=f, k=k, pf=pf, rhs=rhs: e.matmul(
                            pf, wo_b[:, k * D + f * 128:k * D + (f + 1) * 128], rhs,
                            start=(k == 0), stop=(k == NKC - 1)),
                            inc="pe" if k == NKC - 1 else None)
                    s.wait("scalar", mm)
                    m_ob[f % 2] = s.run("scalar", lambda e, f=f, pf=pf: e.activation(
                        obt[f], pf, AF.Identity, bias=self.bo_sb[:, f:f + 1]), inc="act")
                    s.pread(slot, m_ob[f % 2])
                m_ag0_cons = ("pe", s.cnt["pe"])
                m_qc_cons[par] = ("pe", s.cnt["pe"])
                m_xln_cons[par] = ("pe", s.cnt["pe"])
                m_obl = m_ob[(NKC - 1) % 2]
                m_stats = self.ln_stats(obt, sqs, bf=True, wait_src=m_obl,
                                        slots=("bc0", "bc1"))
                s.wait("vector", m_rows_cons)
                s.wait("scalar", m_rows_cons)
                mB, _ = self.ln_rows(m_stats)
                s.wait("gpsimd", m_xc)
                s.wait("vector", m_xc)
                m_aff, m_res = self.affine(obt, None, tmp, mB, resid=xct, pool_out=tmp2,
                                           slots=("bc0", "bc1"))
                m_rows_cons = ("pe", s.cnt["pe"])
                s.wait("scalar", m_res)
                m_xc_cons = s.dma("scalar", "dqS",
                                  self.out1b[:, i * 8 * CH:(i + 1) * 8 * CH], xc)
                m_attc_cons[par] = ("pe", s.cnt["pe"])
        if g == 0:
            self.m_B1_pe = ("pe", s.cnt["pe"])
            self.m_B_rows_cons = m_rows_cons
            self.m_B1_store = s.mark("dqS")
            self.m_B1_dve = ("dve", s.cnt["dve"])
        else:
            self.m_B2_store = s.mark("dqS")
            self.m_B2_pe = ("pe", s.cnt["pe"])
            self.m_B2_dve = ("dve", s.cnt["dve"])
            self.m_C_rows_init = m_rows_cons

    def emit_phase_c(self):
        s = self.s
        o = 0
        lw1_b = self.ab(o, NKC * 512); o += NKC * 512
        lw2_b = self.ab(o, 4 * D); o += 4 * D
        xrc = self.ab(o, NKC * CH); o += NKC * CH
        xrt = [xrc[:, k * CH:(k + 1) * CH] for k in range(NKC)]
        lgh = self.ab(o, 4 * CH); o += 4 * CH
        assert o <= HW1_O
        hw1_b = self.ab(HW1_O, NKC * DHH)
        out1c = self.af(0, NKC * CH)
        o1t = [out1c[:, k * CH:(k + 1) * CH] for k in range(NKC)]
        tmp = self.af(NKC * CH, CH)
        lo = [self.af(NKC * CH + CH, CH), self.af(NKC * CH + 2 * CH, CH)]
        sqs = [self.af(NKC * CH + 3 * CH, CH), self.af(NKC * CH + 4 * CH, CH)]

        s.wait("gpsimd", self.m_B2_pe)
        mw = None
        for k in range(NKC):
            mw = s.dma("gpsimd", "dqG", lw1_b[:, k * 512:(k + 1) * 512],
                       self.lw1[k * 128:(k + 1) * 128, :])
        for k in range(4):
            mw = s.dma("gpsimd", "dqG", lw2_b[:, k * D:(k + 1) * D],
                       self.lw2[k * 128:(k + 1) * 128, :])
        s.wait("vector", mw)
        m_fold = None
        for k in range(NKC):
            m_fold = s.run("vector", lambda e, k=k: e.tensor_scalar(
                lw1_b[:, k * 512:(k + 1) * 512], lw1_b[:, k * 512:(k + 1) * 512],
                self.lgam_sb[:, k:k + 1], None, ALU.mult), inc="dve")
        m_hw1f = None
        for k in range(NKC):
            mwh = s.dma("gpsimd", "dqG", hw1_b[:, k * DHH:(k + 1) * DHH],
                        self.hw1[k * 128:(k + 1) * 128, :])
            s.wait("vector", mwh)
            m_hw1f = s.run("vector", lambda e, k=k: e.tensor_scalar(
                hw1_b[:, k * DHH:(k + 1) * DHH], hw1_b[:, k * DHH:(k + 1) * DHH],
                self.hgam_sb[:, k:k + 1], None, ALU.mult), inc="dve")
        self.m_hw1 = m_hw1f

        m_out1_cons = None
        m_rows_cons = self.m_C_rows_init
        m_xr_cons = None
        m_xr_st = None
        m_lo_cons = [None, None]
        m_lgh_cons = None
        m_srow_st = None
        for i in range(NCH):
            s.wait("sync", m_out1_cons)
            s.wait("sync", self.m_B2_store)
            mo1 = None
            for k in range(NKC):
                mo1 = s.dma("sync", "dqL", o1t[k],
                            self.out1b[:, i * 8 * CH + k * CH:i * 8 * CH + (k + 1) * CH])
            s.wait("tensor", mo1)
            s.wait("scalar", mo1)
            s.wait("vector", m_srow_st)
            m_stats = self.ln_stats(o1t, sqs)
            s.puse("q0")
            mm = None
            for k in range(NKC):
                mm = s.run("tensor", lambda e, k=k: e.matmul(
                    self.pqk[0:1, 0:CH], self.rt_sb[:, k:k + 1], o1t[k],
                    start=(k == 0), stop=(k == NKC - 1)),
                    inc="pe" if k == NKC - 1 else None)
            s.wait("vector", m_rows_cons)
            s.wait("scalar", m_rows_cons)
            mB, m_rms = self.ln_rows(m_stats, rms=True)
            s.wait("vector", mm)
            s.wait("vector", mB)
            s.wait("vector", self.m_R)
            s.wait("vector", m_srow_st)
            s.run("vector", lambda e: e.tensor_mul(
                self.rseg(0, 0), self.pqk[0:1, 0:CH], self.rseg(0, 2048)))
            s.run("vector", lambda e: e.tensor_scalar(
                self.rseg(0, 512), self.rseg(0, 2560), self.R_sc[0:1, 0:1], None,
                ALU.mult))
            m_srow = self.runT("vector", lambda e: e.tensor_add(
                self.rseg(0, 0), self.rseg(0, 0), self.rseg(0, 512)), self.rseg(0, 0), "dve")
            s.pread("q0", m_srow)
            s.wait("scalar", m_srow)
            m_srow_st = s.dma("scalar", "dqS", self.srowb[0:1, i * CH:(i + 1) * CH],
                              self.rseg(0, 0))
            s.wait("vector", m_rms)
            s.run("vector", lambda e: e.tensor_mul(self.rseg(0, 2048), self.rseg(0, 2048),
                                                   self.rseg(0, 1024)))
            mBp = self.runT("vector", lambda e: e.tensor_mul(
                self.rseg(0, 2560), self.rseg(0, 2560), self.rseg(0, 1024)),
                self.rseg(0, 2560), "dve")
            s.wait("vector", m_xr_cons)
            s.wait("vector", m_xr_st)
            m_aff, _ = self.affine(o1t, xrt, tmp, mBp)
            m_rows_cons = ("pe", s.cnt["pe"])
            m_out1_cons = m_aff
            s.wait("scalar", m_aff)
            m_xr_st = s.dma("scalar", "dqS", self.xrb[:, i * 8 * CH:(i + 1) * 8 * CH], xrc)
            s.wait("tensor", m_fold)
            s.wait("tensor", m_aff)
            m_g = None
            for m in range(4):
                slot = "q0" if m % 2 == 0 else "q1"
                ph = self.pslot(slot)
                s.puse(slot)
                mm = None
                for k in range(NKC):
                    mm = s.run("tensor", lambda e, m=m, k=k, ph=ph: e.matmul(
                        ph, lw1_b[:, k * 512 + m * 128:k * 512 + (m + 1) * 128], xrt[k],
                        start=(k == 0), stop=(k == NKC - 1)),
                        inc="pe" if k == NKC - 1 else None)
                s.wait("scalar", mm)
                s.wait("scalar", m_lgh_cons)
                m_g = s.run("scalar", lambda e, m=m, ph=ph: e.activation(
                    lgh[:, m * CH:(m + 1) * CH], ph, AF.Gelu,
                    bias=self.lb1_sb[:, m:m + 1]), inc="act")
                s.pread(slot, m_g)
            s.wait("tensor", m_g)
            for f in range(NKC):
                slot = "av0" if f % 2 == 0 else "av1"
                pf = self.pslot(slot)
                s.puse(slot)
                mm = None
                for kk in range(4):
                    mm = s.run("tensor", lambda e, f=f, kk=kk, pf=pf: e.matmul(
                        pf, lw2_b[:, kk * D + f * 128:kk * D + (f + 1) * 128],
                        lgh[:, kk * CH:(kk + 1) * CH],
                        start=(kk == 0), stop=(kk == 3)),
                        inc="pe" if kk == 3 else None)
                lof = lo[f % 2]
                s.wait("vector", mm)
                s.wait("vector", m_lo_cons[f % 2])
                mev = s.run("vector", lambda e, f=f, pf=pf, lof=lof: e.tensor_scalar(
                    lof, pf, self.lb2_sb[:, f:f + 1], None, ALU.add), inc="dve")
                s.pread(slot, mev)
                s.wait("scalar", mev)
                m_lo_cons[f % 2] = s.dma(
                    "scalar", "dqS",
                    self.lightb[:, i * 8 * CH + f * CH:i * 8 * CH + (f + 1) * CH], lof)
            m_lgh_cons = ("pe", s.cnt["pe"])
            m_xr_cons = ("pe", s.cnt["pe"])
        self.m_C_store = s.mark("dqS")
        self.m_C_rows_cons = m_rows_cons
        self.m_C_pe = ("pe", s.cnt["pe"])
        self.m_C_dve = ("dve", s.cnt["dve"])

    def emit_phase_d(self):
        s = self.s
        s.wait("sync", self.m_C_store)
        m_s = s.dma("sync", "dqL", self.r_s[:, :],
                    self.srowb.rearrange("one (p f) -> (one p) f", p=64))
        s.wait("vector", m_s)
        m_a = self.runT("vector", lambda e: e.memset(self.r_acol[:, :], A1),
                        self.r_acol, "dve")
        for it in range(2, N_ITERS + 1):
            s.wait("scalar", m_a)
            m_r = self.runT("scalar", lambda e: e.activation(
                self.r_t[:, :], self.r_s[:, :], AF.Relu, bias=self.r_acol[:, :]),
                self.r_t, "act")
            s.wait("vector", m_r)
            m_t = self.runT("vector", lambda e: e.tensor_sub(
                self.r_t[:, :], self.r_s[:, :], self.r_t[:, :]), self.r_t, "dve")
            s.wait("scalar", m_t)
            m_e = self.runT("scalar", lambda e: e.activation(
                self.r_e[:, :], self.r_t[:, :], AF.Exp), self.r_e, "act")
            s.wait("vector", m_e)
            m_rs = self.runT("vector", lambda e: e.reduce_sum(
                self.r_col[:, :], self.r_e[:, :], axis=AX.X), self.r_col, "dve")
            s.wait("tensor", m_rs)
            s.puse("av0")
            m_cs = s.run("tensor", lambda e: e.matmul(
                self.pav0[0:1, 0:1], self.ones_col[0:64, :], self.r_col[:, :],
                start=True, stop=True), inc="pe")
            s.wait("scalar", m_cs)
            m_ln = self.runT("scalar", lambda e: e.activation(
                self.r_sc1[:, :], self.pav0[0:1, 0:1], AF.Ln), self.r_sc1, "act")
            s.pread("av0", m_ln)
            s.wait("vector", m_ln)
            m_sc = self.runT("vector", lambda e: e.tensor_scalar(
                self.r_sc1[:, :], self.r_sc1[:, :], -1.0, LOGK, ALU.mult, ALU.add),
                self.r_sc1, "dve")
            s.wait("tensor", m_sc)
            s.puse("bc0")
            m_bc = s.run("tensor", lambda e: e.matmul(
                self.pbc0[0:64, 0:1], self.ones_row[0:1, 0:64], self.r_sc1[:, :],
                start=True, stop=True), inc="pe")
            s.wait("vector", m_bc)
            m_a = self.runT("vector", lambda e: e.tensor_copy(
                self.r_acol[:, :], self.pbc0[0:64, 0:1]), self.r_acol, "dve")
            s.pread("bc0", m_a)
        s.wait("vector", m_a)
        m_uu = self.runT("vector", lambda e: e.tensor_scalar(
            self.r_t[:, :], self.r_s[:, :], self.r_acol[:, :], None, ALU.add),
            self.r_t, "dve")
        s.wait("scalar", m_uu)
        m_sg = self.runT("scalar", lambda e: e.activation(
            self.r_m[:, :], self.r_t[:, :], AF.Sign), self.r_m, "act")
        s.wait("vector", m_sg)
        s.run("vector", lambda e: e.tensor_relu(self.r_m[:, :], self.r_m[:, :]))
        s.run("vector", lambda e: e.tensor_copy(self.r_c1[:, :], self.r_m[:, :]))
        src, dst = self.r_c1, self.r_c2
        for sh in (1, 2, 4, 8, 16, 32):
            s.run("vector", lambda e, src=src, dst=dst: e.tensor_copy(dst[:, :], src[:, :]))
            s.run("vector", lambda e, src=src, dst=dst, sh=sh: e.tensor_add(
                dst[:, sh:64], src[:, sh:64], src[:, 0:64 - sh]))
            src, dst = dst, src
        cum = src
        m_ct = self.runT("vector", lambda e: e.reduce_sum(
            self.r_col[:, :], self.r_m[:, :], axis=AX.X), self.r_col, "dve")
        s.wait("gpsimd", m_ct)
        mb0 = s.dma("gpsimd", "dqG", self.offsd[0:1, 0:64],
                    self.r_col[:, :].rearrange("p one -> (p one)")[None, :]
                    if False else self.r_col[:, :])
        s.wait("gpsimd", mb0)
        m_tr = s.dma("gpsimd", "dqG", self.r_row2[0:1, 0:64], self.offsd[0:1, 0:64])
        s.wait("vector", m_tr)
        s.run("vector", lambda e: e.memset(self.r_row1[:, :], 0.0))
        m_ex = s.run("vector", lambda e: e.tensor_copy(self.r_row1[0:1, 1:64],
                                                       self.r_row2[0:1, 0:63]), inc="dve")
        rsrc, rdst = self.r_row1, self.r_row2
        for sh in (1, 2, 4, 8, 16, 32):
            s.run("vector", lambda e, rsrc=rsrc, rdst=rdst: e.tensor_copy(rdst[:, :],
                                                                          rsrc[:, :]))
            s.run("vector", lambda e, rsrc=rsrc, rdst=rdst, sh=sh: e.tensor_add(
                rdst[0:1, sh:64], rsrc[0:1, sh:64], rsrc[0:1, 0:64 - sh]))
            rsrc, rdst = rdst, rsrc
        m_off = self.runT("vector", lambda e, rsrc=rsrc: e.tensor_copy(
            self.tch_v[0:1, 0:1], rsrc[0:1, 0:1]), rsrc, "dve")
        s.wait("gpsimd", m_off)
        mb1 = s.dma("gpsimd", "dqG", self.offsd[0:1, 0:64], rsrc[0:1, 0:64])
        s.wait("gpsimd", mb1)
        m_tr2 = s.dma("gpsimd", "dqG", self.r_col[:, :],
                      self.offsd.rearrange("one (p f) -> (one p) f", p=64))
        s.wait("vector", m_tr2)
        m_cg = s.run("vector", lambda e, cum=cum: e.tensor_scalar(
            cum[:, :], cum[:, :], self.r_col[:, :], None, ALU.add), inc="dve")

        m_t2 = self.runT("vector", lambda e, cum=cum: e.tensor_scalar(
            self.r_t[:, :], cum[:, :], -1.0, 1024.5, ALU.mult, ALU.add), self.r_t, "dve")
        s.wait("scalar", m_t2)
        m_sg2 = self.runT("scalar", lambda e: e.activation(
            self.r_t[:, :], self.r_t[:, :], AF.Sign), self.r_t, "act")
        s.wait("vector", m_sg2)
        s.run("vector", lambda e: e.tensor_relu(self.r_t[:, :], self.r_t[:, :]))
        m_sel = self.runT("vector", lambda e: e.tensor_mul(
            self.r_m[:, :], self.r_m[:, :], self.r_t[:, :]), self.r_m, "dve")
        s.wait("gpsimd", m_sel)
        self.m_sel_st = s.dma("gpsimd", "dqG",
                              self.selbd.rearrange("one (p f) -> (one p) f", p=64),
                              self.r_m[:, :])

    def hw2r(self, k):
        off = 5120 + k * D if k < 27 else 65536 + (k - 27) * D
        return self.ab(off, D)

    def emit_phase_e1(self):
        s = self.s
        hw1_b = self.ab(HW1_O, NKC * DHH)
        xrc = self.ab(0, NKC * CH)
        hht = [self.ab(NKC * CH, CH), self.ab(NKC * CH + CH, CH)]
        m_xr_cons = self.m_C_pe
        m_hh_st = [None, None]
        s.wait("gpsimd", self.m_C_pe)
        s.wait("gpsimd", self.m_C_dve)
        s.wait("gpsimd", self.m_C_store)
        for k in range(32):
            s.dma("gpsimd", "dqH", self.hw2r(k), self.hw2[k * 128:(k + 1) * 128, :])
        self.m_hw2r = s.mark("dqH")
        s.wait("tensor", self.m_hw1)
        s.wait("tensor", self.m_C_pe)
        s.wait("tensor", self.m_C_dve)
        for i in range(NCH):
            s.wait("sync", m_xr_cons)
            s.wait("sync", self.m_C_store)
            m_xr = s.dma("sync", "dqL", xrc, self.xrb[:, i * 8 * CH:(i + 1) * 8 * CH])
            s.wait("tensor", m_xr)
            for m in range(32):
                slot = "av0" if m % 2 == 0 else "av1"
                ph = self.pslot(slot)
                s.puse(slot)
                mm = None
                for k in range(NKC):
                    mm = s.run("tensor", lambda e, m=m, k=k, ph=ph: e.matmul(
                        ph, hw1_b[:, k * DHH + m * 128:k * DHH + (m + 1) * 128],
                        xrc[:, k * CH:(k + 1) * CH],
                        start=(k == 0), stop=(k == NKC - 1)),
                        inc="pe" if k == NKC - 1 else None)
                ht = hht[m % 2]
                s.wait("scalar", mm)
                s.wait("scalar", m_hh_st[m % 2])
                mg = s.run("scalar", lambda e, m=m, ph=ph, ht=ht: e.activation(
                    ht, ph, AF.Gelu, bias=self.hb1_sb[:, m:m + 1]), inc="act")
                s.pread(slot, mg)
                s.wait("gpsimd", mg)
                m_hh_st[m % 2] = s.dma(
                    "gpsimd", "dqG",
                    self.hhb[:, (i * 32 + m) * CH:(i * 32 + m + 1) * CH], ht)
            m_xr_cons = ("pe", s.cnt["pe"])
        self.m_E1_store = s.mark("dqG")
        self.m_E1_pe = ("pe", s.cnt["pe"])
        self.m_E1_act = ("act", s.cnt["act"])

    def emit_phase_e2(self):
        s = self.s
        hhk = [self.ab(HW1_O + b * CH, CH) for b in range(8)]
        ffnc = self.ab(HW1_O + 8 * CH, NKC * CH)
        fft = [ffnc[:, f * CH:(f + 1) * CH] for f in range(NKC)]
        hv = self.ab(HW1_O + 8 * CH + NKC * CH, NKC * CH)
        out1c = self.af(0, NKC * CH)
        o1t = [out1c[:, k * CH:(k + 1) * CH] for k in range(NKC)]
        tmp = self.af(NKC * CH, CH)
        tmp2 = self.af(NKC * CH + CH, CH)
        lof = [self.af(NKC * CH + 2 * CH, CH), self.af(NKC * CH + 3 * CH, CH)]
        sqs = [self.af(NKC * CH + 4 * CH, CH), self.af(NKC * CH + 5 * CH, CH)]
        SLOT8 = ["q0", "q1", "q2", "q3", "av0", "av1", "bc0", "bc1"]

        m_rows_cons = self.m_C_rows_cons
        m_out1_cons = None
        m_hh_cons = [self.m_E1_store] * 8
        m_lo_cons = [None, None]
        m_y_st = None
        m_sel_ld = None
        m_sel_cons = None
        s.wait("sync", self.m_C_pe)
        s.wait("sync", self.m_C_dve)
        s.wait("sync", self.m_C_store)
        s.wait("sync", self.m_E1_pe)
        s.wait("vector", self.m_E1_pe)
        s.wait("vector", self.m_E1_store)
        for i in range(NCH):
            s.wait("sync", m_out1_cons)
            mo1 = None
            for k in range(NKC):
                mo1 = s.dma("sync", "dqL", o1t[k],
                            self.out1b[:, i * 8 * CH + k * CH:i * 8 * CH + (k + 1) * CH])
            s.wait("sync", self.m_sel_st)
            s.wait("sync", m_sel_cons)
            m_sel_ld = s.dma("sync", "dqL", self.dent[0:1, 0:512],
                             self.selbd[0:1, i * CH:(i + 1) * CH])
            def issue_group(gg):
                for k in range(4 * gg, 4 * gg + 4):
                    s.wait("sync", m_hh_cons[k % 8])
                    s.dma("sync", "dqL", hhk[k % 8],
                          self.hhb[:, (i * 32 + k) * CH:(i * 32 + k + 1) * CH])
                return s.mark("dqL")

            issue_group(0)
            m_grp = issue_group(1)
            for g in range(8):
                s.wait("tensor", m_grp)
                if i == 0 and g == 0:
                    s.wait("tensor", self.m_E1_store)
                    s.wait("tensor", self.m_hw2r)
                for k in range(4 * g, 4 * g + 4):
                    w2k = self.hw2r(k)
                    mm = None
                    for f in range(NKC):
                        if k == 0:
                            s.puse(SLOT8[f])
                        pf = self.pslot(SLOT8[f])
                        mm = s.run("tensor", lambda e, k=k, f=f, pf=pf, w2k=w2k: e.matmul(
                            pf, w2k[:, f * 128:(f + 1) * 128],
                            hhk[k % 8], start=(k == 0), stop=(k == 31)),
                            inc="pe" if f == NKC - 1 else None)
                    m_hh_cons[k % 8] = mm
                if g + 2 < 8:
                    m_grp = issue_group(g + 2)
            s.wait("vector", mm)
            m_hv = None
            for f in range(NKC):
                pf = self.pslot(SLOT8[f])
                m_hv = s.run("vector", lambda e, f=f, pf=pf: e.tensor_scalar(
                    hv[:, f * CH:(f + 1) * CH], pf, self.hb2_sb[:, f:f + 1], None, ALU.add),
                    inc="dve")
                s.pread(SLOT8[f], m_hv)
            s.wait("tensor", m_hv)
            s.wait("tensor", m_sel_ld)
            mb = self.bcast("bc0", self.dent[0:1, 0:512])
            s.wait("vector", mb)
            m_ffn = None
            for f in range(NKC):
                lf = lof[f % 2]
                s.wait("sync", m_lo_cons[f % 2])
                ml = s.dma("sync", "dqL", lf,
                           self.lightb[:, i * 8 * CH + f * CH:i * 8 * CH + (f + 1) * CH])
                s.run("vector", lambda e, f=f: e.tensor_mul(
                    tmp, hv[:, f * CH:(f + 1) * CH], self.pslot("bc0")[:, 0:CH]))
                s.wait("vector", ml)
                m_ffn = s.run("vector", lambda e, f=f, lf=lf: e.tensor_add(
                    fft[f], tmp, lf), inc="dve")
                m_lo_cons[f % 2] = m_ffn
            s.pread("bc0", m_ffn)
            m_sel_cons = ("pe", s.cnt["pe"])
            m_stats = self.ln_stats(fft, sqs, bf=True, wait_src=m_ffn)
            s.wait("vector", m_rows_cons)
            s.wait("scalar", m_rows_cons)
            mB, _ = self.ln_rows(m_stats)
            s.wait("gpsimd", mo1)
            s.wait("vector", mo1)
            s.wait("gpsimd", m_y_st)
            m_aff, m_res = self.affine(fft, None, tmp, mB, resid=o1t, pool_out=tmp2)
            m_rows_cons = ("pe", s.cnt["pe"])
            s.wait("scalar", m_res)
            m_y_st = s.dma("scalar", "dqS", self.yT[:, i * 8 * CH:(i + 1) * 8 * CH], out1c)
            m_out1_cons = m_y_st

    def finalize(self):
        s = self.s
        s.wait("sync", s.mark("dqS"))
        s.wait("sync", s.mark("dqG"))
        nc = self.nc
        with nc.Block() as block:
            @block.sync
            def _(e):
                for f in s.ops["sync"]:
                    f(e)

            @block.tensor
            def _(e):
                for f in s.ops["tensor"]:
                    f(e)

            @block.scalar
            def _(e):
                for f in s.ops["scalar"]:
                    f(e)

            @block.vector
            def _(e):
                for f in s.ops["vector"]:
                    f(e)

            @block.gpsimd
            def _(e):
                for f in s.ops["gpsimd"]:
                    f(e)
        return nc

    def build(self):
        order = ["A", "B1", "B2", "C", "D", "E1", "E2"]
        upto = order.index(self.upto)
        self.emit_preamble()
        self.emit_phase_a()
        self.m_A_store = self.s.mark("dqS")
        if upto >= 1:
            self.emit_phase_b(0)
        if upto >= 2:
            self.emit_phase_b(1)
        if upto >= 3:
            self.emit_phase_c()
        if upto >= 4:
            self.emit_phase_d()
        if upto >= 5:
            self.emit_phase_e1()
        if upto >= 6:
            self.emit_phase_e2()
        return self.finalize()


def make_consts():
    cm = np.zeros((128, 4 * CH), np.float32)
    for dj in range(4):
        r = dj * 128
        jj = np.arange(128)[:, None]
        ii = np.arange(CH)[None, :]
        cm[:, dj * CH:(dj + 1) * CH] = np.where(ii < jj + r, NEG, 0.0)
    ident = np.eye(128, dtype=np.float32)
    return cm.astype(BF), ident.astype(BF), ident


def make_in_maps(inputs):
    x = inputs["L0_attn_x"]
    cm, idb, idf = make_consts()

    def col(v):
        return np.ascontiguousarray(np.asarray(v, np.float32).reshape(-1, 1))
    shared = {
        "wq": np.ascontiguousarray(np.asarray(inputs["L0_attn_Wq"], np.float32)),
        "wk": np.ascontiguousarray(np.asarray(inputs["L0_attn_Wk"], np.float32)),
        "wv": np.ascontiguousarray(np.asarray(inputs["L0_attn_Wv"], np.float32)),
        "wo": np.ascontiguousarray(np.asarray(inputs["L0_attn_Wo"], np.float32)),
        "bo": col(inputs["L0_attn_bo"]),
        "rt": col(inputs["L0_ffn_routing_token"]),
        "lgam": col(inputs["L0_ffn_light_gamma"]),
        "lw1": np.ascontiguousarray(np.asarray(inputs["L0_ffn_light_w1"], np.float32)),
        "lb1": col(inputs["L0_ffn_light_b1"]),
        "lw2": np.ascontiguousarray(np.asarray(inputs["L0_ffn_light_w2"], np.float32)),
        "lb2": col(inputs["L0_ffn_light_b2"]),
        "hgam": col(inputs["L0_ffn_heavy_gamma"]),
        "hw1": np.ascontiguousarray(np.asarray(inputs["L0_ffn_heavy_w1"], np.float32)),
        "hb1": col(inputs["L0_ffn_heavy_b1"]),
        "hw2": np.ascontiguousarray(np.asarray(inputs["L0_ffn_heavy_w2"], np.float32)),
        "hb2": col(inputs["L0_ffn_heavy_b2"]),
        "cmaskd": cm, "idbd": idb, "idfd": idf,
    }
    in_maps = []
    for c in range(8):
        m = dict(shared)
        m["xT"] = np.ascontiguousarray(np.asarray(x[c], np.float32).T)
        in_maps.append(m)
    return in_maps


def assemble_y(res):
    ys = []
    for c in range(8):
        yT = np.asarray(res.results[c]["yT"], np.float32)
        arr = yT.reshape(128, NCH, NKC, CH)
        ys.append(np.transpose(arr, (1, 3, 2, 0)).reshape(S, D))
    return np.stack(ys)


def kernel(**inputs):
    kb = KB(taps=False)
    nc = kb.build()
    in_maps = make_in_maps(inputs)
    res = run_bass_kernel_spmd(nc, in_maps, core_ids=list(range(8)))
    return assemble_y(res)


# version 07
# speedup vs baseline: 1.1923x; 1.1923x over previous
import numpy as np
import ml_dtypes
import concourse.bass as bass
import concourse.mybir as mybir
from concourse.bass_utils import run_bass_kernel_spmd

F32 = mybir.dt.float32
BF16 = mybir.dt.bfloat16
AF = mybir.ActivationFunctionType
ALU = mybir.AluOpType
AX = mybir.AxisListType
BF = ml_dtypes.bfloat16

D = 1024
S = 4096
CH = 512
NCH = S // CH
GH = 8
HPG = GH // 2
DL = 512
DHH = 4096
NKC = D // 128
NJ = S // 128
EPS = 1.0e-5
NEG = -1.0e9
N_ITERS = 50
LOGK = float(np.log(np.float32(1152.0)))
A1 = float(np.log(np.float32(1152.0)) - np.log(np.float32(4096.0)))

ENGINES = ("sync", "tensor", "scalar", "vector", "gpsimd")

KBUF_O = 0
VBUF_O = 16384
AEX_O = 33024
HW1_O = 32768
ARB_N = 71680
ARF_N = 7680


def later(a, b):
    if a is None:
        return b
    if b is None:
        return a
    assert a[0] == b[0], (a, b)
    return a if a[1] >= b[1] else b


class Sched:
    def __init__(self, nc):
        self.nc = nc
        self.ops = {e: [] for e in ENGINES}
        self.sem = {}
        self.cnt = {}
        self.waited = {e: {} for e in ENGINES}
        for name in ("pe", "act", "dve", "pool", "dqL", "dqS", "dqG", "dqH"):
            self.sem[name] = nc.alloc_semaphore(name)
            self.cnt[name] = 0
        self.pread_m = {}

    OWN = {"tensor": "pe", "scalar": "act", "vector": "dve", "gpsimd": "pool"}

    def wait(self, eng, mst):
        if mst is None:
            return
        name, val = mst
        if self.OWN.get(eng) == name:
            return
        if val <= 0 or val <= self.waited[eng].get(name, 0):
            return
        self.waited[eng][name] = val
        h = self.sem[name]
        self.ops[eng].append(lambda e, h=h, v=val: e.wait_ge(h, v))

    def run(self, eng, fn, inc=None):
        if inc is None:
            self.ops[eng].append(fn)
            return None
        self.cnt[inc] += 1
        v = self.cnt[inc]
        h = self.sem[inc]

        def g(e, fn=fn, h=h):
            fn(e).then_inc(h, 1)
        self.ops[eng].append(g)
        return (inc, v)

    def dma(self, eng, qsem, out, in_, **kw):
        self.cnt[qsem] += 16
        v = self.cnt[qsem]
        h = self.sem[qsem]

        def g(e, out=out, in_=in_, h=h, kw=kw):
            e.dma_start(out=out, in_=in_, **kw).then_inc(h, 16)
        self.ops[eng].append(g)
        return (qsem, v)

    def mark(self, name):
        return (name, self.cnt[name])

    def puse(self, slot):
        for name, val in self.pread_m.get(slot, {}).items():
            self.wait("tensor", (name, val))

    def pread(self, slot, mst):
        if mst is None:
            return
        d = self.pread_m.setdefault(slot, {})
        name, val = mst
        d[name] = max(d.get(name, 0), val)


class KB:
    def __init__(self, taps=False, upto="E2"):
        nc = bass.Bass("TRN2", target_bir_lowering=False)
        self.nc = nc
        self.upto = upto
        self.s = Sched(nc)
        ktap = "ExternalOutput" if taps else "Internal"
        dt = nc.dram_tensor

        self.xT = dt("xT", [D, S], F32, kind="ExternalInput").ap()
        self.wq = dt("wq", [D, D], F32, kind="ExternalInput").ap()
        self.wk = dt("wk", [D, D], F32, kind="ExternalInput").ap()
        self.wv = dt("wv", [D, D], F32, kind="ExternalInput").ap()
        self.wo = dt("wo", [D, D], F32, kind="ExternalInput").ap()
        self.bo = dt("bo", [D, 1], F32, kind="ExternalInput").ap()
        self.rt = dt("rt", [D, 1], F32, kind="ExternalInput").ap()
        self.lgam = dt("lgam", [D, 1], F32, kind="ExternalInput").ap()
        self.lw1 = dt("lw1", [D, DL], F32, kind="ExternalInput").ap()
        self.lb1 = dt("lb1", [DL, 1], F32, kind="ExternalInput").ap()
        self.lw2 = dt("lw2", [DL, D], F32, kind="ExternalInput").ap()
        self.lb2 = dt("lb2", [D, 1], F32, kind="ExternalInput").ap()
        self.hgam = dt("hgam", [D, 1], F32, kind="ExternalInput").ap()
        self.hw1 = dt("hw1", [D, DHH], F32, kind="ExternalInput").ap()
        self.hb1 = dt("hb1", [DHH, 1], F32, kind="ExternalInput").ap()
        self.hw2 = dt("hw2", [DHH, D], F32, kind="ExternalInput").ap()
        self.hb2 = dt("hb2", [D, 1], F32, kind="ExternalInput").ap()
        self.cmaskd = dt("cmaskd", [128, 4 * CH], BF16, kind="ExternalInput").ap()
        self.idbd = dt("idbd", [128, 128], BF16, kind="ExternalInput").ap()
        self.idfd = dt("idfd", [128, 128], F32, kind="ExternalInput").ap()

        self.yT = dt("yT", [128, NCH * 8 * CH], F32, kind="ExternalOutput").ap()
        self.xlnb = dt("xlnb", [128, NCH * 8 * CH], BF16, kind=ktap).ap()
        self.k1b = dt("k1b", [128, HPG * S], BF16, kind=ktap).ap()
        self.v1b = dt("v1b", [128, NJ * GH * 65], BF16, kind=ktap).ap()
        self.attg0b = dt("attg0b", [128, NCH * HPG * CH], BF16, kind=ktap).ap()
        self.out1b = dt("out1b", [128, NCH * 8 * CH], F32, kind=ktap).ap()
        self.xrb = dt("xrb", [128, NCH * 8 * CH], BF16, kind=ktap).ap()
        self.lightb = dt("lightb", [128, NCH * 8 * CH], BF16, kind=ktap).ap()
        self.srowb = dt("srowb", [1, S], F32, kind=ktap).ap()
        self.dbgb = dt("dbgb", [4, S], F32, kind=ktap).ap()
        self.offsd = dt("offsd", [1, 64], F32, kind="Internal").ap()
        self.selbd = dt("selbd", [1, S], F32, kind=ktap).ap()
        self.hhb = dt("hhb", [128, NCH * 32 * CH], BF16, kind=ktap).ap()

        self.arb = nc.alloc_sbuf_tensor("arb", [128, ARB_N], BF16)
        self.arf = nc.alloc_sbuf_tensor("arf", [128, ARF_N], F32)

        self.pqk = nc.alloc_psum_tensor("pqk", [128, 4 * 512], F32)
        self.pav0 = nc.alloc_psum_tensor("pav0", [128, 512], F32)
        self.pav1 = nc.alloc_psum_tensor("pav1", [128, 512], F32)
        self.pbc0 = nc.alloc_psum_tensor("pbc0", [128, 512], F32)
        self.pbc1 = nc.alloc_psum_tensor("pbc1", [128, 512], F32)

        self.ones_col = nc.alloc_sbuf_tensor("ones_col", [128, 1], F32)
        self.ones_bcol = nc.alloc_sbuf_tensor("ones_bcol", [128, 1], BF16)
        self.ones_row = nc.alloc_sbuf_tensor("ones_row", [128, 128], F32)
        self.eps_ap = nc.alloc_sbuf_tensor("eps_ap", [1, 1], F32)
        self.bo_sb = nc.alloc_sbuf_tensor("bo_sb", [128, 8], F32)
        self.lb2_sb = nc.alloc_sbuf_tensor("lb2_sb", [128, 8], F32)
        self.hb2_sb = nc.alloc_sbuf_tensor("hb2_sb", [128, 8], F32)
        self.lgam_sb = nc.alloc_sbuf_tensor("lgam_sb", [128, 8], F32)
        self.hgam_sb = nc.alloc_sbuf_tensor("hgam_sb", [128, 8], F32)
        self.lb1_sb = nc.alloc_sbuf_tensor("lb1_sb", [128, 4], F32)
        self.hb1_sb = nc.alloc_sbuf_tensor("hb1_sb", [128, 32], F32)
        self.rt_sb = nc.alloc_sbuf_tensor("rt_sb", [128, 8], F32)
        self.R_sb = nc.alloc_sbuf_tensor("R_sb", [128, 1], F32)
        self.R_sc = nc.alloc_sbuf_tensor("R_sc", [1, 1], F32)
        self.cm_sb = nc.alloc_sbuf_tensor("cm_sb", [128, 4 * CH], BF16)
        self.idb_sb = nc.alloc_sbuf_tensor("idb_sb", [128, 128], BF16)
        self.idf_sb = nc.alloc_sbuf_tensor("idf_sb", [128, 128], F32)
        self.tch_v = nc.alloc_sbuf_tensor("tch_v", [128, 1], F32)
        self.tch_a = nc.alloc_sbuf_tensor("tch_a", [128, 1], F32)
        self.tch_p = nc.alloc_sbuf_tensor("tch_p", [128, 1], F32)
        self.rows = nc.alloc_sbuf_tensor("rows", [128, 3072], F32)
        self.dent = nc.alloc_sbuf_tensor("dent", [128, 1024], F32)

        self.r_s = nc.alloc_sbuf_tensor("r_s", [64, 64], F32)
        self.r_t = nc.alloc_sbuf_tensor("r_t", [64, 64], F32)
        self.r_e = nc.alloc_sbuf_tensor("r_e", [64, 64], F32)
        self.r_m = nc.alloc_sbuf_tensor("r_m", [64, 64], F32)
        self.r_c1 = nc.alloc_sbuf_tensor("r_c1", [64, 64], F32)
        self.r_c2 = nc.alloc_sbuf_tensor("r_c2", [64, 64], F32)
        self.r_col = nc.alloc_sbuf_tensor("r_col", [64, 1], F32)
        self.r_acol = nc.alloc_sbuf_tensor("r_acol", [64, 1], F32)
        self.r_row1 = nc.alloc_sbuf_tensor("r_row1", [1, 64], F32)
        self.r_row2 = nc.alloc_sbuf_tensor("r_row2", [1, 64], F32)
        self.r_sc1 = nc.alloc_sbuf_tensor("r_sc1", [1, 1], F32)

    def ab(self, off, n):
        assert off + n <= ARB_N, (off, n)
        return self.arb[:, off:off + n]

    def af(self, off, n):
        assert off + n <= ARF_N, (off, n)
        return self.arf[:, off:off + n]

    def rseg(self, lane, off, n=CH):
        return self.rows[lane:lane + 1, off:off + n]

    def pslot(self, name):
        if name.startswith("q"):
            i = int(name[1])
            return self.pqk[:, i * 512:(i + 1) * 512]
        return {"av0": self.pav0, "av1": self.pav1,
                "bc0": self.pbc0, "bc1": self.pbc1}[name][:, 0:512]

    def runT(self, eng, fn, out_ap, inc, base=0):
        s = self.s
        s.run(eng, fn)
        src = out_ap[0:1, 0:1]
        if eng == "vector":
            dst = self.tch_v[base:base + 1, 0:1]
            return s.run("vector", lambda e, src=src, dst=dst: e.tensor_copy(dst, src),
                         inc=inc)
        if eng == "scalar":
            dst = self.tch_a[base:base + 1, 0:1]
            return s.run("scalar", lambda e, src=src, dst=dst: e.activation(
                dst, src, AF.Identity), inc=inc)
        if eng == "gpsimd":
            dst = self.tch_p[base:base + 1, 0:1]
            return s.run("gpsimd", lambda e, src=src, dst=dst: e.tensor_copy(dst, src),
                         inc=inc)
        raise ValueError(eng)

    def bcast(self, slot, row_ap, nrows=128, base=0):
        s = self.s
        s.puse(slot)
        p = self.pslot(slot)[0:nrows, :row_ap.shape[-1]]
        return s.run("tensor", lambda e, p=p, r=row_ap, n=nrows, b=base: e.matmul(
            p, self.ones_row[b:b + 1, 0:n], r, start=True, stop=True), inc="pe")

    def emit_preamble(self):
        s = self.s
        s.run("vector", lambda e: e.memset(self.ones_col[:, :], 1.0))
        s.run("vector", lambda e: e.memset(self.ones_bcol[:, :], 1.0))
        s.run("vector", lambda e: e.memset(self.ones_row[:, :], 1.0))
        self.m_ones = self.runT("vector", lambda e: e.memset(self.eps_ap[:, :], EPS),
                                self.eps_ap, "dve")
        for dst, src in [
            (self.bo_sb, self.bo), (self.lb2_sb, self.lb2), (self.hb2_sb, self.hb2),
            (self.lgam_sb, self.lgam), (self.hgam_sb, self.hgam), (self.rt_sb, self.rt),
            (self.lb1_sb, self.lb1), (self.hb1_sb, self.hb1),
        ]:
            s.dma("sync", "dqL", dst[:, :], src.rearrange("(c p) one -> p (c one)", p=128),
                  allow_slow_non_contiguous=True)
        s.dma("sync", "dqL", self.cm_sb[:, :], self.cmaskd[:, :])
        s.dma("sync", "dqL", self.idb_sb[:, :], self.idbd[:, :])
        self.m_consts = s.dma("sync", "dqL", self.idf_sb[:, :], self.idfd[:, :])
        s.wait("tensor", self.m_consts)
        s.wait("tensor", self.m_ones)
        s.wait("vector", self.m_consts)
        mrow = self.runT("vector", lambda e: e.reduce_sum(self.R_sb[:, :], self.rt_sb[:, :],
                                                          axis=AX.X), self.R_sb, "dve")
        s.wait("tensor", mrow)
        m = s.run("tensor", lambda e: e.matmul(self.pbc0[0:1, 0:1], self.ones_col[:, :],
                                               self.R_sb[:, :], start=True, stop=True), inc="pe")
        s.wait("vector", m)
        self.m_R = s.run("vector", lambda e: e.tensor_copy(self.R_sc[:, :],
                                                           self.pbc0[0:1, 0:1]), inc="dve")
        s.pread("bc0", self.m_R)

    def ln_stats(self, tiles, sq_scr, bf=False, wait_src=None, slots=("av0", "av1")):
        s = self.s
        ones = self.ones_bcol if bf else self.ones_col
        sl0, sl1 = slots
        p0 = self.pslot(sl0)[0:1, 0:CH]
        p1 = self.pslot(sl1)[0:1, 0:CH]
        s.puse(sl0)
        s.puse(sl1)
        m_sum = m_sq = None
        m_sq_mm = [None, None]
        for k in range(NKC):
            t = tiles[k]
            if wait_src is not None:
                s.wait("tensor", wait_src)
                s.wait("scalar", wait_src)
            m_sum = s.run("tensor", lambda e, t=t, k=k, o=ones: e.matmul(
                p0, o[:, :], t, start=(k == 0), stop=(k == NKC - 1)),
                inc="pe" if k == NKC - 1 else None)
            scr = sq_scr[k % 2]
            s.wait("scalar", m_sq_mm[k % 2])
            ma = s.run("scalar", lambda e, t=t, scr=scr: e.activation(scr, t, AF.Square),
                       inc="act")
            s.wait("tensor", ma)
            m_sq = m_sq_mm[k % 2] = s.run("tensor", lambda e, scr=scr, k=k: e.matmul(
                p1, self.ones_col[:, :], scr,
                start=(k == 0), stop=(k == NKC - 1)), inc="pe")
        s.wait("vector", m_sum)
        s.wait("vector", m_sq)
        s.run("vector", lambda e: e.tensor_copy(self.rseg(0, 0), p0))
        m = s.run("vector", lambda e: e.tensor_copy(self.rseg(0, 512), p1),
                  inc="dve")
        s.pread(sl0, m)
        s.pread(sl1, m)
        return m

    def ln_rows(self, m_stats, rms=False):
        s = self.s
        SUMS = self.rseg(0, 0)
        SQ = self.rseg(0, 512)
        VAR = self.rseg(0, 1024)
        MEAN = self.rseg(0, 1536)
        Aseg = self.rseg(0, 2048)
        Bseg = self.rseg(0, 2560)
        s.wait("vector", m_stats)
        s.run("vector", lambda e: e.tensor_scalar(MEAN, SUMS, 1.0 / D, None, ALU.mult))
        s.run("vector", lambda e: e.tensor_scalar(VAR, SQ, 1.0 / D, None, ALU.mult))
        s.run("vector", lambda e: e.scalar_tensor_tensor(SUMS, MEAN, 1.0, MEAN, ALU.mult,
                                                         ALU.mult))
        m = self.runT("vector", lambda e: e.tensor_sub(VAR, VAR, SUMS), VAR, "dve")
        s.wait("scalar", m)
        s.wait("scalar", self.m_ones)
        s.run("scalar", lambda e: e.activation(Aseg, VAR, AF.Ln, bias=self.eps_ap[:, :]))
        m_rms = None
        if rms:
            ml2 = self.runT("scalar", lambda e: e.activation(VAR, VAR, AF.Ln), VAR, "act")
            s.wait("vector", ml2)
            s.run("vector", lambda e: e.tensor_sub(VAR, Aseg, VAR))
            mv = self.runT("vector", lambda e: e.tensor_scalar(VAR, VAR, 0.5, None, ALU.mult),
                           VAR, "dve")
            s.wait("scalar", mv)
            m_rms = self.runT("scalar", lambda e: e.activation(VAR, VAR, AF.Exp), VAR, "act")
        mA = self.runT("scalar", lambda e: e.activation(Aseg, Aseg, AF.Exp, scale=-0.5),
                       Aseg, "act")
        s.wait("vector", mA)
        mB = self.runT("vector", lambda e: e.scalar_tensor_tensor(
            Bseg, MEAN, -1.0, Aseg, ALU.mult, ALU.mult), Bseg, "dve")
        return mB, m_rms

    def affine(self, src_tiles, dst_tiles, tmp, m_ready, resid=None, pool_out=None,
               slots=("bc0", "bc1")):
        s = self.s
        m_aff = None
        m_res = None
        sa, sb = slots
        s.wait("tensor", m_ready)
        mA_ = self.bcast(sa, self.rseg(0, 2048))
        mB_ = self.bcast(sb, self.rseg(0, 2560))
        s.wait("vector", mA_)
        s.wait("vector", mB_)
        for k in range(NKC):
            s.run("vector", lambda e, k=k: e.tensor_mul(
                tmp, src_tiles[k], self.pslot(sa)[:, 0:CH]))
            if resid is None:
                m_aff = s.run("vector", lambda e, k=k: e.tensor_add(
                    dst_tiles[k], tmp, self.pslot(sb)), inc="dve")
            else:
                tmp2 = pool_out
                s.wait("vector", m_res)
                m_aff = s.run("vector", lambda e, k=k, tmp2=tmp2: e.tensor_add(
                    tmp2, tmp, self.pslot(sb)), inc="dve")
                s.wait("gpsimd", m_aff)
                m_res = s.run("gpsimd", lambda e, k=k, tmp2=tmp2: e.tensor_add(
                    resid[k], resid[k], tmp2), inc="pool")
        s.pread(sa, m_aff)
        s.pread(sb, m_aff)
        return m_aff, m_res

    def emit_phase_a(self):
        s = self.s
        o = AEX_O
        wk_b = self.ab(o, NKC * D); o += NKC * D
        wv_b = self.ab(o, NKC * D); o += NKC * D
        xlnw = self.ab(o, 2 * NKC * CH); o += 2 * NKC * CH
        ktmp = [self.ab(o, CH), self.ab(o + CH, CH)]; o += 2 * CH
        vtmp = [self.ab(o, GH * 65), self.ab(o + GH * 65 + 8, GH * 65)]
        o += 2 * (GH * 65 + 8)
        assert o <= ARB_N, o
        xc = self.af(0, NKC * CH)
        xct = [xc[:, k * CH:(k + 1) * CH] for k in range(NKC)]
        sqs = [self.af(NKC * CH, CH), self.af(NKC * CH + CH, CH)]
        tmp = self.af(NKC * CH + 2 * CH, CH)
        kbuf = self.ab(KBUF_O, HPG * S)
        vbuf = self.ab(VBUF_O, NJ * GH * 65)
        vview = vbuf.rearrange("p (t e) -> p t e", e=65)

        s.run("vector", lambda e: e.memset(vview[:, :, 64:65], 1.0))
        for b in range(2):
            vv = vtmp[b].rearrange("p (t e) -> p t e", e=65)[:, 0:GH, :]
            s.run("vector", lambda e, vv=vv: e.memset(vv[:, :, 64:65], 1.0))
        m_vones = self.runT("vector", lambda e: e.tensor_copy(
            self.tch_v[0:1, 0:1], vbuf[0:1, 64:65]), vbuf, "dve")

        for k in range(NKC):
            s.dma("gpsimd", "dqG", wk_b[:, k * D:(k + 1) * D], self.wk[k * 128:(k + 1) * 128, :])
            s.dma("gpsimd", "dqG", wv_b[:, k * D:(k + 1) * D], self.wv[k * 128:(k + 1) * 128, :])
        m_w = s.mark("dqG")

        m_xc_cons = None
        m_xln_cons = [None, None]
        m_xln_st = [None, None]
        m_rows_cons = None
        m_ktmp = [None, None]
        m_vtmp = [None, None]
        for i in range(NCH):
            par = i % 2
            xln = xlnw[:, par * NKC * CH:(par + 1) * NKC * CH]
            xlt = [xln[:, k * CH:(k + 1) * CH] for k in range(NKC)]
            s.wait("sync", m_xc_cons)
            mld = None
            for k in range(NKC):
                mld = s.dma("sync", "dqL", xct[k],
                            self.xT[k * 128:(k + 1) * 128, i * CH:(i + 1) * CH])
            s.wait("tensor", mld)
            s.wait("scalar", mld)
            s.wait("tensor", self.m_ones)
            m_stats = self.ln_stats(xct, sqs)
            s.wait("vector", m_rows_cons)
            s.wait("scalar", m_rows_cons)
            mB, _ = self.ln_rows(m_stats)
            s.wait("vector", m_xln_cons[par])
            s.wait("vector", m_xln_st[par])
            m_aff, _ = self.affine(xct, xlt, tmp, mB)
            m_rows_cons = ("pe", s.cnt["pe"])
            m_xc_cons = m_aff
            s.wait("scalar", m_aff)
            m_xln_st[par] = s.dma("scalar", "dqS",
                                  self.xlnb[:, i * 8 * CH:(i + 1) * 8 * CH], xln)

            s.wait("tensor", m_w)
            s.wait("tensor", m_aff)
            for oc in range(8):
                slot = "q0" if oc % 2 == 0 else "q1"
                po = self.pslot(slot)
                s.puse(slot)
                mm = None
                for k in range(NKC):
                    mm = s.run("tensor", lambda e, oc=oc, k=k, po=po, xlt=xlt: e.matmul(
                        po, wk_b[:, k * D + oc * 128:k * D + (oc + 1) * 128], xlt[k],
                        start=(k == 0), stop=(k == NKC - 1)),
                        inc="pe" if k == NKC - 1 else None)
                s.wait("vector", mm)
                if oc < HPG:
                    dst = kbuf[:, oc * S + i * CH:oc * S + (i + 1) * CH]
                    mev = s.run("vector", lambda e, dst=dst, po=po: e.tensor_copy(dst, po),
                                inc="dve")
                else:
                    kt = ktmp[oc % 2]
                    s.wait("vector", m_ktmp[oc % 2])
                    mev = s.run("vector", lambda e, kt=kt, po=po: e.tensor_copy(kt, po),
                                inc="dve")
                    s.wait("scalar", mev)
                    m_ktmp[oc % 2] = s.dma(
                        "scalar", "dqS",
                        self.k1b[:, (oc - HPG) * S + i * CH:(oc - HPG) * S + (i + 1) * CH], kt)
                s.pread(slot, mev)
            s.wait("vector", m_vones)
            for t4 in range(4):
                j = i * 4 + t4
                for half in range(2):
                    slot = "av0" if half == 0 else "av1"
                    pv = self.pslot(slot)
                    s.puse(slot)
                    mm = None
                    for k in range(NKC):
                        mm = s.run("tensor", lambda e, t4=t4, k=k, half=half, pv=pv, xlt=xlt:
                                   e.matmul(pv,
                                            xlt[k][:, t4 * 128:(t4 + 1) * 128],
                                            wv_b[:, k * D + half * 512:k * D + (half + 1) * 512],
                                            start=(k == 0), stop=(k == NKC - 1)),
                                   inc="pe" if k == NKC - 1 else None)
                    s.wait("vector", mm)
                    src = pv.rearrange("p (h d) -> p h d", d=64)
                    if half == 0:
                        dv = vview[:, j * GH:(j + 1) * GH, 0:64]
                        mev = s.run("vector", lambda e, dv=dv, src=src: e.tensor_copy(dv, src),
                                    inc="dve")
                    else:
                        vt = vtmp[t4 % 2]
                        s.wait("vector", m_vtmp[t4 % 2])
                        dvv = vt.rearrange("p (t e) -> p t e", e=65)[:, 0:GH, 0:64]
                        mev = s.run("vector", lambda e, dvv=dvv, src=src: e.tensor_copy(dvv, src),
                                    inc="dve")
                        s.wait("scalar", mev)
                        m_vtmp[t4 % 2] = s.dma(
                            "scalar", "dqS",
                            self.v1b[:, j * GH * 65:(j + 1) * GH * 65],
                            vt[:, 0:GH * 65])
                    s.pread(slot, mev)
            m_xln_cons[par] = ("pe", s.cnt["pe"])
        self.m_A_kv = ("dve", s.cnt["dve"])
        self.m_A_rows_cons = m_rows_cons
        self.m_A_pe = ("pe", s.cnt["pe"])

    def emit_phase_b(self, g):
        s = self.s
        kbuf = self.ab(KBUF_O, HPG * S)
        vbuf = self.ab(VBUF_O, NJ * GH * 65)
        vview = vbuf.rearrange("p (t e) -> p t e", e=65)
        o = AEX_O
        wqh = self.ab(o, NKC * 512); o += NKC * 512
        if g == 1:
            wo_b = self.ab(o, NKC * D); o += NKC * D
        xlnw = self.ab(o, 2 * NKC * CH); o += 2 * NKC * CH
        qcw = self.ab(o, 2 * HPG * CH); o += 2 * HPG * CH
        attcw = self.ab(o, 2 * HPG * CH); o += 2 * HPG * CH
        if g == 1:
            ag0 = self.ab(o, HPG * CH); o += HPG * CH
            obuf = self.ab(o, NKC * CH); o += NKC * CH
            obt = [obuf[:, k * CH:(k + 1) * CH] for k in range(NKC)]
        ebuf = self.ab(o, 3 * 1024); o += 3 * 1024
        assert o <= ARB_N, o
        xc = self.af(0, NKC * CH)
        xct = [xc[:, k * CH:(k + 1) * CH] for k in range(NKC)]
        tmp = self.af(NKC * CH, CH)
        tmp2 = self.af(NKC * CH + CH, CH)
        sqs = [self.af(NKC * CH + 2 * CH, CH), self.af(NKC * CH + 3 * CH, CH)]

        if g == 1:
            s.wait("sync", self.m_B1_pe)
            s.wait("sync", self.m_A_store)
            s.dma("sync", "dqL", kbuf, self.k1b[:, :])
            m_kv_ready = s.dma("sync", "dqL", vview[:, :, 0:64],
                               self.v1b.rearrange("p (t e) -> p t e", e=65)[:, :, 0:64])
        else:
            m_kv_ready = None
        m_prev = self.m_A_pe if g == 0 else self.m_B1_pe
        m_prev_st = self.m_A_store if g == 0 else self.m_B1_store
        m_prev_dve = self.m_A_kv if g == 0 else self.m_B1_dve
        for eng in ("gpsimd", "vector", "scalar", "sync"):
            s.wait(eng, m_prev)
            s.wait(eng, m_prev_st)
            s.wait(eng, m_prev_dve)
        for k in range(NKC):
            s.dma("gpsimd", "dqG", wqh[:, k * 512:(k + 1) * 512],
                  self.wq[k * 128:(k + 1) * 128, g * 512:(g + 1) * 512])
        if g == 1:
            for k in range(NKC):
                s.dma("gpsimd", "dqG", wo_b[:, k * D:(k + 1) * D],
                      self.wo[k * 128:(k + 1) * 128, :])
        m_w = s.mark("dqG")

        m_qc_cons = [None, None]
        m_xln_cons = [m_prev, m_prev]
        m_attc_cons = [None, None]
        m_ag0_cons = m_prev
        m_xc_cons = self.m_A_kv if g == 1 else None
        m_rows_cons = self.m_A_rows_cons if g == 0 else self.m_B_rows_cons
        m_u = None
        m_av_hist = [None, None, None]
        for i in range(NCH):
            jmax = 4 * (i + 1)
            par = i % 2
            xln = xlnw[:, par * NKC * CH:(par + 1) * NKC * CH]
            xlt = [xln[:, k * CH:(k + 1) * CH] for k in range(NKC)]
            qc = qcw[:, par * HPG * CH:(par + 1) * HPG * CH]
            attc = attcw[:, par * HPG * CH:(par + 1) * HPG * CH]
            s.wait("sync", m_xln_cons[par])
            s.wait("sync", self.m_A_store)
            m_xln = s.dma("sync", "dqL", xln, self.xlnb[:, i * 8 * CH:(i + 1) * 8 * CH])
            if g == 1:
                s.wait("sync", m_ag0_cons)
                s.wait("sync", self.m_B1_store)
                m_ag0 = s.dma("sync", "dqL", ag0,
                              self.attg0b[:, i * HPG * CH:(i + 1) * HPG * CH])
                s.wait("sync", m_xc_cons)
                m_xc = None
                for k in range(NKC):
                    m_xc = s.dma("sync", "dqL", xct[k],
                                 self.xT[k * 128:(k + 1) * 128, i * CH:(i + 1) * CH])
            s.wait("tensor", m_w)
            s.wait("tensor", m_xln)
            if g == 0:
                s.wait("tensor", self.m_A_kv)
            else:
                s.wait("tensor", m_kv_ready)
            m_q = None
            for hp in range(HPG):
                slot = "av0" if hp % 2 == 0 else "av1"
                pq = self.pslot(slot)
                s.puse(slot)
                mm = None
                for k in range(NKC):
                    mm = s.run("tensor", lambda e, hp=hp, k=k, pq=pq, xlt=xlt: e.matmul(
                        pq, wqh[:, k * 512 + hp * 128:k * 512 + (hp + 1) * 128], xlt[k],
                        start=(k == 0), stop=(k == NKC - 1)),
                        inc="pe" if k == NKC - 1 else None)
                s.wait("vector", mm)
                s.wait("vector", m_qc_cons[par])
                m_q = s.run("vector", lambda e, hp=hp, pq=pq, qc=qc: e.tensor_copy(
                    qc[:, hp * CH:(hp + 1) * CH], pq), inc="dve")
                s.pread(slot, m_q)
            s.wait("tensor", m_q)
            for hp in range(HPG):
                mqk_hist = {}

                def emit_qk(j, hp=hp, i=i, qc=qc):
                    sp = j % 2
                    sa, sb = f"q{2 * sp}", f"q{2 * sp + 1}"
                    pa, pb = self.pslot(sa), self.pslot(sb)
                    diag = j >= 4 * i
                    s.puse(sa)
                    s.puse(sb)
                    s.run("tensor", lambda e, pa=pa, j=j, diag=diag: e.matmul(
                        pa, kbuf[0:64, hp * S + j * 128:hp * S + (j + 1) * 128],
                        qc[0:64, hp * CH:(hp + 1) * CH],
                        start=True, stop=not diag, tile_position=(0, 0)), inc=None)
                    if diag:
                        dj = j - 4 * i
                        s.run("tensor", lambda e, pa=pa, dj=dj: e.matmul(
                            pa, self.idb_sb[:, :], self.cm_sb[:, dj * CH:(dj + 1) * CH],
                            start=False, stop=True), inc=None)
                    mqk = s.run("tensor", lambda e, pb=pb, j=j, diag=diag: e.matmul(
                        pb, kbuf[64:128, hp * S + j * 128:hp * S + (j + 1) * 128],
                        qc[64:128, hp * CH:(hp + 1) * CH],
                        start=True, stop=not diag, tile_position=(64, 0)),
                        inc=None if diag else "pe")
                    if diag:
                        dj = j - 4 * i
                        mqk = s.run("tensor", lambda e, pb=pb, dj=dj: e.matmul(
                            pb, self.idb_sb[:, :], self.cm_sb[:, dj * CH:(dj + 1) * CH],
                            start=False, stop=True), inc="pe")
                    mqk_hist[j] = mqk

                emit_qk(0)
                for j in range(jmax):
                    sp = j % 2
                    se = j % 3
                    sa, sb = f"q{2 * sp}", f"q{2 * sp + 1}"
                    if j + 1 < jmax:
                        emit_qk(j + 1)
                    s.wait("scalar", mqk_hist.pop(j))
                    s.wait("scalar", m_av_hist[se])
                    m_exp = s.run("scalar", lambda e, sp=sp, se=se: e.activation(
                        ebuf[:, se * 1024:(se + 1) * 1024],
                        self.pqk[:, (2 * sp) * 512:(2 * sp + 2) * 512],
                        AF.Exp, scale=0.125), inc="act")
                    s.pread(sa, m_exp)
                    s.pread(sb, m_exp)
                    s.wait("tensor", m_exp)
                    if j == 0:
                        s.puse("av0")
                        s.puse("av1")
                    s.run("tensor", lambda e, hp=hp, j=j, se=se, jmax=jmax: e.matmul(
                        self.pav0[0:65, :], vview[:, j * GH + 2 * hp, :],
                        ebuf[:, se * 1024:se * 1024 + 512],
                        start=(j == 0), stop=(j == jmax - 1)), inc=None)
                    m_av = s.run("tensor", lambda e, hp=hp, j=j, se=se, jmax=jmax: e.matmul(
                        self.pav1[0:65, :], vview[:, j * GH + 2 * hp + 1, :],
                        ebuf[:, se * 1024 + 512:(se + 1) * 1024],
                        start=(j == 0), stop=(j == jmax - 1)), inc="pe")
                    m_av_hist[se] = m_av
                s.wait("vector", m_av)
                s.wait("vector", m_attc_cons[par])
                s.run("vector", lambda e, hp=hp, attc=attc: e.tensor_copy(
                    attc[0:64, hp * CH:(hp + 1) * CH], self.pav0[0:64, :]))
                s.run("vector", lambda e, hp=hp, attc=attc: e.tensor_copy(
                    attc[64:128, hp * CH:(hp + 1) * CH], self.pav1[0:64, :]))
                s.run("vector", lambda e: e.tensor_copy(
                    self.dent[64:65, 0:512], self.pav0[64:65, :]))
                m_den = self.runT("vector", lambda e: e.tensor_copy(
                    self.dent[64:65, 512:1024], self.pav1[64:65, :]),
                    self.dent[64:65, :], "dve", base=64)
                s.pread("av0", m_den)
                s.pread("av1", m_den)
                s.wait("scalar", m_den)
                s.run("scalar", lambda e: e.activation(
                    self.dent[64:65, 0:1024], self.dent[64:65, 0:1024], AF.Ln))
                m_rc = self.runT("scalar", lambda e: e.activation(
                    self.dent[64:65, 0:1024], self.dent[64:65, 0:1024], AF.Exp, scale=-1.0),
                    self.dent[64:65, :], "act", base=64)
                s.wait("tensor", m_rc)
                for ab in range(2):
                    slot = "bc0" if ab == 0 else "bc1"
                    mb = self.bcast(slot, self.dent[64:65, ab * 512:(ab + 1) * 512],
                                    nrows=64, base=64)
                    s.wait("vector", mb)
                    sl = attc[ab * 64:(ab + 1) * 64, hp * CH:(hp + 1) * CH]
                    m_u = s.run("vector", lambda e, sl=sl, slot=slot: e.tensor_mul(
                        sl, sl, self.pslot(slot)[0:64, :]), inc="dve")
                    s.pread(slot, m_u)
            if g == 0:
                s.wait("scalar", m_u)
                m_attc_cons[par] = s.dma("scalar", "dqS",
                                         self.attg0b[:, i * HPG * CH:(i + 1) * HPG * CH],
                                         attc)
                m_qc_cons[par] = ("pe", s.cnt["pe"])
                m_xln_cons[par] = ("pe", s.cnt["pe"])
            else:
                s.wait("tensor", m_ag0)
                s.wait("tensor", m_u)
                m_ob = [None, None]
                for f in range(NKC):
                    slot = "bc0" if f % 2 == 0 else "bc1"
                    pf = self.pslot(slot)
                    s.puse(slot)
                    mm = None
                    for k in range(NKC):
                        rhs = ag0[:, k * CH:(k + 1) * CH] if k < 4 else \
                            attc[:, (k - 4) * CH:(k - 3) * CH]
                        mm = s.run("tensor", lambda e, f=f, k=k, pf=pf, rhs=rhs: e.matmul(
                            pf, wo_b[:, k * D + f * 128:k * D + (f + 1) * 128], rhs,
                            start=(k == 0), stop=(k == NKC - 1)),
                            inc="pe" if k == NKC - 1 else None)
                    s.wait("scalar", mm)
                    m_ob[f % 2] = s.run("scalar", lambda e, f=f, pf=pf: e.activation(
                        obt[f], pf, AF.Identity, bias=self.bo_sb[:, f:f + 1]), inc="act")
                    s.pread(slot, m_ob[f % 2])
                m_ag0_cons = ("pe", s.cnt["pe"])
                m_qc_cons[par] = ("pe", s.cnt["pe"])
                m_xln_cons[par] = ("pe", s.cnt["pe"])
                m_obl = m_ob[(NKC - 1) % 2]
                m_stats = self.ln_stats(obt, sqs, bf=True, wait_src=m_obl,
                                        slots=("bc0", "bc1"))
                s.wait("vector", m_rows_cons)
                s.wait("scalar", m_rows_cons)
                mB, _ = self.ln_rows(m_stats)
                s.wait("gpsimd", m_xc)
                s.wait("vector", m_xc)
                m_aff, m_res = self.affine(obt, None, tmp, mB, resid=xct, pool_out=tmp2,
                                           slots=("bc0", "bc1"))
                m_rows_cons = ("pe", s.cnt["pe"])
                s.wait("scalar", m_res)
                m_xc_cons = s.dma("scalar", "dqS",
                                  self.out1b[:, i * 8 * CH:(i + 1) * 8 * CH], xc)
                m_attc_cons[par] = ("pe", s.cnt["pe"])
        if g == 0:
            self.m_B1_pe = ("pe", s.cnt["pe"])
            self.m_B_rows_cons = m_rows_cons
            self.m_B1_store = s.mark("dqS")
            self.m_B1_dve = ("dve", s.cnt["dve"])
        else:
            self.m_B2_store = s.mark("dqS")
            self.m_B2_pe = ("pe", s.cnt["pe"])
            self.m_B2_dve = ("dve", s.cnt["dve"])
            self.m_C_rows_init = m_rows_cons

    def emit_phase_c(self):
        s = self.s
        o = 0
        lw1_b = self.ab(o, NKC * 512); o += NKC * 512
        lw2_b = self.ab(o, 4 * D); o += 4 * D
        xrc = self.ab(o, NKC * CH); o += NKC * CH
        xrt = [xrc[:, k * CH:(k + 1) * CH] for k in range(NKC)]
        lgh = self.ab(o, 4 * CH); o += 4 * CH
        assert o <= HW1_O
        hw1_b = self.ab(HW1_O, NKC * DHH)
        out1c = self.af(0, NKC * CH)
        o1t = [out1c[:, k * CH:(k + 1) * CH] for k in range(NKC)]
        tmp = self.af(NKC * CH, CH)
        lo = [self.af(NKC * CH + CH, CH), self.af(NKC * CH + 2 * CH, CH)]
        sqs = [self.af(NKC * CH + 3 * CH, CH), self.af(NKC * CH + 4 * CH, CH)]

        s.wait("gpsimd", self.m_B2_pe)
        mw = None
        for k in range(NKC):
            mw = s.dma("gpsimd", "dqG", lw1_b[:, k * 512:(k + 1) * 512],
                       self.lw1[k * 128:(k + 1) * 128, :])
        for k in range(4):
            mw = s.dma("gpsimd", "dqG", lw2_b[:, k * D:(k + 1) * D],
                       self.lw2[k * 128:(k + 1) * 128, :])
        s.wait("vector", mw)
        m_fold = None
        for k in range(NKC):
            m_fold = s.run("vector", lambda e, k=k: e.tensor_scalar(
                lw1_b[:, k * 512:(k + 1) * 512], lw1_b[:, k * 512:(k + 1) * 512],
                self.lgam_sb[:, k:k + 1], None, ALU.mult), inc="dve")
        m_hw1f = None
        for k in range(NKC):
            mwh = s.dma("gpsimd", "dqG", hw1_b[:, k * DHH:(k + 1) * DHH],
                        self.hw1[k * 128:(k + 1) * 128, :])
            s.wait("vector", mwh)
            m_hw1f = s.run("vector", lambda e, k=k: e.tensor_scalar(
                hw1_b[:, k * DHH:(k + 1) * DHH], hw1_b[:, k * DHH:(k + 1) * DHH],
                self.hgam_sb[:, k:k + 1], None, ALU.mult), inc="dve")
        self.m_hw1 = m_hw1f

        m_out1_cons = None
        m_rows_cons = self.m_C_rows_init
        m_xr_cons = None
        m_xr_st = None
        m_lo_cons = [None, None]
        m_lgh_cons = None
        m_srow_st = None
        for i in range(NCH):
            s.wait("sync", m_out1_cons)
            s.wait("sync", self.m_B2_store)
            mo1 = None
            for k in range(NKC):
                mo1 = s.dma("sync", "dqL", o1t[k],
                            self.out1b[:, i * 8 * CH + k * CH:i * 8 * CH + (k + 1) * CH])
            s.wait("tensor", mo1)
            s.wait("scalar", mo1)
            s.wait("vector", m_srow_st)
            m_stats = self.ln_stats(o1t, sqs)
            s.puse("q0")
            mm = None
            for k in range(NKC):
                mm = s.run("tensor", lambda e, k=k: e.matmul(
                    self.pqk[0:1, 0:CH], self.rt_sb[:, k:k + 1], o1t[k],
                    start=(k == 0), stop=(k == NKC - 1)),
                    inc="pe" if k == NKC - 1 else None)
            s.wait("vector", m_rows_cons)
            s.wait("scalar", m_rows_cons)
            mB, m_rms = self.ln_rows(m_stats, rms=True)
            s.wait("vector", mm)
            s.wait("vector", mB)
            s.wait("vector", self.m_R)
            s.wait("vector", m_srow_st)
            s.run("vector", lambda e: e.tensor_mul(
                self.rseg(0, 0), self.pqk[0:1, 0:CH], self.rseg(0, 2048)))
            s.run("vector", lambda e: e.tensor_scalar(
                self.rseg(0, 512), self.rseg(0, 2560), self.R_sc[0:1, 0:1], None,
                ALU.mult))
            m_srow = self.runT("vector", lambda e: e.tensor_add(
                self.rseg(0, 0), self.rseg(0, 0), self.rseg(0, 512)), self.rseg(0, 0), "dve")
            s.pread("q0", m_srow)
            s.wait("scalar", m_srow)
            m_srow_st = s.dma("scalar", "dqS", self.srowb[0:1, i * CH:(i + 1) * CH],
                              self.rseg(0, 0))
            s.wait("vector", m_rms)
            s.run("vector", lambda e: e.tensor_mul(self.rseg(0, 2048), self.rseg(0, 2048),
                                                   self.rseg(0, 1024)))
            mBp = self.runT("vector", lambda e: e.tensor_mul(
                self.rseg(0, 2560), self.rseg(0, 2560), self.rseg(0, 1024)),
                self.rseg(0, 2560), "dve")
            s.wait("vector", m_xr_cons)
            s.wait("vector", m_xr_st)
            m_aff, _ = self.affine(o1t, xrt, tmp, mBp)
            m_rows_cons = ("pe", s.cnt["pe"])
            m_out1_cons = m_aff
            s.wait("scalar", m_aff)
            m_xr_st = s.dma("scalar", "dqS", self.xrb[:, i * 8 * CH:(i + 1) * 8 * CH], xrc)
            s.wait("tensor", m_fold)
            s.wait("tensor", m_aff)
            m_g = None
            for m in range(4):
                slot = "q0" if m % 2 == 0 else "q1"
                ph = self.pslot(slot)
                s.puse(slot)
                mm = None
                for k in range(NKC):
                    mm = s.run("tensor", lambda e, m=m, k=k, ph=ph: e.matmul(
                        ph, lw1_b[:, k * 512 + m * 128:k * 512 + (m + 1) * 128], xrt[k],
                        start=(k == 0), stop=(k == NKC - 1)),
                        inc="pe" if k == NKC - 1 else None)
                s.wait("scalar", mm)
                s.wait("scalar", m_lgh_cons)
                m_g = s.run("scalar", lambda e, m=m, ph=ph: e.activation(
                    lgh[:, m * CH:(m + 1) * CH], ph, AF.Gelu,
                    bias=self.lb1_sb[:, m:m + 1]), inc="act")
                s.pread(slot, m_g)
            s.wait("tensor", m_g)
            for f in range(NKC):
                slot = "av0" if f % 2 == 0 else "av1"
                pf = self.pslot(slot)
                s.puse(slot)
                mm = None
                for kk in range(4):
                    mm = s.run("tensor", lambda e, f=f, kk=kk, pf=pf: e.matmul(
                        pf, lw2_b[:, kk * D + f * 128:kk * D + (f + 1) * 128],
                        lgh[:, kk * CH:(kk + 1) * CH],
                        start=(kk == 0), stop=(kk == 3)),
                        inc="pe" if kk == 3 else None)
                lof = lo[f % 2]
                s.wait("vector", mm)
                s.wait("vector", m_lo_cons[f % 2])
                mev = s.run("vector", lambda e, f=f, pf=pf, lof=lof: e.tensor_scalar(
                    lof, pf, self.lb2_sb[:, f:f + 1], None, ALU.add), inc="dve")
                s.pread(slot, mev)
                s.wait("gpsimd", mev)
                m_lo_cons[f % 2] = s.dma(
                    "gpsimd", "dqH",
                    self.lightb[:, i * 8 * CH + f * CH:i * 8 * CH + (f + 1) * CH], lof)
            m_lgh_cons = ("pe", s.cnt["pe"])
            m_xr_cons = ("pe", s.cnt["pe"])
        self.m_C_store = s.mark("dqS")
        self.m_C_rows_cons = m_rows_cons
        self.m_C_pe = ("pe", s.cnt["pe"])
        self.m_C_dve = ("dve", s.cnt["dve"])

    def emit_phase_d(self):
        s = self.s
        s.wait("sync", self.m_C_store)
        m_s = s.dma("sync", "dqL", self.r_s[:, :],
                    self.srowb.rearrange("one (p f) -> (one p) f", p=64))
        s.wait("vector", m_s)
        m_a = self.runT("vector", lambda e: e.memset(self.r_acol[:, :], A1),
                        self.r_acol, "dve")
        for it in range(2, N_ITERS + 1):
            s.wait("scalar", m_a)
            m_r = self.runT("scalar", lambda e: e.activation(
                self.r_t[:, :], self.r_s[:, :], AF.Relu, bias=self.r_acol[:, :]),
                self.r_t, "act")
            s.wait("vector", m_r)
            m_t = self.runT("vector", lambda e: e.tensor_sub(
                self.r_t[:, :], self.r_s[:, :], self.r_t[:, :]), self.r_t, "dve")
            s.wait("scalar", m_t)
            m_e = self.runT("scalar", lambda e: e.activation(
                self.r_e[:, :], self.r_t[:, :], AF.Exp), self.r_e, "act")
            s.wait("vector", m_e)
            m_rs = self.runT("vector", lambda e: e.reduce_sum(
                self.r_col[:, :], self.r_e[:, :], axis=AX.X), self.r_col, "dve")
            s.wait("tensor", m_rs)
            s.puse("av0")
            m_cs = s.run("tensor", lambda e: e.matmul(
                self.pav0[0:1, 0:1], self.ones_col[0:64, :], self.r_col[:, :],
                start=True, stop=True), inc="pe")
            s.wait("scalar", m_cs)
            m_ln = self.runT("scalar", lambda e: e.activation(
                self.r_sc1[:, :], self.pav0[0:1, 0:1], AF.Ln), self.r_sc1, "act")
            s.pread("av0", m_ln)
            s.wait("vector", m_ln)
            m_sc = self.runT("vector", lambda e: e.tensor_scalar(
                self.r_sc1[:, :], self.r_sc1[:, :], -1.0, LOGK, ALU.mult, ALU.add),
                self.r_sc1, "dve")
            s.wait("tensor", m_sc)
            s.puse("bc0")
            m_bc = s.run("tensor", lambda e: e.matmul(
                self.pbc0[0:64, 0:1], self.ones_row[0:1, 0:64], self.r_sc1[:, :],
                start=True, stop=True), inc="pe")
            s.wait("vector", m_bc)
            m_a = self.runT("vector", lambda e: e.tensor_copy(
                self.r_acol[:, :], self.pbc0[0:64, 0:1]), self.r_acol, "dve")
            s.pread("bc0", m_a)
        s.wait("vector", m_a)
        m_uu = self.runT("vector", lambda e: e.tensor_scalar(
            self.r_t[:, :], self.r_s[:, :], self.r_acol[:, :], None, ALU.add),
            self.r_t, "dve")
        s.wait("scalar", m_uu)
        m_sg = self.runT("scalar", lambda e: e.activation(
            self.r_m[:, :], self.r_t[:, :], AF.Sign), self.r_m, "act")
        s.wait("vector", m_sg)
        s.run("vector", lambda e: e.tensor_relu(self.r_m[:, :], self.r_m[:, :]))
        s.run("vector", lambda e: e.tensor_copy(self.r_c1[:, :], self.r_m[:, :]))
        src, dst = self.r_c1, self.r_c2
        for sh in (1, 2, 4, 8, 16, 32):
            s.run("vector", lambda e, src=src, dst=dst: e.tensor_copy(dst[:, :], src[:, :]))
            s.run("vector", lambda e, src=src, dst=dst, sh=sh: e.tensor_add(
                dst[:, sh:64], src[:, sh:64], src[:, 0:64 - sh]))
            src, dst = dst, src
        cum = src
        m_ct = self.runT("vector", lambda e: e.reduce_sum(
            self.r_col[:, :], self.r_m[:, :], axis=AX.X), self.r_col, "dve")
        s.wait("gpsimd", m_ct)
        mb0 = s.dma("gpsimd", "dqG", self.offsd[0:1, 0:64],
                    self.r_col[:, :].rearrange("p one -> (p one)")[None, :]
                    if False else self.r_col[:, :])
        s.wait("gpsimd", mb0)
        m_tr = s.dma("gpsimd", "dqG", self.r_row2[0:1, 0:64], self.offsd[0:1, 0:64])
        s.wait("vector", m_tr)
        s.run("vector", lambda e: e.memset(self.r_row1[:, :], 0.0))
        m_ex = s.run("vector", lambda e: e.tensor_copy(self.r_row1[0:1, 1:64],
                                                       self.r_row2[0:1, 0:63]), inc="dve")
        rsrc, rdst = self.r_row1, self.r_row2
        for sh in (1, 2, 4, 8, 16, 32):
            s.run("vector", lambda e, rsrc=rsrc, rdst=rdst: e.tensor_copy(rdst[:, :],
                                                                          rsrc[:, :]))
            s.run("vector", lambda e, rsrc=rsrc, rdst=rdst, sh=sh: e.tensor_add(
                rdst[0:1, sh:64], rsrc[0:1, sh:64], rsrc[0:1, 0:64 - sh]))
            rsrc, rdst = rdst, rsrc
        m_off = self.runT("vector", lambda e, rsrc=rsrc: e.tensor_copy(
            self.tch_v[0:1, 0:1], rsrc[0:1, 0:1]), rsrc, "dve")
        s.wait("gpsimd", m_off)
        mb1 = s.dma("gpsimd", "dqG", self.offsd[0:1, 0:64], rsrc[0:1, 0:64])
        s.wait("gpsimd", mb1)
        m_tr2 = s.dma("gpsimd", "dqG", self.r_col[:, :],
                      self.offsd.rearrange("one (p f) -> (one p) f", p=64))
        s.wait("vector", m_tr2)
        m_cg = s.run("vector", lambda e, cum=cum: e.tensor_scalar(
            cum[:, :], cum[:, :], self.r_col[:, :], None, ALU.add), inc="dve")

        m_t2 = self.runT("vector", lambda e, cum=cum: e.tensor_scalar(
            self.r_t[:, :], cum[:, :], -1.0, 1024.5, ALU.mult, ALU.add), self.r_t, "dve")
        s.wait("scalar", m_t2)
        m_sg2 = self.runT("scalar", lambda e: e.activation(
            self.r_t[:, :], self.r_t[:, :], AF.Sign), self.r_t, "act")
        s.wait("vector", m_sg2)
        s.run("vector", lambda e: e.tensor_relu(self.r_t[:, :], self.r_t[:, :]))
        m_sel = self.runT("vector", lambda e: e.tensor_mul(
            self.r_m[:, :], self.r_m[:, :], self.r_t[:, :]), self.r_m, "dve")
        s.wait("gpsimd", m_sel)
        self.m_sel_st = s.dma("gpsimd", "dqG",
                              self.selbd.rearrange("one (p f) -> (one p) f", p=64),
                              self.r_m[:, :])

    def hw2r(self, k):
        off = 5120 + k * D if k < 27 else 65536 + (k - 27) * D
        return self.ab(off, D)

    def emit_phase_e1(self):
        s = self.s
        hw1_b = self.ab(HW1_O, NKC * DHH)
        xrc = self.ab(0, NKC * CH)
        hht = [self.ab(NKC * CH, CH), self.ab(NKC * CH + CH, CH)]
        m_xr_cons = self.m_C_pe
        m_hh_st = [None, None]
        s.wait("gpsimd", self.m_C_pe)
        s.wait("gpsimd", self.m_C_dve)
        s.wait("gpsimd", self.m_C_store)
        for k in range(32):
            s.dma("gpsimd", "dqH", self.hw2r(k), self.hw2[k * 128:(k + 1) * 128, :])
        self.m_hw2r = s.mark("dqH")
        s.wait("tensor", self.m_hw1)
        s.wait("tensor", self.m_C_pe)
        s.wait("tensor", self.m_C_dve)
        for i in range(NCH):
            s.wait("sync", m_xr_cons)
            s.wait("sync", self.m_C_store)
            m_xr = s.dma("sync", "dqL", xrc, self.xrb[:, i * 8 * CH:(i + 1) * 8 * CH])
            s.wait("tensor", m_xr)
            for m in range(32):
                slot = "av0" if m % 2 == 0 else "av1"
                ph = self.pslot(slot)
                s.puse(slot)
                mm = None
                for k in range(NKC):
                    mm = s.run("tensor", lambda e, m=m, k=k, ph=ph: e.matmul(
                        ph, hw1_b[:, k * DHH + m * 128:k * DHH + (m + 1) * 128],
                        xrc[:, k * CH:(k + 1) * CH],
                        start=(k == 0), stop=(k == NKC - 1)),
                        inc="pe" if k == NKC - 1 else None)
                ht = hht[m % 2]
                s.wait("scalar", mm)
                s.wait("scalar", m_hh_st[m % 2])
                mg = s.run("scalar", lambda e, m=m, ph=ph, ht=ht: e.activation(
                    ht, ph, AF.Gelu, bias=self.hb1_sb[:, m:m + 1]), inc="act")
                s.pread(slot, mg)
                s.wait("gpsimd", mg)
                m_hh_st[m % 2] = s.dma(
                    "gpsimd", "dqG",
                    self.hhb[:, (i * 32 + m) * CH:(i * 32 + m + 1) * CH], ht)
            m_xr_cons = ("pe", s.cnt["pe"])
        self.m_E1_store = s.mark("dqG")
        self.m_E1_pe = ("pe", s.cnt["pe"])
        self.m_E1_act = ("act", s.cnt["act"])

    def emit_phase_e2(self):
        s = self.s
        hhk = [self.ab(HW1_O + b * CH, CH) for b in range(8)]
        ffnc = self.ab(HW1_O + 8 * CH, NKC * CH)
        fft = [ffnc[:, f * CH:(f + 1) * CH] for f in range(NKC)]
        hv = self.ab(HW1_O + 8 * CH + NKC * CH, NKC * CH)
        out1c = self.af(0, NKC * CH)
        o1t = [out1c[:, k * CH:(k + 1) * CH] for k in range(NKC)]
        tmp = self.af(NKC * CH, CH)
        tmp2 = self.af(NKC * CH + CH, CH)
        lof = [self.af(NKC * CH + 2 * CH, CH), self.af(NKC * CH + 3 * CH, CH)]
        sqs = [self.af(NKC * CH + 4 * CH, CH), self.af(NKC * CH + 5 * CH, CH)]
        SLOT8 = ["q0", "q1", "q2", "q3", "av0", "av1", "bc0", "bc1"]

        m_rows_cons = self.m_C_rows_cons
        m_out1_cons = None
        m_hh_cons = [self.m_E1_store] * 8
        m_lo_cons = [None, None]
        m_y_st = None
        m_sel_ld = None
        m_sel_cons = None
        s.wait("sync", self.m_C_pe)
        s.wait("sync", self.m_C_dve)
        s.wait("sync", self.m_C_store)
        s.wait("sync", self.m_E1_pe)
        s.wait("vector", self.m_E1_pe)
        s.wait("vector", self.m_E1_store)
        for i in range(NCH):
            s.wait("sync", m_out1_cons)
            mo1 = None
            for k in range(NKC):
                mo1 = s.dma("sync", "dqL", o1t[k],
                            self.out1b[:, i * 8 * CH + k * CH:i * 8 * CH + (k + 1) * CH])
            s.wait("sync", self.m_sel_st)
            s.wait("sync", m_sel_cons)
            m_sel_ld = s.dma("sync", "dqL", self.dent[0:1, 0:512],
                             self.selbd[0:1, i * CH:(i + 1) * CH])
            def issue_group(gg):
                eng, q = ("sync", "dqL") if gg % 2 == 0 else ("gpsimd", "dqG")
                for k in range(4 * gg, 4 * gg + 4):
                    s.wait(eng, m_hh_cons[k % 8])
                return s.dma(eng, q, self.ab(HW1_O + (gg % 2) * 4 * CH, 4 * CH),
                             self.hhb[:, (i * 32 + 4 * gg) * CH:
                                      (i * 32 + 4 * gg + 4) * CH])

            m_grps = [issue_group(0), issue_group(1)]
            for g in range(8):
                s.wait("tensor", m_grps[g])
                if i == 0 and g == 0:
                    s.wait("tensor", self.m_E1_store)
                    s.wait("tensor", self.m_hw2r)
                for k in range(4 * g, 4 * g + 4):
                    w2k = self.hw2r(k)
                    mm = None
                    for f in range(NKC):
                        if k == 0:
                            s.puse(SLOT8[f])
                        pf = self.pslot(SLOT8[f])
                        mm = s.run("tensor", lambda e, k=k, f=f, pf=pf, w2k=w2k: e.matmul(
                            pf, w2k[:, f * 128:(f + 1) * 128],
                            hhk[k % 8], start=(k == 0), stop=(k == 31)),
                            inc="pe" if f == NKC - 1 else None)
                    m_hh_cons[k % 8] = mm
                if g + 2 < 8:
                    m_grps.append(issue_group(g + 2))
            s.wait("vector", mm)
            m_hv = None
            for f in range(NKC):
                pf = self.pslot(SLOT8[f])
                m_hv = s.run("vector", lambda e, f=f, pf=pf: e.tensor_scalar(
                    hv[:, f * CH:(f + 1) * CH], pf, self.hb2_sb[:, f:f + 1], None, ALU.add),
                    inc="dve")
                s.pread(SLOT8[f], m_hv)
            s.wait("tensor", m_hv)
            s.wait("tensor", m_sel_ld)
            mb = self.bcast("bc0", self.dent[0:1, 0:512])
            s.wait("vector", mb)
            m_ffn = None
            for f in range(NKC):
                lf = lof[f % 2]
                s.wait("gpsimd", m_lo_cons[f % 2])
                ml = s.dma("gpsimd", "dqH", lf,
                           self.lightb[:, i * 8 * CH + f * CH:i * 8 * CH + (f + 1) * CH])
                s.run("vector", lambda e, f=f: e.tensor_mul(
                    tmp, hv[:, f * CH:(f + 1) * CH], self.pslot("bc0")[:, 0:CH]))
                s.wait("vector", ml)
                m_ffn = s.run("vector", lambda e, f=f, lf=lf: e.tensor_add(
                    fft[f], tmp, lf), inc="dve")
                m_lo_cons[f % 2] = m_ffn
            s.pread("bc0", m_ffn)
            m_sel_cons = ("pe", s.cnt["pe"])
            m_stats = self.ln_stats(fft, sqs, bf=True, wait_src=m_ffn)
            s.wait("vector", m_rows_cons)
            s.wait("scalar", m_rows_cons)
            mB, _ = self.ln_rows(m_stats)
            s.wait("gpsimd", mo1)
            s.wait("vector", mo1)
            s.wait("gpsimd", m_y_st)
            m_aff, m_res = self.affine(fft, None, tmp, mB, resid=o1t, pool_out=tmp2)
            m_rows_cons = ("pe", s.cnt["pe"])
            s.wait("scalar", m_res)
            m_y_st = s.dma("scalar", "dqS", self.yT[:, i * 8 * CH:(i + 1) * 8 * CH], out1c)
            m_out1_cons = m_y_st

    def finalize(self):
        s = self.s
        s.wait("sync", s.mark("dqS"))
        s.wait("sync", s.mark("dqG"))
        nc = self.nc
        with nc.Block() as block:
            @block.sync
            def _(e):
                for f in s.ops["sync"]:
                    f(e)

            @block.tensor
            def _(e):
                for f in s.ops["tensor"]:
                    f(e)

            @block.scalar
            def _(e):
                for f in s.ops["scalar"]:
                    f(e)

            @block.vector
            def _(e):
                for f in s.ops["vector"]:
                    f(e)

            @block.gpsimd
            def _(e):
                for f in s.ops["gpsimd"]:
                    f(e)
        return nc

    def build(self):
        order = ["A", "B1", "B2", "C", "D", "E1", "E2"]
        upto = order.index(self.upto)
        self.emit_preamble()
        self.emit_phase_a()
        self.m_A_store = self.s.mark("dqS")
        if upto >= 1:
            self.emit_phase_b(0)
        if upto >= 2:
            self.emit_phase_b(1)
        if upto >= 3:
            self.emit_phase_c()
        if upto >= 4:
            self.emit_phase_d()
        if upto >= 5:
            self.emit_phase_e1()
        if upto >= 6:
            self.emit_phase_e2()
        return self.finalize()


def make_consts():
    cm = np.zeros((128, 4 * CH), np.float32)
    for dj in range(4):
        r = dj * 128
        jj = np.arange(128)[:, None]
        ii = np.arange(CH)[None, :]
        cm[:, dj * CH:(dj + 1) * CH] = np.where(ii < jj + r, NEG, 0.0)
    ident = np.eye(128, dtype=np.float32)
    return cm.astype(BF), ident.astype(BF), ident


def make_in_maps(inputs):
    x = inputs["L0_attn_x"]
    cm, idb, idf = make_consts()

    def col(v):
        return np.ascontiguousarray(np.asarray(v, np.float32).reshape(-1, 1))
    shared = {
        "wq": np.ascontiguousarray(np.asarray(inputs["L0_attn_Wq"], np.float32)),
        "wk": np.ascontiguousarray(np.asarray(inputs["L0_attn_Wk"], np.float32)),
        "wv": np.ascontiguousarray(np.asarray(inputs["L0_attn_Wv"], np.float32)),
        "wo": np.ascontiguousarray(np.asarray(inputs["L0_attn_Wo"], np.float32)),
        "bo": col(inputs["L0_attn_bo"]),
        "rt": col(inputs["L0_ffn_routing_token"]),
        "lgam": col(inputs["L0_ffn_light_gamma"]),
        "lw1": np.ascontiguousarray(np.asarray(inputs["L0_ffn_light_w1"], np.float32)),
        "lb1": col(inputs["L0_ffn_light_b1"]),
        "lw2": np.ascontiguousarray(np.asarray(inputs["L0_ffn_light_w2"], np.float32)),
        "lb2": col(inputs["L0_ffn_light_b2"]),
        "hgam": col(inputs["L0_ffn_heavy_gamma"]),
        "hw1": np.ascontiguousarray(np.asarray(inputs["L0_ffn_heavy_w1"], np.float32)),
        "hb1": col(inputs["L0_ffn_heavy_b1"]),
        "hw2": np.ascontiguousarray(np.asarray(inputs["L0_ffn_heavy_w2"], np.float32)),
        "hb2": col(inputs["L0_ffn_heavy_b2"]),
        "cmaskd": cm, "idbd": idb, "idfd": idf,
    }
    in_maps = []
    for c in range(8):
        m = dict(shared)
        m["xT"] = np.ascontiguousarray(np.asarray(x[c], np.float32).T)
        in_maps.append(m)
    return in_maps


def assemble_y(res):
    ys = []
    for c in range(8):
        yT = np.asarray(res.results[c]["yT"], np.float32)
        arr = yT.reshape(128, NCH, NKC, CH)
        ys.append(np.transpose(arr, (1, 3, 2, 0)).reshape(S, D))
    return np.stack(ys)


def kernel(**inputs):
    kb = KB(taps=False)
    nc = kb.build()
    in_maps = make_in_maps(inputs)
    res = run_bass_kernel_spmd(nc, in_maps, core_ids=list(range(8)))
    return assemble_y(res)


# version 08
# speedup vs baseline: 1.2576x; 1.0548x over previous
import numpy as np
import ml_dtypes
import concourse.bass as bass
import concourse.mybir as mybir
from concourse.bass_utils import run_bass_kernel_spmd

F32 = mybir.dt.float32
BF16 = mybir.dt.bfloat16
AF = mybir.ActivationFunctionType
ALU = mybir.AluOpType
AX = mybir.AxisListType
BF = ml_dtypes.bfloat16

D = 1024
S = 4096
CH = 512
NCH = S // CH
GH = 8
HPG = GH // 2
DL = 512
DHH = 4096
NKC = D // 128
NJ = S // 128
EPS = 1.0e-5
NEG = -1.0e9
N_ITERS = 50
LOGK = float(np.log(np.float32(1152.0)))
A1 = float(np.log(np.float32(1152.0)) - np.log(np.float32(4096.0)))

ENGINES = ("sync", "tensor", "scalar", "vector", "gpsimd")

KBUF_O = 0
VBUF_O = 16384
AEX_O = 33024
HW1_O = 32768
ARB_N = 71680
ARF_N = 7680


def later(a, b):
    if a is None:
        return b
    if b is None:
        return a
    assert a[0] == b[0], (a, b)
    return a if a[1] >= b[1] else b


class Sched:
    def __init__(self, nc):
        self.nc = nc
        self.ops = {e: [] for e in ENGINES}
        self.sem = {}
        self.cnt = {}
        self.waited = {e: {} for e in ENGINES}
        for name in ("pe", "act", "dve", "pool", "dqL", "dqS", "dqG", "dqH"):
            self.sem[name] = nc.alloc_semaphore(name)
            self.cnt[name] = 0
        self.pread_m = {}

    OWN = {"tensor": "pe", "scalar": "act", "vector": "dve", "gpsimd": "pool"}

    def wait(self, eng, mst):
        if mst is None:
            return
        name, val = mst
        if self.OWN.get(eng) == name:
            return
        if val <= 0 or val <= self.waited[eng].get(name, 0):
            return
        self.waited[eng][name] = val
        h = self.sem[name]
        self.ops[eng].append(lambda e, h=h, v=val: e.wait_ge(h, v))

    def run(self, eng, fn, inc=None):
        if inc is None:
            self.ops[eng].append(fn)
            return None
        self.cnt[inc] += 1
        v = self.cnt[inc]
        h = self.sem[inc]

        def g(e, fn=fn, h=h):
            fn(e).then_inc(h, 1)
        self.ops[eng].append(g)
        return (inc, v)

    def dma(self, eng, qsem, out, in_, **kw):
        self.cnt[qsem] += 16
        v = self.cnt[qsem]
        h = self.sem[qsem]

        def g(e, out=out, in_=in_, h=h, kw=kw):
            e.dma_start(out=out, in_=in_, **kw).then_inc(h, 16)
        self.ops[eng].append(g)
        return (qsem, v)

    def mark(self, name):
        return (name, self.cnt[name])

    def puse(self, slot):
        for name, val in self.pread_m.get(slot, {}).items():
            self.wait("tensor", (name, val))

    def pread(self, slot, mst):
        if mst is None:
            return
        d = self.pread_m.setdefault(slot, {})
        name, val = mst
        d[name] = max(d.get(name, 0), val)


class KB:
    def __init__(self, taps=False, upto="E2"):
        nc = bass.Bass("TRN2", target_bir_lowering=False)
        self.nc = nc
        self.upto = upto
        self.s = Sched(nc)
        ktap = "ExternalOutput" if taps else "Internal"
        dt = nc.dram_tensor

        self.xT = dt("xT", [D, S], F32, kind="ExternalInput").ap()
        self.wq = dt("wq", [D, D], F32, kind="ExternalInput").ap()
        self.wk = dt("wk", [D, D], F32, kind="ExternalInput").ap()
        self.wv = dt("wv", [D, D], F32, kind="ExternalInput").ap()
        self.wo = dt("wo", [D, D], F32, kind="ExternalInput").ap()
        self.bo = dt("bo", [D, 1], F32, kind="ExternalInput").ap()
        self.rt = dt("rt", [D, 1], F32, kind="ExternalInput").ap()
        self.lgam = dt("lgam", [D, 1], F32, kind="ExternalInput").ap()
        self.lw1 = dt("lw1", [D, DL], F32, kind="ExternalInput").ap()
        self.lb1 = dt("lb1", [DL, 1], F32, kind="ExternalInput").ap()
        self.lw2 = dt("lw2", [DL, D], F32, kind="ExternalInput").ap()
        self.lb2 = dt("lb2", [D, 1], F32, kind="ExternalInput").ap()
        self.hgam = dt("hgam", [D, 1], F32, kind="ExternalInput").ap()
        self.hw1 = dt("hw1", [D, DHH], F32, kind="ExternalInput").ap()
        self.hb1 = dt("hb1", [DHH, 1], F32, kind="ExternalInput").ap()
        self.hw2 = dt("hw2", [DHH, D], F32, kind="ExternalInput").ap()
        self.hb2 = dt("hb2", [D, 1], F32, kind="ExternalInput").ap()
        self.cmaskd = dt("cmaskd", [128, 4 * CH], BF16, kind="ExternalInput").ap()
        self.idbd = dt("idbd", [128, 128], BF16, kind="ExternalInput").ap()
        self.idfd = dt("idfd", [128, 128], F32, kind="ExternalInput").ap()

        self.yT = dt("yT", [128, NCH * 8 * CH], F32, kind="ExternalOutput").ap()
        self.xlnb = dt("xlnb", [128, NCH * 8 * CH], BF16, kind=ktap).ap()
        self.k1b = dt("k1b", [128, HPG * S], BF16, kind=ktap).ap()
        self.v1b = dt("v1b", [128, NJ * GH * 65], BF16, kind=ktap).ap()
        self.attg0b = dt("attg0b", [128, NCH * HPG * CH], BF16, kind=ktap).ap()
        self.out1b = dt("out1b", [128, NCH * 8 * CH], F32, kind=ktap).ap()
        self.xrb = dt("xrb", [128, NCH * 8 * CH], BF16, kind=ktap).ap()
        self.lightb = dt("lightb", [128, NCH * 8 * CH], BF16, kind=ktap).ap()
        self.srowb = dt("srowb", [1, S], F32, kind=ktap).ap()
        self.dbgb = dt("dbgb", [4, S], F32, kind=ktap).ap()
        self.offsd = dt("offsd", [1, 64], F32, kind="Internal").ap()
        self.selbd = dt("selbd", [1, S], F32, kind=ktap).ap()
        self.hhb = dt("hhb", [128, NCH * 32 * CH], BF16, kind=ktap).ap()

        self.arb = nc.alloc_sbuf_tensor("arb", [128, ARB_N], BF16)
        self.arf = nc.alloc_sbuf_tensor("arf", [128, ARF_N], F32)

        self.pqk = nc.alloc_psum_tensor("pqk", [128, 4 * 512], F32)
        self.pav0 = nc.alloc_psum_tensor("pav0", [128, 512], F32)
        self.pav1 = nc.alloc_psum_tensor("pav1", [128, 512], F32)
        self.pbc0 = nc.alloc_psum_tensor("pbc0", [128, 512], F32)
        self.pbc1 = nc.alloc_psum_tensor("pbc1", [128, 512], F32)

        self.ones_col = nc.alloc_sbuf_tensor("ones_col", [128, 1], F32)
        self.ones_bcol = nc.alloc_sbuf_tensor("ones_bcol", [128, 1], BF16)
        self.ones_row = nc.alloc_sbuf_tensor("ones_row", [128, 128], F32)
        self.eps_ap = nc.alloc_sbuf_tensor("eps_ap", [1, 1], F32)
        self.bo_sb = nc.alloc_sbuf_tensor("bo_sb", [128, 8], F32)
        self.lb2_sb = nc.alloc_sbuf_tensor("lb2_sb", [128, 8], F32)
        self.hb2_sb = nc.alloc_sbuf_tensor("hb2_sb", [128, 8], F32)
        self.lgam_sb = nc.alloc_sbuf_tensor("lgam_sb", [128, 8], F32)
        self.hgam_sb = nc.alloc_sbuf_tensor("hgam_sb", [128, 8], F32)
        self.lb1_sb = nc.alloc_sbuf_tensor("lb1_sb", [128, 4], F32)
        self.hb1_sb = nc.alloc_sbuf_tensor("hb1_sb", [128, 32], F32)
        self.rt_sb = nc.alloc_sbuf_tensor("rt_sb", [128, 8], F32)
        self.R_sb = nc.alloc_sbuf_tensor("R_sb", [128, 1], F32)
        self.R_sc = nc.alloc_sbuf_tensor("R_sc", [1, 1], F32)
        self.cm_sb = nc.alloc_sbuf_tensor("cm_sb", [128, 4 * CH], BF16)
        self.idb_sb = nc.alloc_sbuf_tensor("idb_sb", [128, 128], BF16)
        self.idf_sb = nc.alloc_sbuf_tensor("idf_sb", [128, 128], F32)
        self.tch_v = nc.alloc_sbuf_tensor("tch_v", [128, 1], F32)
        self.tch_a = nc.alloc_sbuf_tensor("tch_a", [128, 1], F32)
        self.tch_p = nc.alloc_sbuf_tensor("tch_p", [128, 1], F32)
        self.rows = nc.alloc_sbuf_tensor("rows", [128, 3072], F32)
        self.dent = nc.alloc_sbuf_tensor("dent", [128, 1024], F32)

        self.r_s = nc.alloc_sbuf_tensor("r_s", [64, 64], F32)
        self.r_t = nc.alloc_sbuf_tensor("r_t", [64, 64], F32)
        self.r_e = nc.alloc_sbuf_tensor("r_e", [64, 64], F32)
        self.r_m = nc.alloc_sbuf_tensor("r_m", [64, 64], F32)
        self.r_c1 = nc.alloc_sbuf_tensor("r_c1", [64, 64], F32)
        self.r_c2 = nc.alloc_sbuf_tensor("r_c2", [64, 64], F32)
        self.r_col = nc.alloc_sbuf_tensor("r_col", [64, 1], F32)
        self.r_acol = nc.alloc_sbuf_tensor("r_acol", [64, 1], F32)
        self.r_row1 = nc.alloc_sbuf_tensor("r_row1", [1, 64], F32)
        self.r_row2 = nc.alloc_sbuf_tensor("r_row2", [1, 64], F32)
        self.r_sc1 = nc.alloc_sbuf_tensor("r_sc1", [1, 1], F32)

    def ab(self, off, n):
        assert off + n <= ARB_N, (off, n)
        return self.arb[:, off:off + n]

    def af(self, off, n):
        assert off + n <= ARF_N, (off, n)
        return self.arf[:, off:off + n]

    def rseg(self, lane, off, n=CH):
        return self.rows[lane:lane + 1, off:off + n]

    def pslot(self, name):
        if name.startswith("q"):
            i = int(name[1])
            return self.pqk[:, i * 512:(i + 1) * 512]
        return {"av0": self.pav0, "av1": self.pav1,
                "bc0": self.pbc0, "bc1": self.pbc1}[name][:, 0:512]

    def runT(self, eng, fn, out_ap, inc, base=0):
        s = self.s
        s.run(eng, fn)
        src = out_ap[0:1, 0:1]
        if eng == "vector":
            dst = self.tch_v[base:base + 1, 0:1]
            return s.run("vector", lambda e, src=src, dst=dst: e.tensor_copy(dst, src),
                         inc=inc)
        if eng == "scalar":
            dst = self.tch_a[base:base + 1, 0:1]
            return s.run("scalar", lambda e, src=src, dst=dst: e.activation(
                dst, src, AF.Identity), inc=inc)
        if eng == "gpsimd":
            dst = self.tch_p[base:base + 1, 0:1]
            return s.run("gpsimd", lambda e, src=src, dst=dst: e.tensor_copy(dst, src),
                         inc=inc)
        raise ValueError(eng)

    def bcast(self, slot, row_ap, nrows=128, base=0):
        s = self.s
        s.puse(slot)
        p = self.pslot(slot)[0:nrows, :row_ap.shape[-1]]
        return s.run("tensor", lambda e, p=p, r=row_ap, n=nrows, b=base: e.matmul(
            p, self.ones_row[b:b + 1, 0:n], r, start=True, stop=True), inc="pe")

    def emit_preamble(self):
        s = self.s
        s.run("vector", lambda e: e.memset(self.ones_col[:, :], 1.0))
        s.run("vector", lambda e: e.memset(self.ones_bcol[:, :], 1.0))
        s.run("vector", lambda e: e.memset(self.ones_row[:, :], 1.0))
        self.m_ones = self.runT("vector", lambda e: e.memset(self.eps_ap[:, :], EPS),
                                self.eps_ap, "dve")
        for dst, src in [
            (self.bo_sb, self.bo), (self.lb2_sb, self.lb2), (self.hb2_sb, self.hb2),
            (self.lgam_sb, self.lgam), (self.hgam_sb, self.hgam), (self.rt_sb, self.rt),
            (self.lb1_sb, self.lb1), (self.hb1_sb, self.hb1),
        ]:
            s.dma("sync", "dqL", dst[:, :], src.rearrange("(c p) one -> p (c one)", p=128),
                  allow_slow_non_contiguous=True)
        s.dma("sync", "dqL", self.cm_sb[:, :], self.cmaskd[:, :])
        s.dma("sync", "dqL", self.idb_sb[:, :], self.idbd[:, :])
        self.m_consts = s.dma("sync", "dqL", self.idf_sb[:, :], self.idfd[:, :])
        s.wait("tensor", self.m_consts)
        s.wait("tensor", self.m_ones)
        s.wait("vector", self.m_consts)
        mrow = self.runT("vector", lambda e: e.reduce_sum(self.R_sb[:, :], self.rt_sb[:, :],
                                                          axis=AX.X), self.R_sb, "dve")
        s.wait("tensor", mrow)
        m = s.run("tensor", lambda e: e.matmul(self.pbc0[0:1, 0:1], self.ones_col[:, :],
                                               self.R_sb[:, :], start=True, stop=True), inc="pe")
        s.wait("vector", m)
        self.m_R = s.run("vector", lambda e: e.tensor_copy(self.R_sc[:, :],
                                                           self.pbc0[0:1, 0:1]), inc="dve")
        s.pread("bc0", self.m_R)

    def ln_stats(self, tiles, sq_scr, bf=False, wait_src=None, slots=("av0", "av1")):
        s = self.s
        ones = self.ones_bcol if bf else self.ones_col
        sl0, sl1 = slots
        p0 = self.pslot(sl0)[0:1, 0:CH]
        p1 = self.pslot(sl1)[0:1, 0:CH]
        s.puse(sl0)
        s.puse(sl1)
        m_sum = m_sq = None
        m_sq_mm = [None, None]
        for k in range(NKC):
            t = tiles[k]
            ws = wait_src[k] if isinstance(wait_src, list) else wait_src
            if ws is not None:
                s.wait("tensor", ws)
                s.wait("scalar", ws)
            m_sum = s.run("tensor", lambda e, t=t, k=k, o=ones: e.matmul(
                p0, o[:, :], t, start=(k == 0), stop=(k == NKC - 1)),
                inc="pe" if k == NKC - 1 else None)
            scr = sq_scr[k % 2]
            s.wait("scalar", m_sq_mm[k % 2])
            ma = s.run("scalar", lambda e, t=t, scr=scr: e.activation(scr, t, AF.Square),
                       inc="act")
            s.wait("tensor", ma)
            m_sq = m_sq_mm[k % 2] = s.run("tensor", lambda e, scr=scr, k=k: e.matmul(
                p1, self.ones_col[:, :], scr,
                start=(k == 0), stop=(k == NKC - 1)), inc="pe")
        s.wait("vector", m_sum)
        s.wait("vector", m_sq)
        s.run("vector", lambda e: e.tensor_copy(self.rseg(0, 0), p0))
        m = s.run("vector", lambda e: e.tensor_copy(self.rseg(0, 512), p1),
                  inc="dve")
        s.pread(sl0, m)
        s.pread(sl1, m)
        return m

    def ln_rows(self, m_stats, rms=False):
        s = self.s
        SUMS = self.rseg(0, 0)
        SQ = self.rseg(0, 512)
        VAR = self.rseg(0, 1024)
        MEAN = self.rseg(0, 1536)
        Aseg = self.rseg(0, 2048)
        Bseg = self.rseg(0, 2560)
        s.wait("vector", m_stats)
        s.run("vector", lambda e: e.tensor_scalar(MEAN, SUMS, 1.0 / D, None, ALU.mult))
        s.run("vector", lambda e: e.tensor_scalar(VAR, SQ, 1.0 / D, None, ALU.mult))
        s.run("vector", lambda e: e.scalar_tensor_tensor(SUMS, MEAN, 1.0, MEAN, ALU.mult,
                                                         ALU.mult))
        m = self.runT("vector", lambda e: e.tensor_sub(VAR, VAR, SUMS), VAR, "dve")
        s.wait("scalar", m)
        s.wait("scalar", self.m_ones)
        s.run("scalar", lambda e: e.activation(Aseg, VAR, AF.Ln, bias=self.eps_ap[:, :]))
        m_rms = None
        if rms:
            ml2 = self.runT("scalar", lambda e: e.activation(VAR, VAR, AF.Ln), VAR, "act")
            s.wait("vector", ml2)
            s.run("vector", lambda e: e.tensor_sub(VAR, Aseg, VAR))
            mv = self.runT("vector", lambda e: e.tensor_scalar(VAR, VAR, 0.5, None, ALU.mult),
                           VAR, "dve")
            s.wait("scalar", mv)
            m_rms = self.runT("scalar", lambda e: e.activation(VAR, VAR, AF.Exp), VAR, "act")
        mA = self.runT("scalar", lambda e: e.activation(Aseg, Aseg, AF.Exp, scale=-0.5),
                       Aseg, "act")
        s.wait("vector", mA)
        mB = self.runT("vector", lambda e: e.scalar_tensor_tensor(
            Bseg, MEAN, -1.0, Aseg, ALU.mult, ALU.mult), Bseg, "dve")
        return mB, m_rms

    def affine(self, src_tiles, dst_tiles, tmp, m_ready, resid=None, pool_out=None,
               slots=("bc0", "bc1")):
        s = self.s
        m_aff = None
        m_res = None
        sa, sb = slots
        s.wait("tensor", m_ready)
        mA_ = self.bcast(sa, self.rseg(0, 2048))
        mB_ = self.bcast(sb, self.rseg(0, 2560))
        s.wait("vector", mA_)
        s.wait("vector", mB_)
        for k in range(NKC):
            s.run("vector", lambda e, k=k: e.tensor_mul(
                tmp, src_tiles[k], self.pslot(sa)[:, 0:CH]))
            if resid is None:
                m_aff = s.run("vector", lambda e, k=k: e.tensor_add(
                    dst_tiles[k], tmp, self.pslot(sb)), inc="dve")
            else:
                tmp2 = pool_out
                s.wait("vector", m_res)
                m_aff = s.run("vector", lambda e, k=k, tmp2=tmp2: e.tensor_add(
                    tmp2, tmp, self.pslot(sb)), inc="dve")
                s.wait("gpsimd", m_aff)
                m_res = s.run("gpsimd", lambda e, k=k, tmp2=tmp2: e.tensor_add(
                    resid[k], resid[k], tmp2), inc="pool")
        s.pread(sa, m_aff)
        s.pread(sb, m_aff)
        return m_aff, m_res

    def emit_phase_a(self):
        s = self.s
        o = AEX_O
        wk_b = self.ab(o, NKC * D); o += NKC * D
        wv_b = self.ab(o, NKC * D); o += NKC * D
        xlnw = self.ab(o, 2 * NKC * CH); o += 2 * NKC * CH
        ktmp = [self.ab(o, CH), self.ab(o + CH, CH)]; o += 2 * CH
        vtmp = [self.ab(o, GH * 65), self.ab(o + GH * 65 + 8, GH * 65)]
        o += 2 * (GH * 65 + 8)
        assert o <= ARB_N, o
        xc = self.af(0, NKC * CH)
        xct = [xc[:, k * CH:(k + 1) * CH] for k in range(NKC)]
        sqs = [self.af(NKC * CH, CH), self.af(NKC * CH + CH, CH)]
        tmp = self.af(NKC * CH + 2 * CH, CH)
        kbuf = self.ab(KBUF_O, HPG * S)
        vbuf = self.ab(VBUF_O, NJ * GH * 65)
        vview = vbuf.rearrange("p (t e) -> p t e", e=65)

        s.run("vector", lambda e: e.memset(vview[:, :, 64:65], 1.0))
        for b in range(2):
            vv = vtmp[b].rearrange("p (t e) -> p t e", e=65)[:, 0:GH, :]
            s.run("vector", lambda e, vv=vv: e.memset(vv[:, :, 64:65], 1.0))
        m_vones = self.runT("vector", lambda e: e.tensor_copy(
            self.tch_v[0:1, 0:1], vbuf[0:1, 64:65]), vbuf, "dve")

        for k in range(NKC):
            s.dma("gpsimd", "dqG", wk_b[:, k * D:(k + 1) * D], self.wk[k * 128:(k + 1) * 128, :])
            s.dma("gpsimd", "dqG", wv_b[:, k * D:(k + 1) * D], self.wv[k * 128:(k + 1) * 128, :])
        m_w = s.mark("dqG")

        m_xc_cons = None
        m_xln_cons = [None, None]
        m_xln_st = [None, None]
        m_rows_cons = None
        m_ktmp = [None, None]
        m_vtmp = [None, None]
        for i in range(NCH):
            par = i % 2
            xln = xlnw[:, par * NKC * CH:(par + 1) * NKC * CH]
            xlt = [xln[:, k * CH:(k + 1) * CH] for k in range(NKC)]
            s.wait("sync", m_xc_cons)
            mld = None
            for k in range(NKC):
                mld = s.dma("sync", "dqL", xct[k],
                            self.xT[k * 128:(k + 1) * 128, i * CH:(i + 1) * CH])
            s.wait("tensor", mld)
            s.wait("scalar", mld)
            s.wait("tensor", self.m_ones)
            m_stats = self.ln_stats(xct, sqs)
            s.wait("vector", m_rows_cons)
            s.wait("scalar", m_rows_cons)
            mB, _ = self.ln_rows(m_stats)
            s.wait("vector", m_xln_cons[par])
            s.wait("vector", m_xln_st[par])
            m_aff, _ = self.affine(xct, xlt, tmp, mB)
            m_rows_cons = ("pe", s.cnt["pe"])
            m_xc_cons = m_aff
            s.wait("scalar", m_aff)
            m_xln_st[par] = s.dma("scalar", "dqS",
                                  self.xlnb[:, i * 8 * CH:(i + 1) * 8 * CH], xln)

            s.wait("tensor", m_w)
            s.wait("tensor", m_aff)
            for oc in range(8):
                slot = "q0" if oc % 2 == 0 else "q1"
                po = self.pslot(slot)
                s.puse(slot)
                mm = None
                for k in range(NKC):
                    mm = s.run("tensor", lambda e, oc=oc, k=k, po=po, xlt=xlt: e.matmul(
                        po, wk_b[:, k * D + oc * 128:k * D + (oc + 1) * 128], xlt[k],
                        start=(k == 0), stop=(k == NKC - 1)),
                        inc="pe" if k == NKC - 1 else None)
                s.wait("vector", mm)
                if oc < HPG:
                    dst = kbuf[:, oc * S + i * CH:oc * S + (i + 1) * CH]
                    mev = s.run("vector", lambda e, dst=dst, po=po: e.tensor_copy(dst, po),
                                inc="dve")
                else:
                    kt = ktmp[oc % 2]
                    s.wait("vector", m_ktmp[oc % 2])
                    mev = s.run("vector", lambda e, kt=kt, po=po: e.tensor_copy(kt, po),
                                inc="dve")
                    s.wait("scalar", mev)
                    m_ktmp[oc % 2] = s.dma(
                        "scalar", "dqS",
                        self.k1b[:, (oc - HPG) * S + i * CH:(oc - HPG) * S + (i + 1) * CH], kt)
                s.pread(slot, mev)
            s.wait("vector", m_vones)
            for t4 in range(4):
                j = i * 4 + t4
                for half in range(2):
                    slot = "av0" if half == 0 else "av1"
                    pv = self.pslot(slot)
                    s.puse(slot)
                    mm = None
                    for k in range(NKC):
                        mm = s.run("tensor", lambda e, t4=t4, k=k, half=half, pv=pv, xlt=xlt:
                                   e.matmul(pv,
                                            xlt[k][:, t4 * 128:(t4 + 1) * 128],
                                            wv_b[:, k * D + half * 512:k * D + (half + 1) * 512],
                                            start=(k == 0), stop=(k == NKC - 1)),
                                   inc="pe" if k == NKC - 1 else None)
                    s.wait("vector", mm)
                    src = pv.rearrange("p (h d) -> p h d", d=64)
                    if half == 0:
                        dv = vview[:, j * GH:(j + 1) * GH, 0:64]
                        mev = s.run("vector", lambda e, dv=dv, src=src: e.tensor_copy(dv, src),
                                    inc="dve")
                    else:
                        vt = vtmp[t4 % 2]
                        s.wait("vector", m_vtmp[t4 % 2])
                        dvv = vt.rearrange("p (t e) -> p t e", e=65)[:, 0:GH, 0:64]
                        mev = s.run("vector", lambda e, dvv=dvv, src=src: e.tensor_copy(dvv, src),
                                    inc="dve")
                        s.wait("scalar", mev)
                        m_vtmp[t4 % 2] = s.dma(
                            "scalar", "dqS",
                            self.v1b[:, j * GH * 65:(j + 1) * GH * 65],
                            vt[:, 0:GH * 65])
                    s.pread(slot, mev)
            m_xln_cons[par] = ("pe", s.cnt["pe"])
        self.m_A_kv = ("dve", s.cnt["dve"])
        self.m_A_rows_cons = m_rows_cons
        self.m_A_pe = ("pe", s.cnt["pe"])

    def emit_phase_b(self, g):
        s = self.s
        kbuf = self.ab(KBUF_O, HPG * S)
        vbuf = self.ab(VBUF_O, NJ * GH * 65)
        vview = vbuf.rearrange("p (t e) -> p t e", e=65)
        o = AEX_O
        wqh = self.ab(o, NKC * 512); o += NKC * 512
        if g == 1:
            wo_b = self.ab(o, NKC * D); o += NKC * D
        xlnw = self.ab(o, 2 * NKC * CH); o += 2 * NKC * CH
        qcw = self.ab(o, 2 * HPG * CH); o += 2 * HPG * CH
        attcw = self.ab(o, 2 * HPG * CH); o += 2 * HPG * CH
        if g == 1:
            ag0 = self.ab(o, HPG * CH); o += HPG * CH
            obuf = self.ab(o, NKC * CH); o += NKC * CH
            obt = [obuf[:, k * CH:(k + 1) * CH] for k in range(NKC)]
        ebuf = self.ab(o, 3 * 1024); o += 3 * 1024
        assert o <= ARB_N, o
        xc = self.af(0, NKC * CH)
        xct = [xc[:, k * CH:(k + 1) * CH] for k in range(NKC)]
        tmp = self.af(NKC * CH, CH)
        tmp2 = self.af(NKC * CH + CH, CH)
        sqs = [self.af(NKC * CH + 2 * CH, CH), self.af(NKC * CH + 3 * CH, CH)]

        if g == 1:
            s.wait("sync", self.m_B1_pe)
            s.wait("sync", self.m_A_store)
            s.dma("sync", "dqL", kbuf, self.k1b[:, :])
            m_kv_ready = s.dma("sync", "dqL", vview[:, :, 0:64],
                               self.v1b.rearrange("p (t e) -> p t e", e=65)[:, :, 0:64])
        else:
            m_kv_ready = None
        m_prev = self.m_A_pe if g == 0 else self.m_B1_pe
        m_prev_st = self.m_A_store if g == 0 else self.m_B1_store
        m_prev_dve = self.m_A_kv if g == 0 else self.m_B1_dve
        for eng in ("gpsimd", "vector", "scalar", "sync"):
            s.wait(eng, m_prev)
            s.wait(eng, m_prev_st)
            s.wait(eng, m_prev_dve)
        for k in range(NKC):
            s.dma("gpsimd", "dqG", wqh[:, k * 512:(k + 1) * 512],
                  self.wq[k * 128:(k + 1) * 128, g * 512:(g + 1) * 512])
        if g == 1:
            for k in range(NKC):
                s.dma("gpsimd", "dqG", wo_b[:, k * D:(k + 1) * D],
                      self.wo[k * 128:(k + 1) * 128, :])
        m_w = s.mark("dqG")

        m_qc_cons = [None, None]
        m_xln_cons = [m_prev, m_prev]
        m_attc_cons = [None, None]
        m_ag0_cons = m_prev
        m_xc_cons = self.m_A_kv if g == 1 else None
        m_rows_cons = self.m_A_rows_cons if g == 0 else self.m_B_rows_cons
        m_u = None
        m_av_hist = [None, None, None]
        for i in range(NCH):
            jmax = 4 * (i + 1)
            par = i % 2
            xln = xlnw[:, par * NKC * CH:(par + 1) * NKC * CH]
            xlt = [xln[:, k * CH:(k + 1) * CH] for k in range(NKC)]
            qc = qcw[:, par * HPG * CH:(par + 1) * HPG * CH]
            attc = attcw[:, par * HPG * CH:(par + 1) * HPG * CH]
            s.wait("sync", m_xln_cons[par])
            s.wait("sync", self.m_A_store)
            m_xln = s.dma("sync", "dqL", xln, self.xlnb[:, i * 8 * CH:(i + 1) * 8 * CH])
            if g == 1:
                s.wait("sync", m_ag0_cons)
                s.wait("sync", self.m_B1_store)
                m_ag0 = s.dma("sync", "dqL", ag0,
                              self.attg0b[:, i * HPG * CH:(i + 1) * HPG * CH])
                s.wait("sync", m_xc_cons)
                m_xc = None
                for k in range(NKC):
                    m_xc = s.dma("sync", "dqL", xct[k],
                                 self.xT[k * 128:(k + 1) * 128, i * CH:(i + 1) * CH])
            s.wait("tensor", m_w)
            s.wait("tensor", m_xln)
            if g == 0:
                s.wait("tensor", self.m_A_kv)
            else:
                s.wait("tensor", m_kv_ready)
            m_q = None
            for hp in range(HPG):
                slot = "av0" if hp % 2 == 0 else "av1"
                pq = self.pslot(slot)
                s.puse(slot)
                mm = None
                for k in range(NKC):
                    mm = s.run("tensor", lambda e, hp=hp, k=k, pq=pq, xlt=xlt: e.matmul(
                        pq, wqh[:, k * 512 + hp * 128:k * 512 + (hp + 1) * 128], xlt[k],
                        start=(k == 0), stop=(k == NKC - 1)),
                        inc="pe" if k == NKC - 1 else None)
                s.wait("vector", mm)
                s.wait("vector", m_qc_cons[par])
                m_q = s.run("vector", lambda e, hp=hp, pq=pq, qc=qc: e.tensor_copy(
                    qc[:, hp * CH:(hp + 1) * CH], pq), inc="dve")
                s.pread(slot, m_q)
            s.wait("tensor", m_q)
            for hp in range(HPG):
                mqk_hist = {}

                def emit_qk(j, hp=hp, i=i, qc=qc):
                    sp = j % 2
                    sa, sb = f"q{2 * sp}", f"q{2 * sp + 1}"
                    pa, pb = self.pslot(sa), self.pslot(sb)
                    diag = j >= 4 * i
                    s.puse(sa)
                    s.puse(sb)
                    s.run("tensor", lambda e, pa=pa, j=j, diag=diag: e.matmul(
                        pa, kbuf[0:64, hp * S + j * 128:hp * S + (j + 1) * 128],
                        qc[0:64, hp * CH:(hp + 1) * CH],
                        start=True, stop=not diag, tile_position=(0, 0)), inc=None)
                    if diag:
                        dj = j - 4 * i
                        s.run("tensor", lambda e, pa=pa, dj=dj: e.matmul(
                            pa, self.idb_sb[:, :], self.cm_sb[:, dj * CH:(dj + 1) * CH],
                            start=False, stop=True), inc=None)
                    mqk = s.run("tensor", lambda e, pb=pb, j=j, diag=diag: e.matmul(
                        pb, kbuf[64:128, hp * S + j * 128:hp * S + (j + 1) * 128],
                        qc[64:128, hp * CH:(hp + 1) * CH],
                        start=True, stop=not diag, tile_position=(64, 0)),
                        inc=None if diag else "pe")
                    if diag:
                        dj = j - 4 * i
                        mqk = s.run("tensor", lambda e, pb=pb, dj=dj: e.matmul(
                            pb, self.idb_sb[:, :], self.cm_sb[:, dj * CH:(dj + 1) * CH],
                            start=False, stop=True), inc="pe")
                    mqk_hist[j] = mqk

                emit_qk(0)
                for j in range(jmax):
                    sp = j % 2
                    se = j % 3
                    sa, sb = f"q{2 * sp}", f"q{2 * sp + 1}"
                    if j + 1 < jmax:
                        emit_qk(j + 1)
                    s.wait("scalar", mqk_hist.pop(j))
                    s.wait("scalar", m_av_hist[se])
                    m_exp = s.run("scalar", lambda e, sp=sp, se=se: e.activation(
                        ebuf[:, se * 1024:(se + 1) * 1024],
                        self.pqk[:, (2 * sp) * 512:(2 * sp + 2) * 512],
                        AF.Exp, scale=0.125), inc="act")
                    s.pread(sa, m_exp)
                    s.pread(sb, m_exp)
                    s.wait("tensor", m_exp)
                    if j == 0:
                        s.puse("av0")
                        s.puse("av1")
                    s.run("tensor", lambda e, hp=hp, j=j, se=se, jmax=jmax: e.matmul(
                        self.pav0[0:65, :], vview[:, j * GH + 2 * hp, :],
                        ebuf[:, se * 1024:se * 1024 + 512],
                        start=(j == 0), stop=(j == jmax - 1)), inc=None)
                    m_av = s.run("tensor", lambda e, hp=hp, j=j, se=se, jmax=jmax: e.matmul(
                        self.pav1[0:65, :], vview[:, j * GH + 2 * hp + 1, :],
                        ebuf[:, se * 1024 + 512:(se + 1) * 1024],
                        start=(j == 0), stop=(j == jmax - 1)), inc="pe")
                    m_av_hist[se] = m_av
                s.wait("vector", m_av)
                s.wait("vector", m_attc_cons[par])
                s.run("vector", lambda e, hp=hp, attc=attc: e.tensor_copy(
                    attc[0:64, hp * CH:(hp + 1) * CH], self.pav0[0:64, :]))
                s.run("vector", lambda e, hp=hp, attc=attc: e.tensor_copy(
                    attc[64:128, hp * CH:(hp + 1) * CH], self.pav1[0:64, :]))
                s.run("vector", lambda e: e.tensor_copy(
                    self.dent[64:65, 0:512], self.pav0[64:65, :]))
                m_den = self.runT("vector", lambda e: e.tensor_copy(
                    self.dent[64:65, 512:1024], self.pav1[64:65, :]),
                    self.dent[64:65, :], "dve", base=64)
                s.pread("av0", m_den)
                s.pread("av1", m_den)
                s.wait("scalar", m_den)
                s.run("scalar", lambda e: e.activation(
                    self.dent[64:65, 0:1024], self.dent[64:65, 0:1024], AF.Ln))
                m_rc = self.runT("scalar", lambda e: e.activation(
                    self.dent[64:65, 0:1024], self.dent[64:65, 0:1024], AF.Exp, scale=-1.0),
                    self.dent[64:65, :], "act", base=64)
                s.wait("tensor", m_rc)
                for ab in range(2):
                    slot = "bc0" if ab == 0 else "bc1"
                    mb = self.bcast(slot, self.dent[64:65, ab * 512:(ab + 1) * 512],
                                    nrows=64, base=64)
                    s.wait("vector", mb)
                    sl = attc[ab * 64:(ab + 1) * 64, hp * CH:(hp + 1) * CH]
                    m_u = s.run("vector", lambda e, sl=sl, slot=slot: e.tensor_mul(
                        sl, sl, self.pslot(slot)[0:64, :]), inc="dve")
                    s.pread(slot, m_u)
            if g == 0:
                s.wait("scalar", m_u)
                m_attc_cons[par] = s.dma("scalar", "dqS",
                                         self.attg0b[:, i * HPG * CH:(i + 1) * HPG * CH],
                                         attc)
                m_qc_cons[par] = ("pe", s.cnt["pe"])
                m_xln_cons[par] = ("pe", s.cnt["pe"])
            else:
                s.wait("tensor", m_ag0)
                s.wait("tensor", m_u)
                m_ob = [None, None]
                m_obs = []
                for f in range(NKC):
                    slot = "bc0" if f % 2 == 0 else "bc1"
                    pf = self.pslot(slot)
                    s.puse(slot)
                    mm = None
                    for k in range(NKC):
                        rhs = ag0[:, k * CH:(k + 1) * CH] if k < 4 else \
                            attc[:, (k - 4) * CH:(k - 3) * CH]
                        mm = s.run("tensor", lambda e, f=f, k=k, pf=pf, rhs=rhs: e.matmul(
                            pf, wo_b[:, k * D + f * 128:k * D + (f + 1) * 128], rhs,
                            start=(k == 0), stop=(k == NKC - 1)),
                            inc="pe" if k == NKC - 1 else None)
                    s.wait("scalar", mm)
                    m_ob[f % 2] = s.run("scalar", lambda e, f=f, pf=pf: e.activation(
                        obt[f], pf, AF.Identity, bias=self.bo_sb[:, f:f + 1]), inc="act")
                    s.pread(slot, m_ob[f % 2])
                    m_obs.append(m_ob[f % 2])
                m_ag0_cons = ("pe", s.cnt["pe"])
                m_qc_cons[par] = ("pe", s.cnt["pe"])
                m_xln_cons[par] = ("pe", s.cnt["pe"])
                m_stats = self.ln_stats(obt, sqs, bf=True, wait_src=m_obs)
                s.wait("vector", m_rows_cons)
                s.wait("scalar", m_rows_cons)
                mB, _ = self.ln_rows(m_stats)
                s.wait("gpsimd", m_xc)
                s.wait("vector", m_xc)
                m_aff, m_res = self.affine(obt, None, tmp, mB, resid=xct, pool_out=tmp2,
                                           slots=("bc0", "bc1"))
                m_rows_cons = ("pe", s.cnt["pe"])
                s.wait("scalar", m_res)
                m_xc_cons = s.dma("scalar", "dqS",
                                  self.out1b[:, i * 8 * CH:(i + 1) * 8 * CH], xc)
                m_attc_cons[par] = ("pe", s.cnt["pe"])
        if g == 0:
            self.m_B1_pe = ("pe", s.cnt["pe"])
            self.m_B_rows_cons = m_rows_cons
            self.m_B1_store = s.mark("dqS")
            self.m_B1_dve = ("dve", s.cnt["dve"])
        else:
            self.m_B2_store = s.mark("dqS")
            self.m_B2_pe = ("pe", s.cnt["pe"])
            self.m_B2_dve = ("dve", s.cnt["dve"])
            self.m_C_rows_init = m_rows_cons

    def emit_phase_c(self):
        s = self.s
        o = 0
        lw1_b = self.ab(o, NKC * 512); o += NKC * 512
        lw2_b = self.ab(o, 4 * D); o += 4 * D
        xrc = self.ab(o, NKC * CH); o += NKC * CH
        xrt = [xrc[:, k * CH:(k + 1) * CH] for k in range(NKC)]
        lgh = self.ab(o, 4 * CH); o += 4 * CH
        assert o <= HW1_O
        hw1_b = self.ab(HW1_O, NKC * DHH)
        out1c = self.af(0, NKC * CH)
        o1t = [out1c[:, k * CH:(k + 1) * CH] for k in range(NKC)]
        tmp = self.af(NKC * CH, CH)
        lo = [self.af(NKC * CH + CH, CH), self.af(NKC * CH + 2 * CH, CH)]
        sqs = [self.af(NKC * CH + 3 * CH, CH), self.af(NKC * CH + 4 * CH, CH)]

        s.wait("gpsimd", self.m_B2_pe)
        mw = None
        for k in range(NKC):
            mw = s.dma("gpsimd", "dqG", lw1_b[:, k * 512:(k + 1) * 512],
                       self.lw1[k * 128:(k + 1) * 128, :])
        for k in range(4):
            mw = s.dma("gpsimd", "dqG", lw2_b[:, k * D:(k + 1) * D],
                       self.lw2[k * 128:(k + 1) * 128, :])
        s.wait("vector", mw)
        m_fold = None
        for k in range(NKC):
            m_fold = s.run("vector", lambda e, k=k: e.tensor_scalar(
                lw1_b[:, k * 512:(k + 1) * 512], lw1_b[:, k * 512:(k + 1) * 512],
                self.lgam_sb[:, k:k + 1], None, ALU.mult), inc="dve")
        m_hw1f = None
        for k in range(NKC):
            mwh = s.dma("gpsimd", "dqG", hw1_b[:, k * DHH:(k + 1) * DHH],
                        self.hw1[k * 128:(k + 1) * 128, :])
            s.wait("vector", mwh)
            m_hw1f = s.run("vector", lambda e, k=k: e.tensor_scalar(
                hw1_b[:, k * DHH:(k + 1) * DHH], hw1_b[:, k * DHH:(k + 1) * DHH],
                self.hgam_sb[:, k:k + 1], None, ALU.mult), inc="dve")
        self.m_hw1 = m_hw1f

        m_out1_cons = None
        m_rows_cons = self.m_C_rows_init
        m_xr_cons = None
        m_xr_st = None
        m_lo_cons = [None, None]
        m_lgh_cons = None
        m_srow_st = None
        for i in range(NCH):
            s.wait("sync", m_out1_cons)
            s.wait("sync", self.m_B2_store)
            mo1 = None
            for k in range(NKC):
                mo1 = s.dma("sync", "dqL", o1t[k],
                            self.out1b[:, i * 8 * CH + k * CH:i * 8 * CH + (k + 1) * CH])
            s.wait("tensor", mo1)
            s.wait("scalar", mo1)
            s.wait("vector", m_srow_st)
            m_stats = self.ln_stats(o1t, sqs)
            s.puse("q0")
            mm = None
            for k in range(NKC):
                mm = s.run("tensor", lambda e, k=k: e.matmul(
                    self.pqk[0:1, 0:CH], self.rt_sb[:, k:k + 1], o1t[k],
                    start=(k == 0), stop=(k == NKC - 1)),
                    inc="pe" if k == NKC - 1 else None)
            s.wait("vector", m_rows_cons)
            s.wait("scalar", m_rows_cons)
            mB, m_rms = self.ln_rows(m_stats, rms=True)
            s.wait("vector", mm)
            s.wait("vector", mB)
            s.wait("vector", self.m_R)
            s.wait("vector", m_srow_st)
            s.run("vector", lambda e: e.tensor_mul(
                self.rseg(0, 0), self.pqk[0:1, 0:CH], self.rseg(0, 2048)))
            s.run("vector", lambda e: e.tensor_scalar(
                self.rseg(0, 512), self.rseg(0, 2560), self.R_sc[0:1, 0:1], None,
                ALU.mult))
            m_srow = self.runT("vector", lambda e: e.tensor_add(
                self.rseg(0, 0), self.rseg(0, 0), self.rseg(0, 512)), self.rseg(0, 0), "dve")
            s.pread("q0", m_srow)
            s.wait("scalar", m_srow)
            m_srow_st = s.dma("scalar", "dqS", self.srowb[0:1, i * CH:(i + 1) * CH],
                              self.rseg(0, 0))
            s.wait("vector", m_rms)
            s.run("vector", lambda e: e.tensor_mul(self.rseg(0, 2048), self.rseg(0, 2048),
                                                   self.rseg(0, 1024)))
            mBp = self.runT("vector", lambda e: e.tensor_mul(
                self.rseg(0, 2560), self.rseg(0, 2560), self.rseg(0, 1024)),
                self.rseg(0, 2560), "dve")
            s.wait("vector", m_xr_cons)
            s.wait("vector", m_xr_st)
            m_aff, _ = self.affine(o1t, xrt, tmp, mBp)
            m_rows_cons = ("pe", s.cnt["pe"])
            m_out1_cons = m_aff
            s.wait("scalar", m_aff)
            m_xr_st = s.dma("scalar", "dqS", self.xrb[:, i * 8 * CH:(i + 1) * 8 * CH], xrc)
            s.wait("tensor", m_fold)
            s.wait("tensor", m_aff)
            m_g = None
            for m in range(4):
                slot = "q0" if m % 2 == 0 else "q1"
                ph = self.pslot(slot)
                s.puse(slot)
                mm = None
                for k in range(NKC):
                    mm = s.run("tensor", lambda e, m=m, k=k, ph=ph: e.matmul(
                        ph, lw1_b[:, k * 512 + m * 128:k * 512 + (m + 1) * 128], xrt[k],
                        start=(k == 0), stop=(k == NKC - 1)),
                        inc="pe" if k == NKC - 1 else None)
                s.wait("scalar", mm)
                s.wait("scalar", m_lgh_cons)
                m_g = s.run("scalar", lambda e, m=m, ph=ph: e.activation(
                    lgh[:, m * CH:(m + 1) * CH], ph, AF.Gelu,
                    bias=self.lb1_sb[:, m:m + 1]), inc="act")
                s.pread(slot, m_g)
            s.wait("tensor", m_g)
            for f in range(NKC):
                slot = "av0" if f % 2 == 0 else "av1"
                pf = self.pslot(slot)
                s.puse(slot)
                mm = None
                for kk in range(4):
                    mm = s.run("tensor", lambda e, f=f, kk=kk, pf=pf: e.matmul(
                        pf, lw2_b[:, kk * D + f * 128:kk * D + (f + 1) * 128],
                        lgh[:, kk * CH:(kk + 1) * CH],
                        start=(kk == 0), stop=(kk == 3)),
                        inc="pe" if kk == 3 else None)
                lof = lo[f % 2]
                s.wait("vector", mm)
                s.wait("vector", m_lo_cons[f % 2])
                mev = s.run("vector", lambda e, f=f, pf=pf, lof=lof: e.tensor_scalar(
                    lof, pf, self.lb2_sb[:, f:f + 1], None, ALU.add), inc="dve")
                s.pread(slot, mev)
                s.wait("gpsimd", mev)
                m_lo_cons[f % 2] = s.dma(
                    "gpsimd", "dqH",
                    self.lightb[:, i * 8 * CH + f * CH:i * 8 * CH + (f + 1) * CH], lof)
            m_lgh_cons = ("pe", s.cnt["pe"])
            m_xr_cons = ("pe", s.cnt["pe"])
        self.m_C_store = s.mark("dqS")
        self.m_C_rows_cons = m_rows_cons
        self.m_C_pe = ("pe", s.cnt["pe"])
        self.m_C_dve = ("dve", s.cnt["dve"])

    def emit_phase_d(self):
        s = self.s
        s.wait("sync", self.m_C_store)
        m_s = s.dma("sync", "dqL", self.r_s[:, :],
                    self.srowb.rearrange("one (p f) -> (one p) f", p=64))
        s.wait("vector", m_s)
        m_a = self.runT("vector", lambda e: e.memset(self.r_acol[:, :], A1),
                        self.r_acol, "dve")
        for it in range(2, N_ITERS + 1):
            s.wait("scalar", m_a)
            m_r = self.runT("scalar", lambda e: e.activation(
                self.r_t[:, :], self.r_s[:, :], AF.Relu, bias=self.r_acol[:, :]),
                self.r_t, "act")
            s.wait("vector", m_r)
            m_t = self.runT("vector", lambda e: e.tensor_sub(
                self.r_t[:, :], self.r_s[:, :], self.r_t[:, :]), self.r_t, "dve")
            s.wait("scalar", m_t)
            m_e = self.runT("scalar", lambda e: e.activation(
                self.r_e[:, :], self.r_t[:, :], AF.Exp), self.r_e, "act")
            s.wait("vector", m_e)
            m_rs = self.runT("vector", lambda e: e.reduce_sum(
                self.r_col[:, :], self.r_e[:, :], axis=AX.X), self.r_col, "dve")
            s.wait("tensor", m_rs)
            s.puse("av0")
            m_cs = s.run("tensor", lambda e: e.matmul(
                self.pav0[0:1, 0:1], self.ones_col[0:64, :], self.r_col[:, :],
                start=True, stop=True), inc="pe")
            s.wait("scalar", m_cs)
            m_ln = self.runT("scalar", lambda e: e.activation(
                self.r_sc1[:, :], self.pav0[0:1, 0:1], AF.Ln), self.r_sc1, "act")
            s.pread("av0", m_ln)
            s.wait("vector", m_ln)
            m_sc = self.runT("vector", lambda e: e.tensor_scalar(
                self.r_sc1[:, :], self.r_sc1[:, :], -1.0, LOGK, ALU.mult, ALU.add),
                self.r_sc1, "dve")
            s.wait("tensor", m_sc)
            s.puse("bc0")
            m_bc = s.run("tensor", lambda e: e.matmul(
                self.pbc0[0:64, 0:1], self.ones_row[0:1, 0:64], self.r_sc1[:, :],
                start=True, stop=True), inc="pe")
            s.wait("vector", m_bc)
            m_a = self.runT("vector", lambda e: e.tensor_copy(
                self.r_acol[:, :], self.pbc0[0:64, 0:1]), self.r_acol, "dve")
            s.pread("bc0", m_a)
        s.wait("vector", m_a)
        m_uu = self.runT("vector", lambda e: e.tensor_scalar(
            self.r_t[:, :], self.r_s[:, :], self.r_acol[:, :], None, ALU.add),
            self.r_t, "dve")
        s.wait("scalar", m_uu)
        m_sg = self.runT("scalar", lambda e: e.activation(
            self.r_m[:, :], self.r_t[:, :], AF.Sign), self.r_m, "act")
        s.wait("vector", m_sg)
        s.run("vector", lambda e: e.tensor_relu(self.r_m[:, :], self.r_m[:, :]))
        s.run("vector", lambda e: e.tensor_copy(self.r_c1[:, :], self.r_m[:, :]))
        src, dst = self.r_c1, self.r_c2
        for sh in (1, 2, 4, 8, 16, 32):
            s.run("vector", lambda e, src=src, dst=dst: e.tensor_copy(dst[:, :], src[:, :]))
            s.run("vector", lambda e, src=src, dst=dst, sh=sh: e.tensor_add(
                dst[:, sh:64], src[:, sh:64], src[:, 0:64 - sh]))
            src, dst = dst, src
        cum = src
        m_ct = self.runT("vector", lambda e: e.reduce_sum(
            self.r_col[:, :], self.r_m[:, :], axis=AX.X), self.r_col, "dve")
        s.wait("gpsimd", m_ct)
        mb0 = s.dma("gpsimd", "dqG", self.offsd[0:1, 0:64],
                    self.r_col[:, :].rearrange("p one -> (p one)")[None, :]
                    if False else self.r_col[:, :])
        s.wait("gpsimd", mb0)
        m_tr = s.dma("gpsimd", "dqG", self.r_row2[0:1, 0:64], self.offsd[0:1, 0:64])
        s.wait("vector", m_tr)
        s.run("vector", lambda e: e.memset(self.r_row1[:, :], 0.0))
        m_ex = s.run("vector", lambda e: e.tensor_copy(self.r_row1[0:1, 1:64],
                                                       self.r_row2[0:1, 0:63]), inc="dve")
        rsrc, rdst = self.r_row1, self.r_row2
        for sh in (1, 2, 4, 8, 16, 32):
            s.run("vector", lambda e, rsrc=rsrc, rdst=rdst: e.tensor_copy(rdst[:, :],
                                                                          rsrc[:, :]))
            s.run("vector", lambda e, rsrc=rsrc, rdst=rdst, sh=sh: e.tensor_add(
                rdst[0:1, sh:64], rsrc[0:1, sh:64], rsrc[0:1, 0:64 - sh]))
            rsrc, rdst = rdst, rsrc
        m_off = self.runT("vector", lambda e, rsrc=rsrc: e.tensor_copy(
            self.tch_v[0:1, 0:1], rsrc[0:1, 0:1]), rsrc, "dve")
        s.wait("gpsimd", m_off)
        mb1 = s.dma("gpsimd", "dqG", self.offsd[0:1, 0:64], rsrc[0:1, 0:64])
        s.wait("gpsimd", mb1)
        m_tr2 = s.dma("gpsimd", "dqG", self.r_col[:, :],
                      self.offsd.rearrange("one (p f) -> (one p) f", p=64))
        s.wait("vector", m_tr2)
        m_cg = s.run("vector", lambda e, cum=cum: e.tensor_scalar(
            cum[:, :], cum[:, :], self.r_col[:, :], None, ALU.add), inc="dve")

        m_t2 = self.runT("vector", lambda e, cum=cum: e.tensor_scalar(
            self.r_t[:, :], cum[:, :], -1.0, 1024.5, ALU.mult, ALU.add), self.r_t, "dve")
        s.wait("scalar", m_t2)
        m_sg2 = self.runT("scalar", lambda e: e.activation(
            self.r_t[:, :], self.r_t[:, :], AF.Sign), self.r_t, "act")
        s.wait("vector", m_sg2)
        s.run("vector", lambda e: e.tensor_relu(self.r_t[:, :], self.r_t[:, :]))
        m_sel = self.runT("vector", lambda e: e.tensor_mul(
            self.r_m[:, :], self.r_m[:, :], self.r_t[:, :]), self.r_m, "dve")
        s.wait("gpsimd", m_sel)
        self.m_sel_st = s.dma("gpsimd", "dqG",
                              self.selbd.rearrange("one (p f) -> (one p) f", p=64),
                              self.r_m[:, :])

    def hw2r(self, k):
        off = 5120 + k * D if k < 27 else 65536 + (k - 27) * D
        return self.ab(off, D)

    def emit_phase_e1(self):
        s = self.s
        hw1_b = self.ab(HW1_O, NKC * DHH)
        xrc = self.ab(0, NKC * CH)
        hht = [self.ab(NKC * CH, CH), self.ab(NKC * CH + CH, CH)]
        m_xr_cons = self.m_C_pe
        m_hh_st = [None, None]
        s.wait("gpsimd", self.m_C_pe)
        s.wait("gpsimd", self.m_C_dve)
        s.wait("gpsimd", self.m_C_store)
        for k in range(32):
            s.dma("gpsimd", "dqH", self.hw2r(k), self.hw2[k * 128:(k + 1) * 128, :])
        self.m_hw2r = s.mark("dqH")
        s.wait("tensor", self.m_hw1)
        s.wait("tensor", self.m_C_pe)
        s.wait("tensor", self.m_C_dve)
        for i in range(NCH):
            s.wait("sync", m_xr_cons)
            s.wait("sync", self.m_C_store)
            m_xr = s.dma("sync", "dqL", xrc, self.xrb[:, i * 8 * CH:(i + 1) * 8 * CH])
            s.wait("tensor", m_xr)
            for m in range(32):
                slot = "av0" if m % 2 == 0 else "av1"
                ph = self.pslot(slot)
                s.puse(slot)
                mm = None
                for k in range(NKC):
                    mm = s.run("tensor", lambda e, m=m, k=k, ph=ph: e.matmul(
                        ph, hw1_b[:, k * DHH + m * 128:k * DHH + (m + 1) * 128],
                        xrc[:, k * CH:(k + 1) * CH],
                        start=(k == 0), stop=(k == NKC - 1)),
                        inc="pe" if k == NKC - 1 else None)
                ht = hht[m % 2]
                s.wait("scalar", mm)
                s.wait("scalar", m_hh_st[m % 2])
                mg = s.run("scalar", lambda e, m=m, ph=ph, ht=ht: e.activation(
                    ht, ph, AF.Gelu, bias=self.hb1_sb[:, m:m + 1]), inc="act")
                s.pread(slot, mg)
                s.wait("gpsimd", mg)
                m_hh_st[m % 2] = s.dma(
                    "gpsimd", "dqG",
                    self.hhb[:, (i * 32 + m) * CH:(i * 32 + m + 1) * CH], ht)
            m_xr_cons = ("pe", s.cnt["pe"])
        self.m_E1_store = s.mark("dqG")
        self.m_E1_pe = ("pe", s.cnt["pe"])
        self.m_E1_act = ("act", s.cnt["act"])

    def emit_phase_e2(self):
        s = self.s
        hhk = [self.ab(HW1_O + b * CH, CH) for b in range(16)]
        ffnc = self.ab(HW1_O + 16 * CH, NKC * CH)
        fft = [ffnc[:, f * CH:(f + 1) * CH] for f in range(NKC)]
        hv = self.ab(HW1_O + 16 * CH + NKC * CH, NKC * CH)
        out1c = self.af(0, NKC * CH)
        o1t = [out1c[:, k * CH:(k + 1) * CH] for k in range(NKC)]
        tmp = self.af(NKC * CH, CH)
        tmp2 = self.af(NKC * CH + CH, CH)
        lof = [self.af(NKC * CH + 2 * CH, CH), self.af(NKC * CH + 3 * CH, CH)]
        sqs = [self.af(NKC * CH + 4 * CH, CH), self.af(NKC * CH + 5 * CH, CH)]
        SLOT8 = ["q0", "q1", "q2", "q3", "av0", "av1", "bc0", "bc1"]

        m_rows_cons = self.m_C_rows_cons
        m_out1_cons = None
        m_hh_cons = [self.m_E1_store] * 16
        m_lo_cons = [None, None]
        m_y_st = None
        m_sel_ld = None
        m_sel_cons = None
        s.wait("sync", self.m_C_pe)
        s.wait("sync", self.m_C_dve)
        s.wait("sync", self.m_C_store)
        s.wait("sync", self.m_E1_pe)
        s.wait("vector", self.m_E1_pe)
        s.wait("vector", self.m_E1_store)
        s.wait("scalar", self.m_C_pe)
        s.wait("scalar", self.m_C_dve)

        def issue_group(i, gg):
            eng, q = ("sync", "dqL") if gg % 2 == 0 else ("gpsimd", "dqG")
            for k in range(4 * gg, 4 * gg + 4):
                s.wait(eng, m_hh_cons[k % 16])
            return s.dma(eng, q, self.ab(HW1_O + (gg % 4) * 4 * CH, 4 * CH),
                         self.hhb[:, (i * 32 + 4 * gg) * CH:
                                  (i * 32 + 4 * gg + 4) * CH])

        for i in range(NCH):
            m_grps = [issue_group(i, g) for g in range(4)]
            s.wait("scalar", m_out1_cons)
            mo1 = None
            for k in range(NKC):
                mo1 = s.dma("scalar", "dqS", o1t[k],
                            self.out1b[:, i * 8 * CH + k * CH:i * 8 * CH + (k + 1) * CH])
            s.wait("sync", self.m_sel_st)
            s.wait("sync", m_sel_cons)
            m_sel_ld = s.dma("sync", "dqL", self.dent[0:1, 0:512],
                             self.selbd[0:1, i * CH:(i + 1) * CH])
            for g in range(8):
                s.wait("tensor", m_grps[g])
                if i == 0 and g == 0:
                    s.wait("tensor", self.m_E1_store)
                    s.wait("tensor", self.m_hw2r)
                for k in range(4 * g, 4 * g + 4):
                    w2k = self.hw2r(k)
                    mm = None
                    for f in range(NKC):
                        if k == 0:
                            s.puse(SLOT8[f])
                        pf = self.pslot(SLOT8[f])
                        mm = s.run("tensor", lambda e, k=k, f=f, pf=pf, w2k=w2k: e.matmul(
                            pf, w2k[:, f * 128:(f + 1) * 128],
                            hhk[k % 16], start=(k == 0), stop=(k == 31)),
                            inc="pe" if f == NKC - 1 else None)
                    m_hh_cons[k % 16] = mm
                if g + 4 < 8:
                    m_grps.append(issue_group(i, g + 4))
            s.wait("vector", mm)
            m_hv = None
            for f in range(NKC):
                pf = self.pslot(SLOT8[f])
                m_hv = s.run("vector", lambda e, f=f, pf=pf: e.tensor_scalar(
                    hv[:, f * CH:(f + 1) * CH], pf, self.hb2_sb[:, f:f + 1], None, ALU.add),
                    inc="dve")
                s.pread(SLOT8[f], m_hv)
            s.wait("tensor", m_hv)
            s.wait("tensor", m_sel_ld)
            mb = self.bcast("bc0", self.dent[0:1, 0:512])
            s.wait("vector", mb)
            m_ffn = None
            m_ffns = []
            for f in range(NKC):
                lf = lof[f % 2]
                s.wait("gpsimd", m_lo_cons[f % 2])
                ml = s.dma("gpsimd", "dqH", lf,
                           self.lightb[:, i * 8 * CH + f * CH:i * 8 * CH + (f + 1) * CH])
                s.run("vector", lambda e, f=f: e.tensor_mul(
                    tmp, hv[:, f * CH:(f + 1) * CH], self.pslot("bc0")[:, 0:CH]))
                s.wait("vector", ml)
                m_ffn = s.run("vector", lambda e, f=f, lf=lf: e.tensor_add(
                    fft[f], tmp, lf), inc="dve")
                m_lo_cons[f % 2] = m_ffn
                m_ffns.append(m_ffn)
            s.pread("bc0", m_ffn)
            m_sel_cons = ("pe", s.cnt["pe"])
            m_stats = self.ln_stats(fft, sqs, bf=True, wait_src=m_ffns)
            s.wait("vector", m_rows_cons)
            s.wait("scalar", m_rows_cons)
            mB, _ = self.ln_rows(m_stats)
            s.wait("gpsimd", mo1)
            s.wait("vector", mo1)
            s.wait("gpsimd", m_y_st)
            m_aff, m_res = self.affine(fft, None, tmp, mB, resid=o1t, pool_out=tmp2)
            m_rows_cons = ("pe", s.cnt["pe"])
            s.wait("scalar", m_res)
            m_y_st = s.dma("scalar", "dqS", self.yT[:, i * 8 * CH:(i + 1) * 8 * CH], out1c)
            m_out1_cons = m_y_st

    def finalize(self):
        s = self.s
        s.wait("sync", s.mark("dqS"))
        s.wait("sync", s.mark("dqG"))
        nc = self.nc
        with nc.Block() as block:
            @block.sync
            def _(e):
                for f in s.ops["sync"]:
                    f(e)

            @block.tensor
            def _(e):
                for f in s.ops["tensor"]:
                    f(e)

            @block.scalar
            def _(e):
                for f in s.ops["scalar"]:
                    f(e)

            @block.vector
            def _(e):
                for f in s.ops["vector"]:
                    f(e)

            @block.gpsimd
            def _(e):
                for f in s.ops["gpsimd"]:
                    f(e)
        return nc

    def build(self):
        order = ["A", "B1", "B2", "C", "D", "E1", "E2"]
        upto = order.index(self.upto)
        self.emit_preamble()
        self.emit_phase_a()
        self.m_A_store = self.s.mark("dqS")
        if upto >= 1:
            self.emit_phase_b(0)
        if upto >= 2:
            self.emit_phase_b(1)
        if upto >= 3:
            self.emit_phase_c()
        if upto >= 4:
            self.emit_phase_d()
        if upto >= 5:
            self.emit_phase_e1()
        if upto >= 6:
            self.emit_phase_e2()
        return self.finalize()


def make_consts():
    cm = np.zeros((128, 4 * CH), np.float32)
    for dj in range(4):
        r = dj * 128
        jj = np.arange(128)[:, None]
        ii = np.arange(CH)[None, :]
        cm[:, dj * CH:(dj + 1) * CH] = np.where(ii < jj + r, NEG, 0.0)
    ident = np.eye(128, dtype=np.float32)
    return cm.astype(BF), ident.astype(BF), ident


def make_in_maps(inputs):
    x = inputs["L0_attn_x"]
    cm, idb, idf = make_consts()

    def col(v):
        return np.ascontiguousarray(np.asarray(v, np.float32).reshape(-1, 1))
    shared = {
        "wq": np.ascontiguousarray(np.asarray(inputs["L0_attn_Wq"], np.float32)),
        "wk": np.ascontiguousarray(np.asarray(inputs["L0_attn_Wk"], np.float32)),
        "wv": np.ascontiguousarray(np.asarray(inputs["L0_attn_Wv"], np.float32)),
        "wo": np.ascontiguousarray(np.asarray(inputs["L0_attn_Wo"], np.float32)),
        "bo": col(inputs["L0_attn_bo"]),
        "rt": col(inputs["L0_ffn_routing_token"]),
        "lgam": col(inputs["L0_ffn_light_gamma"]),
        "lw1": np.ascontiguousarray(np.asarray(inputs["L0_ffn_light_w1"], np.float32)),
        "lb1": col(inputs["L0_ffn_light_b1"]),
        "lw2": np.ascontiguousarray(np.asarray(inputs["L0_ffn_light_w2"], np.float32)),
        "lb2": col(inputs["L0_ffn_light_b2"]),
        "hgam": col(inputs["L0_ffn_heavy_gamma"]),
        "hw1": np.ascontiguousarray(np.asarray(inputs["L0_ffn_heavy_w1"], np.float32)),
        "hb1": col(inputs["L0_ffn_heavy_b1"]),
        "hw2": np.ascontiguousarray(np.asarray(inputs["L0_ffn_heavy_w2"], np.float32)),
        "hb2": col(inputs["L0_ffn_heavy_b2"]),
        "cmaskd": cm, "idbd": idb, "idfd": idf,
    }
    in_maps = []
    for c in range(8):
        m = dict(shared)
        m["xT"] = np.ascontiguousarray(np.asarray(x[c], np.float32).T)
        in_maps.append(m)
    return in_maps


def assemble_y(res):
    ys = []
    for c in range(8):
        yT = np.asarray(res.results[c]["yT"], np.float32)
        arr = yT.reshape(128, NCH, NKC, CH)
        ys.append(np.transpose(arr, (1, 3, 2, 0)).reshape(S, D))
    return np.stack(ys)


def kernel(**inputs):
    kb = KB(taps=False)
    nc = kb.build()
    in_maps = make_in_maps(inputs)
    res = run_bass_kernel_spmd(nc, in_maps, core_ids=list(range(8)))
    return assemble_y(res)


# version 09
# speedup vs baseline: 1.2723x; 1.0117x over previous
import numpy as np
import ml_dtypes
import concourse.bass as bass
import concourse.mybir as mybir
from concourse.bass_utils import run_bass_kernel_spmd

F32 = mybir.dt.float32
BF16 = mybir.dt.bfloat16
AF = mybir.ActivationFunctionType
ALU = mybir.AluOpType
AX = mybir.AxisListType
BF = ml_dtypes.bfloat16

D = 1024
S = 4096
CH = 512
NCH = S // CH
GH = 8
HPG = GH // 2
DL = 512
DHH = 4096
NKC = D // 128
NJ = S // 128
EPS = 1.0e-5
NEG = -1.0e9
N_ITERS = 50
LOGK = float(np.log(np.float32(1152.0)))
A1 = float(np.log(np.float32(1152.0)) - np.log(np.float32(4096.0)))

ENGINES = ("sync", "tensor", "scalar", "vector", "gpsimd")

KBUF_O = 0
VBUF_O = 16384
AEX_O = 33024
HW1_O = 32768
ARB_N = 71680
ARF_N = 7680


def later(a, b):
    if a is None:
        return b
    if b is None:
        return a
    assert a[0] == b[0], (a, b)
    return a if a[1] >= b[1] else b


class Sched:
    def __init__(self, nc):
        self.nc = nc
        self.ops = {e: [] for e in ENGINES}
        self.sem = {}
        self.cnt = {}
        self.waited = {e: {} for e in ENGINES}
        for name in ("pe", "act", "dve", "pool", "dqL", "dqS", "dqG", "dqH"):
            self.sem[name] = nc.alloc_semaphore(name)
            self.cnt[name] = 0
        self.pread_m = {}

    OWN = {"tensor": "pe", "scalar": "act", "vector": "dve", "gpsimd": "pool"}

    def wait(self, eng, mst):
        if mst is None:
            return
        name, val = mst
        if self.OWN.get(eng) == name:
            return
        if val <= 0 or val <= self.waited[eng].get(name, 0):
            return
        self.waited[eng][name] = val
        h = self.sem[name]
        self.ops[eng].append(lambda e, h=h, v=val: e.wait_ge(h, v))

    def run(self, eng, fn, inc=None):
        if inc is None:
            self.ops[eng].append(fn)
            return None
        self.cnt[inc] += 1
        v = self.cnt[inc]
        h = self.sem[inc]

        def g(e, fn=fn, h=h):
            fn(e).then_inc(h, 1)
        self.ops[eng].append(g)
        return (inc, v)

    def dma(self, eng, qsem, out, in_, **kw):
        self.cnt[qsem] += 16
        v = self.cnt[qsem]
        h = self.sem[qsem]

        def g(e, out=out, in_=in_, h=h, kw=kw):
            e.dma_start(out=out, in_=in_, **kw).then_inc(h, 16)
        self.ops[eng].append(g)
        return (qsem, v)

    def mark(self, name):
        return (name, self.cnt[name])

    def puse(self, slot):
        for name, val in self.pread_m.get(slot, {}).items():
            self.wait("tensor", (name, val))

    def pread(self, slot, mst):
        if mst is None:
            return
        d = self.pread_m.setdefault(slot, {})
        name, val = mst
        d[name] = max(d.get(name, 0), val)


class KB:
    def __init__(self, taps=False, upto="E2"):
        nc = bass.Bass("TRN2", target_bir_lowering=False)
        self.nc = nc
        self.upto = upto
        self.s = Sched(nc)
        ktap = "ExternalOutput" if taps else "Internal"
        dt = nc.dram_tensor

        self.xT = dt("xT", [D, S], F32, kind="ExternalInput").ap()
        self.wq = dt("wq", [D, D], F32, kind="ExternalInput").ap()
        self.wk = dt("wk", [D, D], F32, kind="ExternalInput").ap()
        self.wv = dt("wv", [D, D], F32, kind="ExternalInput").ap()
        self.wo = dt("wo", [D, D], F32, kind="ExternalInput").ap()
        self.bo = dt("bo", [D, 1], F32, kind="ExternalInput").ap()
        self.rt = dt("rt", [D, 1], F32, kind="ExternalInput").ap()
        self.lgam = dt("lgam", [D, 1], F32, kind="ExternalInput").ap()
        self.lw1 = dt("lw1", [D, DL], F32, kind="ExternalInput").ap()
        self.lb1 = dt("lb1", [DL, 1], F32, kind="ExternalInput").ap()
        self.lw2 = dt("lw2", [DL, D], F32, kind="ExternalInput").ap()
        self.lb2 = dt("lb2", [D, 1], F32, kind="ExternalInput").ap()
        self.hgam = dt("hgam", [D, 1], F32, kind="ExternalInput").ap()
        self.hw1 = dt("hw1", [D, DHH], F32, kind="ExternalInput").ap()
        self.hb1 = dt("hb1", [DHH, 1], F32, kind="ExternalInput").ap()
        self.hw2 = dt("hw2", [DHH, D], F32, kind="ExternalInput").ap()
        self.hb2 = dt("hb2", [D, 1], F32, kind="ExternalInput").ap()
        self.cmaskd = dt("cmaskd", [128, 4 * CH], BF16, kind="ExternalInput").ap()
        self.idbd = dt("idbd", [128, 128], BF16, kind="ExternalInput").ap()
        self.idfd = dt("idfd", [128, 128], F32, kind="ExternalInput").ap()

        self.yT = dt("yT", [128, NCH * 8 * CH], F32, kind="ExternalOutput").ap()
        self.xlnb = dt("xlnb", [128, NCH * 8 * CH], BF16, kind=ktap).ap()
        self.k1b = dt("k1b", [128, HPG * S], BF16, kind=ktap).ap()
        self.v1b = dt("v1b", [128, NJ * GH * 65], BF16, kind=ktap).ap()
        self.attg0b = dt("attg0b", [128, NCH * HPG * CH], BF16, kind=ktap).ap()
        self.out1b = dt("out1b", [128, NCH * 8 * CH], F32, kind=ktap).ap()
        self.xrb = dt("xrb", [128, NCH * 8 * CH], BF16, kind=ktap).ap()
        self.lightb = dt("lightb", [128, NCH * 8 * CH], BF16, kind=ktap).ap()
        self.srowb = dt("srowb", [1, S], F32, kind=ktap).ap()
        self.dbgb = dt("dbgb", [4, S], F32, kind=ktap).ap()
        self.offsd = dt("offsd", [1, 64], F32, kind="Internal").ap()
        self.selbd = dt("selbd", [1, S], F32, kind=ktap).ap()
        self.hhb = dt("hhb", [128, NCH * 32 * CH], BF16, kind=ktap).ap()

        self.arb = nc.alloc_sbuf_tensor("arb", [128, ARB_N], BF16)
        self.arf = nc.alloc_sbuf_tensor("arf", [128, ARF_N], F32)

        self.pqk = nc.alloc_psum_tensor("pqk", [128, 4 * 512], F32)
        self.pav0 = nc.alloc_psum_tensor("pav0", [128, 512], F32)
        self.pav1 = nc.alloc_psum_tensor("pav1", [128, 512], F32)
        self.pbc0 = nc.alloc_psum_tensor("pbc0", [128, 512], F32)
        self.pbc1 = nc.alloc_psum_tensor("pbc1", [128, 512], F32)

        self.ones_col = nc.alloc_sbuf_tensor("ones_col", [128, 1], F32)
        self.ones_bcol = nc.alloc_sbuf_tensor("ones_bcol", [128, 1], BF16)
        self.ones_row = nc.alloc_sbuf_tensor("ones_row", [128, 128], F32)
        self.eps_ap = nc.alloc_sbuf_tensor("eps_ap", [1, 1], F32)
        self.bo_sb = nc.alloc_sbuf_tensor("bo_sb", [128, 8], F32)
        self.lb2_sb = nc.alloc_sbuf_tensor("lb2_sb", [128, 8], F32)
        self.hb2_sb = nc.alloc_sbuf_tensor("hb2_sb", [128, 8], F32)
        self.lgam_sb = nc.alloc_sbuf_tensor("lgam_sb", [128, 8], F32)
        self.hgam_sb = nc.alloc_sbuf_tensor("hgam_sb", [128, 8], F32)
        self.lb1_sb = nc.alloc_sbuf_tensor("lb1_sb", [128, 4], F32)
        self.hb1_sb = nc.alloc_sbuf_tensor("hb1_sb", [128, 32], F32)
        self.rt_sb = nc.alloc_sbuf_tensor("rt_sb", [128, 8], F32)
        self.R_sb = nc.alloc_sbuf_tensor("R_sb", [128, 1], F32)
        self.R_sc = nc.alloc_sbuf_tensor("R_sc", [1, 1], F32)
        self.cm_sb = nc.alloc_sbuf_tensor("cm_sb", [128, 4 * CH], BF16)
        self.idb_sb = nc.alloc_sbuf_tensor("idb_sb", [128, 128], BF16)
        self.idf_sb = nc.alloc_sbuf_tensor("idf_sb", [128, 128], F32)
        self.tch_v = nc.alloc_sbuf_tensor("tch_v", [128, 1], F32)
        self.tch_a = nc.alloc_sbuf_tensor("tch_a", [128, 1], F32)
        self.tch_p = nc.alloc_sbuf_tensor("tch_p", [128, 1], F32)
        self.rows = nc.alloc_sbuf_tensor("rows", [128, 3072], F32)
        self.dent = nc.alloc_sbuf_tensor("dent", [128, 1024], F32)

        self.r_s = nc.alloc_sbuf_tensor("r_s", [64, 64], F32)
        self.r_t = nc.alloc_sbuf_tensor("r_t", [64, 64], F32)
        self.r_e = nc.alloc_sbuf_tensor("r_e", [64, 64], F32)
        self.r_m = nc.alloc_sbuf_tensor("r_m", [64, 64], F32)
        self.r_c1 = nc.alloc_sbuf_tensor("r_c1", [64, 64], F32)
        self.r_c2 = nc.alloc_sbuf_tensor("r_c2", [64, 64], F32)
        self.r_col = nc.alloc_sbuf_tensor("r_col", [64, 1], F32)
        self.r_acol = nc.alloc_sbuf_tensor("r_acol", [64, 1], F32)
        self.r_row1 = nc.alloc_sbuf_tensor("r_row1", [1, 64], F32)
        self.r_row2 = nc.alloc_sbuf_tensor("r_row2", [1, 64], F32)
        self.r_sc1 = nc.alloc_sbuf_tensor("r_sc1", [1, 1], F32)

    def ab(self, off, n):
        assert off + n <= ARB_N, (off, n)
        return self.arb[:, off:off + n]

    def af(self, off, n):
        assert off + n <= ARF_N, (off, n)
        return self.arf[:, off:off + n]

    def rseg(self, lane, off, n=CH):
        return self.rows[lane:lane + 1, off:off + n]

    def pslot(self, name):
        if name.startswith("q"):
            i = int(name[1])
            return self.pqk[:, i * 512:(i + 1) * 512]
        return {"av0": self.pav0, "av1": self.pav1,
                "bc0": self.pbc0, "bc1": self.pbc1}[name][:, 0:512]

    def runT(self, eng, fn, out_ap, inc, base=0):
        s = self.s
        s.run(eng, fn)
        src = out_ap[0:1, 0:1]
        if eng == "vector":
            dst = self.tch_v[base:base + 1, 0:1]
            return s.run("vector", lambda e, src=src, dst=dst: e.tensor_copy(dst, src),
                         inc=inc)
        if eng == "scalar":
            dst = self.tch_a[base:base + 1, 0:1]
            return s.run("scalar", lambda e, src=src, dst=dst: e.activation(
                dst, src, AF.Identity), inc=inc)
        if eng == "gpsimd":
            dst = self.tch_p[base:base + 1, 0:1]
            return s.run("gpsimd", lambda e, src=src, dst=dst: e.tensor_copy(dst, src),
                         inc=inc)
        raise ValueError(eng)

    def bcast(self, slot, row_ap, nrows=128, base=0):
        s = self.s
        s.puse(slot)
        p = self.pslot(slot)[0:nrows, :row_ap.shape[-1]]
        return s.run("tensor", lambda e, p=p, r=row_ap, n=nrows, b=base: e.matmul(
            p, self.ones_row[b:b + 1, 0:n], r, start=True, stop=True), inc="pe")

    def emit_preamble(self):
        s = self.s
        s.run("vector", lambda e: e.memset(self.ones_col[:, :], 1.0))
        s.run("vector", lambda e: e.memset(self.ones_bcol[:, :], 1.0))
        s.run("vector", lambda e: e.memset(self.ones_row[:, :], 1.0))
        self.m_ones = self.runT("vector", lambda e: e.memset(self.eps_ap[:, :], EPS),
                                self.eps_ap, "dve")
        for dst, src in [
            (self.bo_sb, self.bo), (self.lb2_sb, self.lb2), (self.hb2_sb, self.hb2),
            (self.lgam_sb, self.lgam), (self.hgam_sb, self.hgam), (self.rt_sb, self.rt),
            (self.lb1_sb, self.lb1), (self.hb1_sb, self.hb1),
        ]:
            s.dma("sync", "dqL", dst[:, :], src.rearrange("(c p) one -> p (c one)", p=128),
                  allow_slow_non_contiguous=True)
        s.dma("sync", "dqL", self.cm_sb[:, :], self.cmaskd[:, :])
        s.dma("sync", "dqL", self.idb_sb[:, :], self.idbd[:, :])
        self.m_consts = s.dma("sync", "dqL", self.idf_sb[:, :], self.idfd[:, :])
        s.wait("tensor", self.m_consts)
        s.wait("tensor", self.m_ones)
        s.wait("vector", self.m_consts)
        mrow = self.runT("vector", lambda e: e.reduce_sum(self.R_sb[:, :], self.rt_sb[:, :],
                                                          axis=AX.X), self.R_sb, "dve")
        s.wait("tensor", mrow)
        m = s.run("tensor", lambda e: e.matmul(self.pbc0[0:1, 0:1], self.ones_col[:, :],
                                               self.R_sb[:, :], start=True, stop=True), inc="pe")
        s.wait("vector", m)
        self.m_R = s.run("vector", lambda e: e.tensor_copy(self.R_sc[:, :],
                                                           self.pbc0[0:1, 0:1]), inc="dve")
        s.pread("bc0", self.m_R)

    def ln_stats(self, tiles, sq_scr, bf=False, wait_src=None, slots=("av0", "av1")):
        s = self.s
        ones = self.ones_bcol if bf else self.ones_col
        sl0, sl1 = slots
        p0 = self.pslot(sl0)[0:1, 0:CH]
        p1 = self.pslot(sl1)[0:1, 0:CH]
        s.puse(sl0)
        s.puse(sl1)
        m_sum = m_sq = None
        m_sq_mm = [None, None]
        for k in range(NKC):
            t = tiles[k]
            ws = wait_src[k] if isinstance(wait_src, list) else wait_src
            if ws is not None:
                s.wait("tensor", ws)
                s.wait("scalar", ws)
            m_sum = s.run("tensor", lambda e, t=t, k=k, o=ones: e.matmul(
                p0, o[:, :], t, start=(k == 0), stop=(k == NKC - 1)),
                inc="pe" if k == NKC - 1 else None)
            scr = sq_scr[k % 2]
            s.wait("scalar", m_sq_mm[k % 2])
            ma = s.run("scalar", lambda e, t=t, scr=scr: e.activation(scr, t, AF.Square),
                       inc="act")
            s.wait("tensor", ma)
            m_sq = m_sq_mm[k % 2] = s.run("tensor", lambda e, scr=scr, k=k: e.matmul(
                p1, self.ones_col[:, :], scr,
                start=(k == 0), stop=(k == NKC - 1)), inc="pe")
        s.wait("vector", m_sum)
        s.wait("vector", m_sq)
        s.run("vector", lambda e: e.tensor_copy(self.rseg(0, 0), p0))
        m = s.run("vector", lambda e: e.tensor_copy(self.rseg(0, 512), p1),
                  inc="dve")
        s.pread(sl0, m)
        s.pread(sl1, m)
        return m

    def ln_rows(self, m_stats, rms=False):
        s = self.s
        SUMS = self.rseg(0, 0)
        SQ = self.rseg(0, 512)
        VAR = self.rseg(0, 1024)
        MEAN = self.rseg(0, 1536)
        Aseg = self.rseg(0, 2048)
        Bseg = self.rseg(0, 2560)
        s.wait("vector", m_stats)
        s.run("vector", lambda e: e.tensor_scalar(MEAN, SUMS, 1.0 / D, None, ALU.mult))
        s.run("vector", lambda e: e.tensor_scalar(VAR, SQ, 1.0 / D, None, ALU.mult))
        s.run("vector", lambda e: e.scalar_tensor_tensor(SUMS, MEAN, 1.0, MEAN, ALU.mult,
                                                         ALU.mult))
        m = self.runT("vector", lambda e: e.tensor_sub(VAR, VAR, SUMS), VAR, "dve")
        s.wait("scalar", m)
        s.wait("scalar", self.m_ones)
        s.run("scalar", lambda e: e.activation(Aseg, VAR, AF.Ln, bias=self.eps_ap[:, :]))
        m_rms = None
        if rms:
            ml2 = self.runT("scalar", lambda e: e.activation(VAR, VAR, AF.Ln), VAR, "act")
            s.wait("vector", ml2)
            s.run("vector", lambda e: e.tensor_sub(VAR, Aseg, VAR))
            mv = self.runT("vector", lambda e: e.tensor_scalar(VAR, VAR, 0.5, None, ALU.mult),
                           VAR, "dve")
            s.wait("scalar", mv)
            m_rms = self.runT("scalar", lambda e: e.activation(VAR, VAR, AF.Exp), VAR, "act")
        mA = self.runT("scalar", lambda e: e.activation(Aseg, Aseg, AF.Exp, scale=-0.5),
                       Aseg, "act")
        s.wait("vector", mA)
        mB = self.runT("vector", lambda e: e.scalar_tensor_tensor(
            Bseg, MEAN, -1.0, Aseg, ALU.mult, ALU.mult), Bseg, "dve")
        return mB, m_rms

    def affine(self, src_tiles, dst_tiles, tmp, m_ready, resid=None, pool_out=None,
               slots=("bc0", "bc1")):
        s = self.s
        m_aff = None
        m_res = None
        sa, sb = slots
        s.wait("tensor", m_ready)
        mA_ = self.bcast(sa, self.rseg(0, 2048))
        mB_ = self.bcast(sb, self.rseg(0, 2560))
        s.wait("vector", mA_)
        s.wait("vector", mB_)
        for k in range(NKC):
            s.run("vector", lambda e, k=k: e.tensor_mul(
                tmp, src_tiles[k], self.pslot(sa)[:, 0:CH]))
            if resid is None:
                m_aff = s.run("vector", lambda e, k=k: e.tensor_add(
                    dst_tiles[k], tmp, self.pslot(sb)), inc="dve")
            else:
                tmp2 = pool_out
                s.wait("vector", m_res)
                m_aff = s.run("vector", lambda e, k=k, tmp2=tmp2: e.tensor_add(
                    tmp2, tmp, self.pslot(sb)), inc="dve")
                s.wait("gpsimd", m_aff)
                m_res = s.run("gpsimd", lambda e, k=k, tmp2=tmp2: e.tensor_add(
                    resid[k], resid[k], tmp2), inc="pool")
        s.pread(sa, m_aff)
        s.pread(sb, m_aff)
        return m_aff, m_res

    def emit_phase_a(self):
        s = self.s
        o = AEX_O
        wk_b = self.ab(o, NKC * D); o += NKC * D
        wv_b = self.ab(o, NKC * D); o += NKC * D
        xlnw = self.ab(o, 2 * NKC * CH); o += 2 * NKC * CH
        ktmp = [self.ab(o, CH), self.ab(o + CH, CH)]; o += 2 * CH
        vtmp = [self.ab(o, GH * 65), self.ab(o + GH * 65 + 8, GH * 65)]
        o += 2 * (GH * 65 + 8)
        assert o <= ARB_N, o
        xc = self.af(0, NKC * CH)
        xct = [xc[:, k * CH:(k + 1) * CH] for k in range(NKC)]
        sqs = [self.af(NKC * CH, CH), self.af(NKC * CH + CH, CH)]
        tmp = self.af(NKC * CH + 2 * CH, CH)
        kbuf = self.ab(KBUF_O, HPG * S)
        vbuf = self.ab(VBUF_O, NJ * GH * 65)
        vview = vbuf.rearrange("p (t e) -> p t e", e=65)

        s.run("vector", lambda e: e.memset(vview[:, :, 64:65], 1.0))
        for b in range(2):
            vv = vtmp[b].rearrange("p (t e) -> p t e", e=65)[:, 0:GH, :]
            s.run("vector", lambda e, vv=vv: e.memset(vv[:, :, 64:65], 1.0))
        m_vones = self.runT("vector", lambda e: e.tensor_copy(
            self.tch_v[0:1, 0:1], vbuf[0:1, 64:65]), vbuf, "dve")

        for k in range(NKC):
            s.dma("gpsimd", "dqG", wk_b[:, k * D:(k + 1) * D], self.wk[k * 128:(k + 1) * 128, :])
            s.dma("gpsimd", "dqG", wv_b[:, k * D:(k + 1) * D], self.wv[k * 128:(k + 1) * 128, :])
        m_w = s.mark("dqG")

        m_xc_cons = None
        m_xln_cons = [None, None]
        m_xln_st = [None, None]
        m_rows_cons = None
        m_ktmp = [None, None]
        m_vtmp = [None, None]
        for i in range(NCH):
            par = i % 2
            xln = xlnw[:, par * NKC * CH:(par + 1) * NKC * CH]
            xlt = [xln[:, k * CH:(k + 1) * CH] for k in range(NKC)]
            s.wait("sync", m_xc_cons)
            mld = None
            for k in range(NKC):
                mld = s.dma("sync", "dqL", xct[k],
                            self.xT[k * 128:(k + 1) * 128, i * CH:(i + 1) * CH])
            s.wait("tensor", mld)
            s.wait("scalar", mld)
            s.wait("tensor", self.m_ones)
            m_stats = self.ln_stats(xct, sqs)
            s.wait("vector", m_rows_cons)
            s.wait("scalar", m_rows_cons)
            mB, _ = self.ln_rows(m_stats)
            s.wait("vector", m_xln_cons[par])
            s.wait("vector", m_xln_st[par])
            m_aff, _ = self.affine(xct, xlt, tmp, mB)
            m_rows_cons = ("pe", s.cnt["pe"])
            m_xc_cons = m_aff
            s.wait("scalar", m_aff)
            m_xln_st[par] = s.dma("scalar", "dqS",
                                  self.xlnb[:, i * 8 * CH:(i + 1) * 8 * CH], xln)

            s.wait("tensor", m_w)
            s.wait("tensor", m_aff)
            for oc in range(8):
                slot = "q0" if oc % 2 == 0 else "q1"
                po = self.pslot(slot)
                s.puse(slot)
                mm = None
                for k in range(NKC):
                    mm = s.run("tensor", lambda e, oc=oc, k=k, po=po, xlt=xlt: e.matmul(
                        po, wk_b[:, k * D + oc * 128:k * D + (oc + 1) * 128], xlt[k],
                        start=(k == 0), stop=(k == NKC - 1)),
                        inc="pe" if k == NKC - 1 else None)
                s.wait("vector", mm)
                if oc < HPG:
                    dst = kbuf[:, oc * S + i * CH:oc * S + (i + 1) * CH]
                    mev = s.run("vector", lambda e, dst=dst, po=po: e.tensor_copy(dst, po),
                                inc="dve")
                else:
                    kt = ktmp[oc % 2]
                    s.wait("vector", m_ktmp[oc % 2])
                    mev = s.run("vector", lambda e, kt=kt, po=po: e.tensor_copy(kt, po),
                                inc="dve")
                    s.wait("scalar", mev)
                    m_ktmp[oc % 2] = s.dma(
                        "scalar", "dqS",
                        self.k1b[:, (oc - HPG) * S + i * CH:(oc - HPG) * S + (i + 1) * CH], kt)
                s.pread(slot, mev)
            s.wait("vector", m_vones)
            for t4 in range(4):
                j = i * 4 + t4
                for half in range(2):
                    slot = "av0" if half == 0 else "av1"
                    pv = self.pslot(slot)
                    s.puse(slot)
                    mm = None
                    for k in range(NKC):
                        mm = s.run("tensor", lambda e, t4=t4, k=k, half=half, pv=pv, xlt=xlt:
                                   e.matmul(pv,
                                            xlt[k][:, t4 * 128:(t4 + 1) * 128],
                                            wv_b[:, k * D + half * 512:k * D + (half + 1) * 512],
                                            start=(k == 0), stop=(k == NKC - 1)),
                                   inc="pe" if k == NKC - 1 else None)
                    s.wait("vector", mm)
                    src = pv.rearrange("p (h d) -> p h d", d=64)
                    if half == 0:
                        dv = vview[:, j * GH:(j + 1) * GH, 0:64]
                        mev = s.run("vector", lambda e, dv=dv, src=src: e.tensor_copy(dv, src),
                                    inc="dve")
                    else:
                        vt = vtmp[t4 % 2]
                        s.wait("vector", m_vtmp[t4 % 2])
                        dvv = vt.rearrange("p (t e) -> p t e", e=65)[:, 0:GH, 0:64]
                        mev = s.run("vector", lambda e, dvv=dvv, src=src: e.tensor_copy(dvv, src),
                                    inc="dve")
                        s.wait("scalar", mev)
                        m_vtmp[t4 % 2] = s.dma(
                            "scalar", "dqS",
                            self.v1b[:, j * GH * 65:(j + 1) * GH * 65],
                            vt[:, 0:GH * 65])
                    s.pread(slot, mev)
            m_xln_cons[par] = ("pe", s.cnt["pe"])
        self.m_A_kv = ("dve", s.cnt["dve"])
        self.m_A_rows_cons = m_rows_cons
        self.m_A_pe = ("pe", s.cnt["pe"])

    def emit_phase_b(self, g):
        s = self.s
        kbuf = self.ab(KBUF_O, HPG * S)
        vbuf = self.ab(VBUF_O, NJ * GH * 65)
        vview = vbuf.rearrange("p (t e) -> p t e", e=65)
        o = AEX_O
        wqh = self.ab(o, NKC * 512); o += NKC * 512
        if g == 1:
            wo_b = self.ab(o, NKC * D); o += NKC * D
        xlnw = self.ab(o, 2 * NKC * CH); o += 2 * NKC * CH
        qcw = self.ab(o, 2 * HPG * CH); o += 2 * HPG * CH
        attcw = self.ab(o, 2 * HPG * CH); o += 2 * HPG * CH
        if g == 1:
            ag0 = self.ab(o, HPG * CH); o += HPG * CH
            obuf = self.ab(o, NKC * CH); o += NKC * CH
            obt = [obuf[:, k * CH:(k + 1) * CH] for k in range(NKC)]
        ebuf = self.ab(o, 3 * 1024); o += 3 * 1024
        assert o <= ARB_N, o
        xc = self.af(0, NKC * CH)
        xct = [xc[:, k * CH:(k + 1) * CH] for k in range(NKC)]
        tmp = self.af(NKC * CH, CH)
        tmp2 = self.af(NKC * CH + CH, CH)
        sqs = [self.af(NKC * CH + 2 * CH, CH), self.af(NKC * CH + 3 * CH, CH)]

        if g == 1:
            s.wait("sync", self.m_B1_pe)
            s.wait("sync", self.m_A_store)
            s.dma("sync", "dqL", kbuf, self.k1b[:, :])
            m_kv_ready = s.dma("sync", "dqL", vview[:, :, 0:64],
                               self.v1b.rearrange("p (t e) -> p t e", e=65)[:, :, 0:64])
        else:
            m_kv_ready = None
        m_prev = self.m_A_pe if g == 0 else self.m_B1_pe
        m_prev_st = self.m_A_store if g == 0 else self.m_B1_store
        m_prev_dve = self.m_A_kv if g == 0 else self.m_B1_dve
        for eng in ("gpsimd", "vector", "scalar", "sync"):
            s.wait(eng, m_prev)
            s.wait(eng, m_prev_st)
            s.wait(eng, m_prev_dve)
        for k in range(NKC):
            s.dma("gpsimd", "dqG", wqh[:, k * 512:(k + 1) * 512],
                  self.wq[k * 128:(k + 1) * 128, g * 512:(g + 1) * 512])
        if g == 1:
            for k in range(NKC):
                s.dma("gpsimd", "dqG", wo_b[:, k * D:(k + 1) * D],
                      self.wo[k * 128:(k + 1) * 128, :])
        m_w = s.mark("dqG")

        m_qc_cons = [None, None]
        m_xln_cons = [m_prev, m_prev]
        m_attc_cons = [None, None]
        m_ag0_cons = m_prev
        m_xc_cons = self.m_A_kv if g == 1 else None
        m_rows_cons = self.m_A_rows_cons if g == 0 else self.m_B_rows_cons
        m_u = None
        m_av_hist = [None, None, None]
        for i in range(NCH):
            jmax = 4 * (i + 1)
            par = i % 2
            xln = xlnw[:, par * NKC * CH:(par + 1) * NKC * CH]
            xlt = [xln[:, k * CH:(k + 1) * CH] for k in range(NKC)]
            qc = qcw[:, par * HPG * CH:(par + 1) * HPG * CH]
            attc = attcw[:, par * HPG * CH:(par + 1) * HPG * CH]
            s.wait("sync", m_xln_cons[par])
            s.wait("sync", self.m_A_store)
            m_xln = s.dma("sync", "dqL", xln, self.xlnb[:, i * 8 * CH:(i + 1) * 8 * CH])
            if g == 1:
                s.wait("sync", m_ag0_cons)
                s.wait("sync", self.m_B1_store)
                m_ag0 = s.dma("sync", "dqL", ag0,
                              self.attg0b[:, i * HPG * CH:(i + 1) * HPG * CH])
                s.wait("sync", m_xc_cons)
                m_xc = None
                for k in range(NKC):
                    m_xc = s.dma("sync", "dqL", xct[k],
                                 self.xT[k * 128:(k + 1) * 128, i * CH:(i + 1) * CH])
            s.wait("tensor", m_w)
            s.wait("tensor", m_xln)
            if g == 0:
                s.wait("tensor", self.m_A_kv)
            else:
                s.wait("tensor", m_kv_ready)
            m_q = None
            for hp in range(HPG):
                slot = "av0" if hp % 2 == 0 else "av1"
                pq = self.pslot(slot)
                s.puse(slot)
                mm = None
                for k in range(NKC):
                    mm = s.run("tensor", lambda e, hp=hp, k=k, pq=pq, xlt=xlt: e.matmul(
                        pq, wqh[:, k * 512 + hp * 128:k * 512 + (hp + 1) * 128], xlt[k],
                        start=(k == 0), stop=(k == NKC - 1)),
                        inc="pe" if k == NKC - 1 else None)
                s.wait("vector", mm)
                s.wait("vector", m_qc_cons[par])
                m_q = s.run("vector", lambda e, hp=hp, pq=pq, qc=qc: e.tensor_copy(
                    qc[:, hp * CH:(hp + 1) * CH], pq), inc="dve")
                s.pread(slot, m_q)
            s.wait("tensor", m_q)
            for hp in range(HPG):
                mqk_hist = {}

                def emit_qk(j, hp=hp, i=i, qc=qc):
                    sp = j % 2
                    sa, sb = f"q{2 * sp}", f"q{2 * sp + 1}"
                    pa, pb = self.pslot(sa), self.pslot(sb)
                    diag = j >= 4 * i
                    s.puse(sa)
                    s.puse(sb)
                    s.run("tensor", lambda e, pa=pa, j=j, diag=diag: e.matmul(
                        pa, kbuf[0:64, hp * S + j * 128:hp * S + (j + 1) * 128],
                        qc[0:64, hp * CH:(hp + 1) * CH],
                        start=True, stop=not diag, tile_position=(0, 0)), inc=None)
                    if diag:
                        dj = j - 4 * i
                        s.run("tensor", lambda e, pa=pa, dj=dj: e.matmul(
                            pa, self.idb_sb[:, :], self.cm_sb[:, dj * CH:(dj + 1) * CH],
                            start=False, stop=True), inc=None)
                    mqk = s.run("tensor", lambda e, pb=pb, j=j, diag=diag: e.matmul(
                        pb, kbuf[64:128, hp * S + j * 128:hp * S + (j + 1) * 128],
                        qc[64:128, hp * CH:(hp + 1) * CH],
                        start=True, stop=not diag, tile_position=(64, 0)),
                        inc=None if diag else "pe")
                    if diag:
                        dj = j - 4 * i
                        mqk = s.run("tensor", lambda e, pb=pb, dj=dj: e.matmul(
                            pb, self.idb_sb[:, :], self.cm_sb[:, dj * CH:(dj + 1) * CH],
                            start=False, stop=True), inc="pe")
                    mqk_hist[j] = mqk

                emit_qk(0)
                for j in range(jmax):
                    sp = j % 2
                    se = j % 3
                    sa, sb = f"q{2 * sp}", f"q{2 * sp + 1}"
                    if j + 1 < jmax:
                        emit_qk(j + 1)
                    s.wait("scalar", mqk_hist.pop(j))
                    s.wait("scalar", m_av_hist[se])
                    m_exp = s.run("scalar", lambda e, sp=sp, se=se: e.activation(
                        ebuf[:, se * 1024:(se + 1) * 1024],
                        self.pqk[:, (2 * sp) * 512:(2 * sp + 2) * 512],
                        AF.Exp, scale=0.125), inc="act")
                    s.pread(sa, m_exp)
                    s.pread(sb, m_exp)
                    s.wait("tensor", m_exp)
                    if j == 0:
                        s.puse("av0")
                        s.puse("av1")
                    s.run("tensor", lambda e, hp=hp, j=j, se=se, jmax=jmax: e.matmul(
                        self.pav0[0:65, :], vview[:, j * GH + 2 * hp, :],
                        ebuf[:, se * 1024:se * 1024 + 512],
                        start=(j == 0), stop=(j == jmax - 1)), inc=None)
                    m_av = s.run("tensor", lambda e, hp=hp, j=j, se=se, jmax=jmax: e.matmul(
                        self.pav1[0:65, :], vview[:, j * GH + 2 * hp + 1, :],
                        ebuf[:, se * 1024 + 512:(se + 1) * 1024],
                        start=(j == 0), stop=(j == jmax - 1)), inc="pe")
                    m_av_hist[se] = m_av
                s.wait("vector", m_av)
                s.wait("vector", m_attc_cons[par])
                s.run("vector", lambda e, hp=hp, attc=attc: e.tensor_copy(
                    attc[0:64, hp * CH:(hp + 1) * CH], self.pav0[0:64, :]))
                s.run("vector", lambda e, hp=hp, attc=attc: e.tensor_copy(
                    attc[64:128, hp * CH:(hp + 1) * CH], self.pav1[0:64, :]))
                s.run("vector", lambda e: e.tensor_copy(
                    self.dent[64:65, 0:512], self.pav0[64:65, :]))
                m_den = self.runT("vector", lambda e: e.tensor_copy(
                    self.dent[64:65, 512:1024], self.pav1[64:65, :]),
                    self.dent[64:65, :], "dve", base=64)
                s.pread("av0", m_den)
                s.pread("av1", m_den)
                s.wait("scalar", m_den)
                s.run("scalar", lambda e: e.activation(
                    self.dent[64:65, 0:1024], self.dent[64:65, 0:1024], AF.Ln))
                m_rc = self.runT("scalar", lambda e: e.activation(
                    self.dent[64:65, 0:1024], self.dent[64:65, 0:1024], AF.Exp, scale=-1.0),
                    self.dent[64:65, :], "act", base=64)
                s.wait("tensor", m_rc)
                for ab in range(2):
                    slot = "bc0" if ab == 0 else "bc1"
                    mb = self.bcast(slot, self.dent[64:65, ab * 512:(ab + 1) * 512],
                                    nrows=64, base=64)
                    s.wait("vector", mb)
                    sl = attc[ab * 64:(ab + 1) * 64, hp * CH:(hp + 1) * CH]
                    m_u = s.run("vector", lambda e, sl=sl, slot=slot: e.tensor_mul(
                        sl, sl, self.pslot(slot)[0:64, :]), inc="dve")
                    s.pread(slot, m_u)
            if g == 0:
                s.wait("scalar", m_u)
                m_attc_cons[par] = s.dma("scalar", "dqS",
                                         self.attg0b[:, i * HPG * CH:(i + 1) * HPG * CH],
                                         attc)
                m_qc_cons[par] = ("pe", s.cnt["pe"])
                m_xln_cons[par] = ("pe", s.cnt["pe"])
            else:
                s.wait("tensor", m_ag0)
                s.wait("tensor", m_u)
                m_ob = [None, None]
                m_obs = []
                for f in range(NKC):
                    slot = "bc0" if f % 2 == 0 else "bc1"
                    pf = self.pslot(slot)
                    s.puse(slot)
                    mm = None
                    for k in range(NKC):
                        rhs = ag0[:, k * CH:(k + 1) * CH] if k < 4 else \
                            attc[:, (k - 4) * CH:(k - 3) * CH]
                        mm = s.run("tensor", lambda e, f=f, k=k, pf=pf, rhs=rhs: e.matmul(
                            pf, wo_b[:, k * D + f * 128:k * D + (f + 1) * 128], rhs,
                            start=(k == 0), stop=(k == NKC - 1)),
                            inc="pe" if k == NKC - 1 else None)
                    if f % 2 == 0:
                        s.wait("scalar", mm)
                        mo = s.run("scalar", lambda e, f=f, pf=pf: e.activation(
                            obt[f], pf, AF.Identity, bias=self.bo_sb[:, f:f + 1]),
                            inc="act")
                    else:
                        s.wait("vector", mm)
                        mo = s.run("vector", lambda e, f=f, pf=pf: e.tensor_scalar(
                            obt[f], pf, self.bo_sb[:, f:f + 1], None, ALU.add),
                            inc="dve")
                    s.pread(slot, mo)
                    m_obs.append(mo)
                m_ag0_cons = ("pe", s.cnt["pe"])
                m_qc_cons[par] = ("pe", s.cnt["pe"])
                m_xln_cons[par] = ("pe", s.cnt["pe"])
                m_stats = self.ln_stats(obt, sqs, bf=True, wait_src=m_obs)
                s.wait("vector", m_rows_cons)
                s.wait("scalar", m_rows_cons)
                mB, _ = self.ln_rows(m_stats)
                s.wait("gpsimd", m_xc)
                s.wait("vector", m_xc)
                m_aff, m_res = self.affine(obt, None, tmp, mB, resid=xct, pool_out=tmp2,
                                           slots=("bc0", "bc1"))
                m_rows_cons = ("pe", s.cnt["pe"])
                s.wait("scalar", m_res)
                m_xc_cons = s.dma("scalar", "dqS",
                                  self.out1b[:, i * 8 * CH:(i + 1) * 8 * CH], xc)
                m_attc_cons[par] = ("pe", s.cnt["pe"])
        if g == 0:
            self.m_B1_pe = ("pe", s.cnt["pe"])
            self.m_B_rows_cons = m_rows_cons
            self.m_B1_store = s.mark("dqS")
            self.m_B1_dve = ("dve", s.cnt["dve"])
        else:
            self.m_B2_store = s.mark("dqS")
            self.m_B2_pe = ("pe", s.cnt["pe"])
            self.m_B2_dve = ("dve", s.cnt["dve"])
            self.m_C_rows_init = m_rows_cons

    def emit_phase_c(self):
        s = self.s
        o = 0
        lw1_b = self.ab(o, NKC * 512); o += NKC * 512
        lw2_b = self.ab(o, 4 * D); o += 4 * D
        xrc = self.ab(o, NKC * CH); o += NKC * CH
        xrt = [xrc[:, k * CH:(k + 1) * CH] for k in range(NKC)]
        lgh = self.ab(o, 4 * CH); o += 4 * CH
        assert o <= HW1_O
        hw1_b = self.ab(HW1_O, NKC * DHH)
        out1c = self.af(0, NKC * CH)
        o1t = [out1c[:, k * CH:(k + 1) * CH] for k in range(NKC)]
        tmp = self.af(NKC * CH, CH)
        lo = [self.af(NKC * CH + CH, CH), self.af(NKC * CH + 2 * CH, CH)]
        sqs = [self.af(NKC * CH + 3 * CH, CH), self.af(NKC * CH + 4 * CH, CH)]

        s.wait("gpsimd", self.m_B2_pe)
        mw = None
        for k in range(NKC):
            mw = s.dma("gpsimd", "dqG", lw1_b[:, k * 512:(k + 1) * 512],
                       self.lw1[k * 128:(k + 1) * 128, :])
        for k in range(4):
            mw = s.dma("gpsimd", "dqG", lw2_b[:, k * D:(k + 1) * D],
                       self.lw2[k * 128:(k + 1) * 128, :])
        s.wait("vector", mw)
        m_fold = None
        for k in range(NKC):
            m_fold = s.run("vector", lambda e, k=k: e.tensor_scalar(
                lw1_b[:, k * 512:(k + 1) * 512], lw1_b[:, k * 512:(k + 1) * 512],
                self.lgam_sb[:, k:k + 1], None, ALU.mult), inc="dve")
        m_hw1f = None
        for k in range(NKC):
            mwh = s.dma("gpsimd", "dqG", hw1_b[:, k * DHH:(k + 1) * DHH],
                        self.hw1[k * 128:(k + 1) * 128, :])
            s.wait("vector", mwh)
            m_hw1f = s.run("vector", lambda e, k=k: e.tensor_scalar(
                hw1_b[:, k * DHH:(k + 1) * DHH], hw1_b[:, k * DHH:(k + 1) * DHH],
                self.hgam_sb[:, k:k + 1], None, ALU.mult), inc="dve")
        self.m_hw1 = m_hw1f

        m_out1_cons = None
        m_rows_cons = self.m_C_rows_init
        m_xr_cons = None
        m_xr_st = None
        m_lo_cons = [None, None]
        m_lgh_cons = None
        m_srow_st = None
        for i in range(NCH):
            s.wait("sync", m_out1_cons)
            s.wait("sync", self.m_B2_store)
            mo1 = None
            for k in range(NKC):
                mo1 = s.dma("sync", "dqL", o1t[k],
                            self.out1b[:, i * 8 * CH + k * CH:i * 8 * CH + (k + 1) * CH])
            s.wait("tensor", mo1)
            s.wait("scalar", mo1)
            s.wait("vector", m_srow_st)
            m_stats = self.ln_stats(o1t, sqs)
            s.puse("q0")
            mm = None
            for k in range(NKC):
                mm = s.run("tensor", lambda e, k=k: e.matmul(
                    self.pqk[0:1, 0:CH], self.rt_sb[:, k:k + 1], o1t[k],
                    start=(k == 0), stop=(k == NKC - 1)),
                    inc="pe" if k == NKC - 1 else None)
            s.wait("vector", m_rows_cons)
            s.wait("scalar", m_rows_cons)
            mB, m_rms = self.ln_rows(m_stats, rms=True)
            s.wait("vector", mm)
            s.wait("vector", mB)
            s.wait("vector", self.m_R)
            s.wait("vector", m_srow_st)
            s.run("vector", lambda e: e.tensor_mul(
                self.rseg(0, 0), self.pqk[0:1, 0:CH], self.rseg(0, 2048)))
            s.run("vector", lambda e: e.tensor_scalar(
                self.rseg(0, 512), self.rseg(0, 2560), self.R_sc[0:1, 0:1], None,
                ALU.mult))
            m_srow = self.runT("vector", lambda e: e.tensor_add(
                self.rseg(0, 0), self.rseg(0, 0), self.rseg(0, 512)), self.rseg(0, 0), "dve")
            s.pread("q0", m_srow)
            s.wait("scalar", m_srow)
            m_srow_st = s.dma("scalar", "dqS", self.srowb[0:1, i * CH:(i + 1) * CH],
                              self.rseg(0, 0))
            s.wait("vector", m_rms)
            s.run("vector", lambda e: e.tensor_mul(self.rseg(0, 2048), self.rseg(0, 2048),
                                                   self.rseg(0, 1024)))
            mBp = self.runT("vector", lambda e: e.tensor_mul(
                self.rseg(0, 2560), self.rseg(0, 2560), self.rseg(0, 1024)),
                self.rseg(0, 2560), "dve")
            s.wait("vector", m_xr_cons)
            s.wait("vector", m_xr_st)
            m_aff, _ = self.affine(o1t, xrt, tmp, mBp)
            m_rows_cons = ("pe", s.cnt["pe"])
            m_out1_cons = m_aff
            s.wait("scalar", m_aff)
            m_xr_st = s.dma("scalar", "dqS", self.xrb[:, i * 8 * CH:(i + 1) * 8 * CH], xrc)
            s.wait("tensor", m_fold)
            s.wait("tensor", m_aff)
            m_g = None
            for m in range(4):
                slot = "q0" if m % 2 == 0 else "q1"
                ph = self.pslot(slot)
                s.puse(slot)
                mm = None
                for k in range(NKC):
                    mm = s.run("tensor", lambda e, m=m, k=k, ph=ph: e.matmul(
                        ph, lw1_b[:, k * 512 + m * 128:k * 512 + (m + 1) * 128], xrt[k],
                        start=(k == 0), stop=(k == NKC - 1)),
                        inc="pe" if k == NKC - 1 else None)
                s.wait("scalar", mm)
                s.wait("scalar", m_lgh_cons)
                m_g = s.run("scalar", lambda e, m=m, ph=ph: e.activation(
                    lgh[:, m * CH:(m + 1) * CH], ph, AF.Gelu,
                    bias=self.lb1_sb[:, m:m + 1]), inc="act")
                s.pread(slot, m_g)
            s.wait("tensor", m_g)
            for f in range(NKC):
                slot = "av0" if f % 2 == 0 else "av1"
                pf = self.pslot(slot)
                s.puse(slot)
                mm = None
                for kk in range(4):
                    mm = s.run("tensor", lambda e, f=f, kk=kk, pf=pf: e.matmul(
                        pf, lw2_b[:, kk * D + f * 128:kk * D + (f + 1) * 128],
                        lgh[:, kk * CH:(kk + 1) * CH],
                        start=(kk == 0), stop=(kk == 3)),
                        inc="pe" if kk == 3 else None)
                lof = lo[f % 2]
                s.wait("vector", mm)
                s.wait("vector", m_lo_cons[f % 2])
                mev = s.run("vector", lambda e, f=f, pf=pf, lof=lof: e.tensor_scalar(
                    lof, pf, self.lb2_sb[:, f:f + 1], None, ALU.add), inc="dve")
                s.pread(slot, mev)
                s.wait("gpsimd", mev)
                m_lo_cons[f % 2] = s.dma(
                    "gpsimd", "dqH",
                    self.lightb[:, i * 8 * CH + f * CH:i * 8 * CH + (f + 1) * CH], lof)
            m_lgh_cons = ("pe", s.cnt["pe"])
            m_xr_cons = ("pe", s.cnt["pe"])
        self.m_C_store = s.mark("dqS")
        self.m_C_rows_cons = m_rows_cons
        self.m_C_pe = ("pe", s.cnt["pe"])
        self.m_C_dve = ("dve", s.cnt["dve"])

    def emit_phase_d(self):
        s = self.s
        s.wait("sync", self.m_C_store)
        m_s = s.dma("sync", "dqL", self.r_s[:, :],
                    self.srowb.rearrange("one (p f) -> (one p) f", p=64))
        s.wait("vector", m_s)
        m_a = self.runT("vector", lambda e: e.memset(self.r_acol[:, :], A1),
                        self.r_acol, "dve")
        for it in range(2, N_ITERS + 1):
            s.wait("scalar", m_a)
            m_r = self.runT("scalar", lambda e: e.activation(
                self.r_t[:, :], self.r_s[:, :], AF.Relu, bias=self.r_acol[:, :]),
                self.r_t, "act")
            s.wait("vector", m_r)
            m_t = self.runT("vector", lambda e: e.tensor_sub(
                self.r_t[:, :], self.r_s[:, :], self.r_t[:, :]), self.r_t, "dve")
            s.wait("scalar", m_t)
            m_e = self.runT("scalar", lambda e: e.activation(
                self.r_e[:, :], self.r_t[:, :], AF.Exp), self.r_e, "act")
            s.wait("vector", m_e)
            m_rs = self.runT("vector", lambda e: e.reduce_sum(
                self.r_col[:, :], self.r_e[:, :], axis=AX.X), self.r_col, "dve")
            s.wait("tensor", m_rs)
            s.puse("av0")
            m_cs = s.run("tensor", lambda e: e.matmul(
                self.pav0[0:1, 0:1], self.ones_col[0:64, :], self.r_col[:, :],
                start=True, stop=True), inc="pe")
            s.wait("scalar", m_cs)
            m_ln = self.runT("scalar", lambda e: e.activation(
                self.r_sc1[:, :], self.pav0[0:1, 0:1], AF.Ln), self.r_sc1, "act")
            s.pread("av0", m_ln)
            s.wait("vector", m_ln)
            m_sc = self.runT("vector", lambda e: e.tensor_scalar(
                self.r_sc1[:, :], self.r_sc1[:, :], -1.0, LOGK, ALU.mult, ALU.add),
                self.r_sc1, "dve")
            s.wait("tensor", m_sc)
            s.puse("bc0")
            m_bc = s.run("tensor", lambda e: e.matmul(
                self.pbc0[0:64, 0:1], self.ones_row[0:1, 0:64], self.r_sc1[:, :],
                start=True, stop=True), inc="pe")
            s.wait("vector", m_bc)
            m_a = self.runT("vector", lambda e: e.tensor_copy(
                self.r_acol[:, :], self.pbc0[0:64, 0:1]), self.r_acol, "dve")
            s.pread("bc0", m_a)
        s.wait("vector", m_a)
        m_uu = self.runT("vector", lambda e: e.tensor_scalar(
            self.r_t[:, :], self.r_s[:, :], self.r_acol[:, :], None, ALU.add),
            self.r_t, "dve")
        s.wait("scalar", m_uu)
        m_sg = self.runT("scalar", lambda e: e.activation(
            self.r_m[:, :], self.r_t[:, :], AF.Sign), self.r_m, "act")
        s.wait("vector", m_sg)
        s.run("vector", lambda e: e.tensor_relu(self.r_m[:, :], self.r_m[:, :]))
        s.run("vector", lambda e: e.tensor_copy(self.r_c1[:, :], self.r_m[:, :]))
        src, dst = self.r_c1, self.r_c2
        for sh in (1, 2, 4, 8, 16, 32):
            s.run("vector", lambda e, src=src, dst=dst: e.tensor_copy(dst[:, :], src[:, :]))
            s.run("vector", lambda e, src=src, dst=dst, sh=sh: e.tensor_add(
                dst[:, sh:64], src[:, sh:64], src[:, 0:64 - sh]))
            src, dst = dst, src
        cum = src
        m_ct = self.runT("vector", lambda e: e.reduce_sum(
            self.r_col[:, :], self.r_m[:, :], axis=AX.X), self.r_col, "dve")
        s.wait("gpsimd", m_ct)
        mb0 = s.dma("gpsimd", "dqG", self.offsd[0:1, 0:64],
                    self.r_col[:, :].rearrange("p one -> (p one)")[None, :]
                    if False else self.r_col[:, :])
        s.wait("gpsimd", mb0)
        m_tr = s.dma("gpsimd", "dqG", self.r_row2[0:1, 0:64], self.offsd[0:1, 0:64])
        s.wait("vector", m_tr)
        s.run("vector", lambda e: e.memset(self.r_row1[:, :], 0.0))
        m_ex = s.run("vector", lambda e: e.tensor_copy(self.r_row1[0:1, 1:64],
                                                       self.r_row2[0:1, 0:63]), inc="dve")
        rsrc, rdst = self.r_row1, self.r_row2
        for sh in (1, 2, 4, 8, 16, 32):
            s.run("vector", lambda e, rsrc=rsrc, rdst=rdst: e.tensor_copy(rdst[:, :],
                                                                          rsrc[:, :]))
            s.run("vector", lambda e, rsrc=rsrc, rdst=rdst, sh=sh: e.tensor_add(
                rdst[0:1, sh:64], rsrc[0:1, sh:64], rsrc[0:1, 0:64 - sh]))
            rsrc, rdst = rdst, rsrc
        m_off = self.runT("vector", lambda e, rsrc=rsrc: e.tensor_copy(
            self.tch_v[0:1, 0:1], rsrc[0:1, 0:1]), rsrc, "dve")
        s.wait("gpsimd", m_off)
        mb1 = s.dma("gpsimd", "dqG", self.offsd[0:1, 0:64], rsrc[0:1, 0:64])
        s.wait("gpsimd", mb1)
        m_tr2 = s.dma("gpsimd", "dqG", self.r_col[:, :],
                      self.offsd.rearrange("one (p f) -> (one p) f", p=64))
        s.wait("vector", m_tr2)
        m_cg = s.run("vector", lambda e, cum=cum: e.tensor_scalar(
            cum[:, :], cum[:, :], self.r_col[:, :], None, ALU.add), inc="dve")

        m_t2 = self.runT("vector", lambda e, cum=cum: e.tensor_scalar(
            self.r_t[:, :], cum[:, :], -1.0, 1024.5, ALU.mult, ALU.add), self.r_t, "dve")
        s.wait("scalar", m_t2)
        m_sg2 = self.runT("scalar", lambda e: e.activation(
            self.r_t[:, :], self.r_t[:, :], AF.Sign), self.r_t, "act")
        s.wait("vector", m_sg2)
        s.run("vector", lambda e: e.tensor_relu(self.r_t[:, :], self.r_t[:, :]))
        m_sel = self.runT("vector", lambda e: e.tensor_mul(
            self.r_m[:, :], self.r_m[:, :], self.r_t[:, :]), self.r_m, "dve")
        s.wait("gpsimd", m_sel)
        self.m_sel_st = s.dma("gpsimd", "dqG",
                              self.selbd.rearrange("one (p f) -> (one p) f", p=64),
                              self.r_m[:, :])

    def hw2r(self, k):
        off = 5120 + k * D if k < 27 else 65536 + (k - 27) * D
        return self.ab(off, D)

    def emit_phase_e1(self):
        s = self.s
        hw1_b = self.ab(HW1_O, NKC * DHH)
        xrc = self.ab(0, NKC * CH)
        hht = [self.ab(NKC * CH, CH), self.ab(NKC * CH + CH, CH)]
        m_xr_cons = self.m_C_pe
        m_hh_st = [None, None]
        s.wait("gpsimd", self.m_C_pe)
        s.wait("gpsimd", self.m_C_dve)
        s.wait("gpsimd", self.m_C_store)
        for k in range(32):
            s.dma("gpsimd", "dqH", self.hw2r(k), self.hw2[k * 128:(k + 1) * 128, :])
        self.m_hw2r = s.mark("dqH")
        s.wait("tensor", self.m_hw1)
        s.wait("tensor", self.m_C_pe)
        s.wait("tensor", self.m_C_dve)
        for i in range(NCH):
            s.wait("sync", m_xr_cons)
            s.wait("sync", self.m_C_store)
            m_xr = s.dma("sync", "dqL", xrc, self.xrb[:, i * 8 * CH:(i + 1) * 8 * CH])
            s.wait("tensor", m_xr)
            for m in range(32):
                slot = "av0" if m % 2 == 0 else "av1"
                ph = self.pslot(slot)
                s.puse(slot)
                mm = None
                for k in range(NKC):
                    mm = s.run("tensor", lambda e, m=m, k=k, ph=ph: e.matmul(
                        ph, hw1_b[:, k * DHH + m * 128:k * DHH + (m + 1) * 128],
                        xrc[:, k * CH:(k + 1) * CH],
                        start=(k == 0), stop=(k == NKC - 1)),
                        inc="pe" if k == NKC - 1 else None)
                ht = hht[m % 2]
                s.wait("scalar", mm)
                s.wait("scalar", m_hh_st[m % 2])
                mg = s.run("scalar", lambda e, m=m, ph=ph, ht=ht: e.activation(
                    ht, ph, AF.Gelu, bias=self.hb1_sb[:, m:m + 1]), inc="act")
                s.pread(slot, mg)
                s.wait("gpsimd", mg)
                m_hh_st[m % 2] = s.dma(
                    "gpsimd", "dqG",
                    self.hhb[:, (i * 32 + m) * CH:(i * 32 + m + 1) * CH], ht)
            m_xr_cons = ("pe", s.cnt["pe"])
        self.m_E1_store = s.mark("dqG")
        self.m_E1_pe = ("pe", s.cnt["pe"])
        self.m_E1_act = ("act", s.cnt["act"])

    def emit_phase_e2(self):
        s = self.s
        hhk = [self.ab(HW1_O + b * CH, CH) for b in range(16)]
        ffnc = self.ab(HW1_O + 16 * CH, NKC * CH)
        fft = [ffnc[:, f * CH:(f + 1) * CH] for f in range(NKC)]
        hv = self.ab(HW1_O + 16 * CH + NKC * CH, NKC * CH)
        out1c = self.af(0, NKC * CH)
        o1t = [out1c[:, k * CH:(k + 1) * CH] for k in range(NKC)]
        tmp = self.af(NKC * CH, CH)
        tmp2 = self.af(NKC * CH + CH, CH)
        lof = [self.af(NKC * CH + 2 * CH, CH), self.af(NKC * CH + 3 * CH, CH)]
        sqs = [self.af(NKC * CH + 4 * CH, CH), self.af(NKC * CH + 5 * CH, CH)]
        SLOT8 = ["q0", "q1", "q2", "q3", "av0", "av1", "bc0", "bc1"]

        m_rows_cons = self.m_C_rows_cons
        m_out1_cons = None
        m_hh_cons = [self.m_E1_store] * 16
        m_lo_cons = [None, None]
        m_y_st = None
        m_sel_ld = None
        m_sel_cons = None
        s.wait("sync", self.m_C_pe)
        s.wait("sync", self.m_C_dve)
        s.wait("sync", self.m_C_store)
        s.wait("sync", self.m_E1_pe)
        s.wait("vector", self.m_E1_pe)
        s.wait("vector", self.m_E1_store)
        s.wait("scalar", self.m_C_pe)
        s.wait("scalar", self.m_C_dve)

        def issue_group(i, gg):
            eng, q = ("sync", "dqL") if gg % 2 == 0 else ("gpsimd", "dqG")
            for k in range(4 * gg, 4 * gg + 4):
                s.wait(eng, m_hh_cons[k % 16])
            return s.dma(eng, q, self.ab(HW1_O + (gg % 4) * 4 * CH, 4 * CH),
                         self.hhb[:, (i * 32 + 4 * gg) * CH:
                                  (i * 32 + 4 * gg + 4) * CH])

        for i in range(NCH):
            m_grps = [issue_group(i, g) for g in range(4)]
            s.wait("scalar", m_out1_cons)
            mo1 = None
            for k in range(NKC):
                mo1 = s.dma("scalar", "dqS", o1t[k],
                            self.out1b[:, i * 8 * CH + k * CH:i * 8 * CH + (k + 1) * CH])
            s.wait("sync", self.m_sel_st)
            s.wait("sync", m_sel_cons)
            m_sel_ld = s.dma("sync", "dqL", self.dent[0:1, 0:512],
                             self.selbd[0:1, i * CH:(i + 1) * CH])
            for g in range(8):
                s.wait("tensor", m_grps[g])
                if i == 0 and g == 0:
                    s.wait("tensor", self.m_E1_store)
                    s.wait("tensor", self.m_hw2r)
                for k in range(4 * g, 4 * g + 4):
                    w2k = self.hw2r(k)
                    mm = None
                    for f in range(NKC):
                        if k == 0:
                            s.puse(SLOT8[f])
                        pf = self.pslot(SLOT8[f])
                        mm = s.run("tensor", lambda e, k=k, f=f, pf=pf, w2k=w2k: e.matmul(
                            pf, w2k[:, f * 128:(f + 1) * 128],
                            hhk[k % 16], start=(k == 0), stop=(k == 31)),
                            inc="pe" if f == NKC - 1 else None)
                    m_hh_cons[k % 16] = mm
                if g + 4 < 8:
                    m_grps.append(issue_group(i, g + 4))
            s.wait("vector", mm)
            s.wait("scalar", mm)
            m_hvs = []
            for f in range(NKC):
                pf = self.pslot(SLOT8[f])
                if f % 2 == 0:
                    m_hv = s.run("vector", lambda e, f=f, pf=pf: e.tensor_scalar(
                        hv[:, f * CH:(f + 1) * CH], pf, self.hb2_sb[:, f:f + 1], None,
                        ALU.add), inc="dve")
                else:
                    m_hv = s.run("scalar", lambda e, f=f, pf=pf: e.activation(
                        hv[:, f * CH:(f + 1) * CH], pf, AF.Identity,
                        bias=self.hb2_sb[:, f:f + 1]), inc="act")
                s.pread(SLOT8[f], m_hv)
                m_hvs.append(m_hv)
            s.wait("tensor", m_hv)
            s.wait("tensor", m_sel_ld)
            mb = self.bcast("bc0", self.dent[0:1, 0:512])
            s.wait("vector", mb)
            m_ffn = None
            m_ffns = []
            for f in range(NKC):
                lf = lof[f % 2]
                s.wait("gpsimd", m_lo_cons[f % 2])
                ml = s.dma("gpsimd", "dqH", lf,
                           self.lightb[:, i * 8 * CH + f * CH:i * 8 * CH + (f + 1) * CH])
                s.wait("vector", m_hvs[f])
                s.run("vector", lambda e, f=f: e.tensor_mul(
                    tmp, hv[:, f * CH:(f + 1) * CH], self.pslot("bc0")[:, 0:CH]))
                s.wait("vector", ml)
                m_ffn = s.run("vector", lambda e, f=f, lf=lf: e.tensor_add(
                    fft[f], tmp, lf), inc="dve")
                m_lo_cons[f % 2] = m_ffn
                m_ffns.append(m_ffn)
            s.pread("bc0", m_ffn)
            m_sel_cons = ("pe", s.cnt["pe"])
            m_stats = self.ln_stats(fft, sqs, bf=True, wait_src=m_ffns)
            s.wait("vector", m_rows_cons)
            s.wait("scalar", m_rows_cons)
            mB, _ = self.ln_rows(m_stats)
            s.wait("gpsimd", mo1)
            s.wait("vector", mo1)
            s.wait("gpsimd", m_y_st)
            m_aff, m_res = self.affine(fft, None, tmp, mB, resid=o1t, pool_out=tmp2)
            m_rows_cons = ("pe", s.cnt["pe"])
            s.wait("scalar", m_res)
            m_y_st = s.dma("scalar", "dqS", self.yT[:, i * 8 * CH:(i + 1) * 8 * CH], out1c)
            m_out1_cons = m_y_st

    def finalize(self):
        s = self.s
        s.wait("sync", s.mark("dqS"))
        s.wait("sync", s.mark("dqG"))
        nc = self.nc
        with nc.Block() as block:
            @block.sync
            def _(e):
                for f in s.ops["sync"]:
                    f(e)

            @block.tensor
            def _(e):
                for f in s.ops["tensor"]:
                    f(e)

            @block.scalar
            def _(e):
                for f in s.ops["scalar"]:
                    f(e)

            @block.vector
            def _(e):
                for f in s.ops["vector"]:
                    f(e)

            @block.gpsimd
            def _(e):
                for f in s.ops["gpsimd"]:
                    f(e)
        return nc

    def build(self):
        order = ["A", "B1", "B2", "C", "D", "E1", "E2"]
        upto = order.index(self.upto)
        self.emit_preamble()
        self.emit_phase_a()
        self.m_A_store = self.s.mark("dqS")
        if upto >= 1:
            self.emit_phase_b(0)
        if upto >= 2:
            self.emit_phase_b(1)
        if upto >= 3:
            self.emit_phase_c()
        if upto >= 4:
            self.emit_phase_d()
        if upto >= 5:
            self.emit_phase_e1()
        if upto >= 6:
            self.emit_phase_e2()
        return self.finalize()


def make_consts():
    cm = np.zeros((128, 4 * CH), np.float32)
    for dj in range(4):
        r = dj * 128
        jj = np.arange(128)[:, None]
        ii = np.arange(CH)[None, :]
        cm[:, dj * CH:(dj + 1) * CH] = np.where(ii < jj + r, NEG, 0.0)
    ident = np.eye(128, dtype=np.float32)
    return cm.astype(BF), ident.astype(BF), ident


def make_in_maps(inputs):
    x = inputs["L0_attn_x"]
    cm, idb, idf = make_consts()

    def col(v):
        return np.ascontiguousarray(np.asarray(v, np.float32).reshape(-1, 1))
    shared = {
        "wq": np.ascontiguousarray(np.asarray(inputs["L0_attn_Wq"], np.float32)),
        "wk": np.ascontiguousarray(np.asarray(inputs["L0_attn_Wk"], np.float32)),
        "wv": np.ascontiguousarray(np.asarray(inputs["L0_attn_Wv"], np.float32)),
        "wo": np.ascontiguousarray(np.asarray(inputs["L0_attn_Wo"], np.float32)),
        "bo": col(inputs["L0_attn_bo"]),
        "rt": col(inputs["L0_ffn_routing_token"]),
        "lgam": col(inputs["L0_ffn_light_gamma"]),
        "lw1": np.ascontiguousarray(np.asarray(inputs["L0_ffn_light_w1"], np.float32)),
        "lb1": col(inputs["L0_ffn_light_b1"]),
        "lw2": np.ascontiguousarray(np.asarray(inputs["L0_ffn_light_w2"], np.float32)),
        "lb2": col(inputs["L0_ffn_light_b2"]),
        "hgam": col(inputs["L0_ffn_heavy_gamma"]),
        "hw1": np.ascontiguousarray(np.asarray(inputs["L0_ffn_heavy_w1"], np.float32)),
        "hb1": col(inputs["L0_ffn_heavy_b1"]),
        "hw2": np.ascontiguousarray(np.asarray(inputs["L0_ffn_heavy_w2"], np.float32)),
        "hb2": col(inputs["L0_ffn_heavy_b2"]),
        "cmaskd": cm, "idbd": idb, "idfd": idf,
    }
    in_maps = []
    for c in range(8):
        m = dict(shared)
        m["xT"] = np.ascontiguousarray(np.asarray(x[c], np.float32).T)
        in_maps.append(m)
    return in_maps


def assemble_y(res):
    ys = []
    for c in range(8):
        yT = np.asarray(res.results[c]["yT"], np.float32)
        arr = yT.reshape(128, NCH, NKC, CH)
        ys.append(np.transpose(arr, (1, 3, 2, 0)).reshape(S, D))
    return np.stack(ys)


def kernel(**inputs):
    kb = KB(taps=False)
    nc = kb.build()
    in_maps = make_in_maps(inputs)
    res = run_bass_kernel_spmd(nc, in_maps, core_ids=list(range(8)))
    return assemble_y(res)
